# version 03
# baseline (speedup 1.0000x reference)
import sys

for _p in ("/opt/trn_rl_repo",):
    if _p not in sys.path:
        sys.path.insert(0, _p)

import numpy as np
import ml_dtypes
import concourse.bass as bass
import concourse.mybir as mybir
from concourse.bass_utils import run_bass_kernel_spmd

F32 = mybir.dt.float32
BF16 = mybir.dt.bfloat16
AF = mybir.ActivationFunctionType
ALU = mybir.AluOpType
AX = mybir.AxisListType

NCORES = 8
T = 1024
TT = 8
D = 2048
DC = 16
E = 128
HV = 4096
S = 8192
KT = 64
QBN = 2
QW = 512
II = 5632
ICN = 44
EPS = 1e-5

ENGINES = ("tensor", "vector", "scalar", "sync", "gpsimd")


class Plan:

    def __init__(self, nc, stack):
        self.nc = nc
        self.stack = stack
        self.q = {e: [] for e in ENGINES}
        self.sems = {}
        self.cnt = {}
        self.last_wait = {e: {} for e in ENGINES}

    def sem(self, name):
        if name not in self.sems:
            self.sems[name] = self.stack.enter_context(self.nc.semaphore(name))
            self.cnt[name] = 0
        return self.sems[name]

    def bump(self, name, k=1):
        self.sem(name)
        self.cnt[name] += k
        return self.cnt[name]

    def op(self, eng, fn, waits=(), inc=None, inck=1):
        real_waits = []
        lw = self.last_wait[eng]
        for (s, v) in waits:
            assert v is not None, f"None wait on {s}"
            if v <= 0:
                continue
            assert v <= self.cnt.get(s, 0), f"wait on future {s}={v} (have {self.cnt.get(s, 0)})"
            if lw.get(s, 0) >= v:
                continue
            lw[s] = v
            real_waits.append((self.sem(s), v))
        incsem = self.sem(inc) if inc else None
        target = self.bump(inc, inck) if inc else None

        def run(e, _w=tuple(real_waits), _fn=fn, _is=incsem, _k=inck):
            for (sh, v) in _w:
                e.wait_ge(sh, v)
            inst = _fn(e)
            if _is is not None:
                inst.then_inc(_is, _k)

        self.q[eng].append(run)
        return target

    def dma(self, eng, out, in_, semname, waits=(), transpose=False):
        def fn(e, _o=out, _i=in_, _t=transpose):
            if _t:
                return e.dma_start(out=_o, in_=_i, transpose=True)
            return e.dma_start(out=_o, in_=_i)

        return self.op(eng, fn, waits=waits, inc=semname, inck=16)


def build_program(stage=3, dbg=False):
    nc = bass.Bass("TRN2", target_bir_lowering=False, debug=False)
    import contextlib

    stack = contextlib.ExitStack()

    ap = lambda h: h.ap()
    x_in = ap(nc.dram_tensor("x", [T, D], F32, kind="ExternalInput"))
    wqk_in = ap(nc.dram_tensor("wqk", [D, E], F32, kind="ExternalInput"))
    nrm_in = ap(nc.dram_tensor("nrm", [128, 8], F32, kind="ExternalInput"))
    bhg_in = ap(nc.dram_tensor("bhg", [128, 32], F32, kind="ExternalInput"))
    bhv_in = ap(nc.dram_tensor("bhv", [1, HV], F32, kind="ExternalInput"))
    whid_in = ap(nc.dram_tensor("whid", [D, 2 * HV], F32, kind="ExternalInput"))
    wout_in = ap(nc.dram_tensor("wout", [HV, D], F32, kind="ExternalInput"))
    bout_in = ap(nc.dram_tensor("bout", [1, D], F32, kind="ExternalInput"))
    gwt_in = ap(nc.dram_tensor("gwt", [D, II], F32, kind="ExternalInput"))
    uwt_in = ap(nc.dram_tensor("uwt", [D, II], F32, kind="ExternalInput"))
    dwt_in = ap(nc.dram_tensor("dwt", [II, D], F32, kind="ExternalInput"))
    gs_in = ap(nc.dram_tensor("gs", [128, ICN], F32, kind="ExternalInput"))
    us_in = ap(nc.dram_tensor("us", [128, ICN], F32, kind="ExternalInput"))
    ds_in = ap(nc.dram_tensor("ds", [1, D], F32, kind="ExternalInput"))

    y_out = ap(nc.dram_tensor("y", [T, D], F32, kind="ExternalOutput")) if stage >= 3 else None

    whid_bf = ap(nc.dram_tensor("whid_bf", [D, 2 * HV], BF16, kind="Internal"))
    wout_bf = ap(nc.dram_tensor("wout_bf", [HV, D], BF16, kind="Internal"))
    dwt_bf = ap(nc.dram_tensor("dwt_bf", [II, D], BF16, kind="Internal"))
    k_cin = ap(nc.dram_tensor("k_cin", [128, T], BF16, kind="Internal"))
    k_call = ap(nc.dram_tensor("k_call", [NCORES * 128, T], BF16, kind="Internal"))
    v_cin = [ap(nc.dram_tensor(f"v_cin{c}", [T, 1024], BF16, kind="Internal")) for c in range(4)]
    v_call = [ap(nc.dram_tensor(f"v_call{c}", [S, 1024], BF16, kind="Internal")) for c in range(4)]
    ln2_dram = ap(nc.dram_tensor("ln2_dram", [T, D], BF16, kind="Internal"))
    x2_dram = ap(nc.dram_tensor("x2_dram", [T, D], F32, kind="Internal"))

    A = lambda name, shape, dt: nc.alloc_sbuf_tensor(name, shape, dt).ap()
    ident = A("ident", [128, 128], F32)
    stats = A("stats", [128, 96], F32)
    acst = A("acst", [128, 4], F32)
    nrm_pp = A("nrm_pp", [128, 8], F32)
    bhid_g = A("bhid_g", [128, 32], F32)
    gs_pp = A("gs_pp", [128, ICN], F32)
    us_pp = A("us_pp", [128, ICN], F32)
    ones_s = A("ones_s", [1, 128], F32)
    wqk_b = A("wqk_b", [128, DC * 128], BF16)
    q_bf = A("q_bf", [128, T], BF16)
    k_own = A("k_own", [128, T], BF16)

    def region(name, nbytes):
        h = nc.alloc_sbuf_tensor(name, [128, nbytes], mybir.dt.uint8)
        return nc.lookup_mloc(h).addr

    def at(name, base, off, shape, dt, transpose_dest=False):
        return nc.alloc_sbuf_tensor_at(name, shape, dt, offset=base + off).ap()

    m8 = region("M8", 8192)
    ln1_tm = at("ln1_tm", m8, 0, [128, D], F32)
    qk_f32 = at("qk_f32", m8, 0, [128, T], F32)
    simscr = at("simscr", m8, 4096, [128, 1024], F32)

    x16 = region("X16", 16384)
    x_s = at("x_s", x16, 0, [128, 2 * D], F32)
    vst = at("vst", x16, 0, [128, 2 * 4096], BF16)
    x2h = at("x2h", x16, 0, [128, 2 * 1024], F32)

    s8 = region("S8", 8192)
    sqscr = at("sqscr", s8, 0, [128, D], F32)
    gate_f = at("gate_f", s8, 0, [128, 4 * 512], F32)
    t1_s = at("t1_s", s8, 0, [128, D], F32)
    gs_scr = at("gs_scr", s8, 0, [128, 2 * 1024], F32)

    r1 = region("R1", 32768)
    ln1_fm = at("ln1_fm", r1, 0, [128, DC * 1024], BF16)
    ln2_fm = at("ln2_fm", r1, 0, [128, DC * 1024], BF16)
    ds_row = at("ds_row", r1, 0, [1, D], F32)
    ds_bc = at("ds_bc", r1, 8192, [128, D], F32)

    r2 = region("R2", 16384)
    k_full = at("k_full", r2, 0, [128, S], BF16)
    gw_sb = at("gw_sb", r2, 0, [128, 2 * 4096], BF16)
    dw_sb = at("dw_sb", r2, 0, [128, 2 * 4096], BF16)

    r3 = region("R3", 16384)
    wg_sb = at("wg_sb", r3, 0, [128, DC * 512], BF16)
    bhv_row = at("bhv_row", r3, 0, [1, HV], F32)
    uw_sb = at("uw_sb", r3, 0, [128, 2 * 4096], BF16)
    out_sb = at("out_sb", r3, 0, [128, 2 * 1024], F32)

    ra = region("RA", 98304)
    v_sb = at("v_sb", ra, 0, [128, 2 * 1024], BF16)
    bhv_bc = at("bhv_bc", ra, 4096, [128, HV], F32)
    wv_sb = at("wv_sb", ra, 20480, [128, 2 * 16384], BF16)
    attnT = at("attnT", ra, 0, [128, KT * 512], BF16)
    om = at("om", ra, 65536, [128, 32 * 512], BF16)
    x2_sb = at("x2_sb", ra, 0, [128, 2 * D], F32)
    ln2_tm = at("ln2_tm", ra, 16384, [128, 2 * D], BF16)
    wo_sb = at("wo_sb", ra, 24576, [128, 2 * D], BF16)
    bout_bc = at("bout_bc", ra, 32768, [128, D], F32)
    bout_row = at("bout_row", ra, 40960, [1, D], F32)
    hdn = at("hdn", ra, 0, [128, ICN * 1024], BF16)

    PS_A = nc.alloc_psum_tensor("PS_A", [128, 2048], F32).ap()
    PS_B = nc.alloc_psum_tensor("PS_B", [128, 2048], F32).ap()

    def bankap(i, width=512):
        t = PS_A if i < 4 else PS_B
        return t[:, (i % 4) * 512:(i % 4) * 512 + width]

    p = Plan(nc, stack)
    pg = {}

    def bank_waits(banks):
        w = {}
        for b in banks:
            if b in pg:
                s, v = pg[b]
                w[s] = max(w.get(s, 0), v)
        return list(w.items())

    def free_banks(banks, semname, v):
        for b in banks:
            pg[b] = (semname, v)

    p.op("gpsimd", lambda g: g.memset(acst[:, 0:1], EPS))
    p.op("gpsimd", lambda g: g.memset(acst[:, 1:2], -0.5))
    p.op("gpsimd", lambda g: g.memset(acst[:, 2:3], 1.0 / S))
    p.op("gpsimd", lambda g: g.memset(ones_s, 1.0))
    p.op("gpsimd", lambda g: g.memset(ident, 0.0))
    sg_id = p.op(
        "gpsimd",
        lambda g: g.affine_select(
            out=ident, in_=ident, compare_op=ALU.not_equal, fill=1.0,
            base=0, pattern=[[-1, 128]], channel_multiplier=1),
        inc="sg")
    sw_wqk = p.dma(
        "gpsimd",
        wqk_b.rearrange("q (c e) -> q c e", c=DC),
        wqk_in.rearrange("(c q) e -> q c e", q=128),
        "swqk")
    scst = None
    for (dst, src) in ((nrm_pp, nrm_in), (bhid_g, bhg_in), (gs_pp, gs_in),
                       (us_pp, us_in)):
        scst = p.dma("sync", dst, src, "scst")
    sbhr = p.dma("sync", bhv_row, bhv_in, "sbhr")

    for j in range(4):
        p.dma("gpsimd", whid_bf[:, j * 2048:(j + 1) * 2048],
              whid_in[:, j * 2048:(j + 1) * 2048], f"scwb{j}")

    st_setup = None
    for j in range(8):
        st_setup = p.op(
            "tensor",
            lambda tn, _j=j: tn.matmul(
                bankap(_j % 4) if _j < 4 else bankap(4 + _j % 4),
                ones_s, bhv_row[:, _j * 512:(_j + 1) * 512],
                start=True, stop=True),
            waits=[("sbhr", sbhr)] if j == 0 else (),
            inc="st" if j == 7 else None)
    sv_setup = p.op("vector", lambda v: v.tensor_copy(bhv_bc[:, 0:2048], PS_A),
                    waits=[("st", st_setup)])
    sv_setup = p.op("vector", lambda v: v.tensor_copy(bhv_bc[:, 2048:4096], PS_B),
                    inc="sv")
    free_banks(list(range(8)), "sv", sv_setup)

    sv_xfree1 = [None] * TT
    sxr_cnt = {0: 0, 1: 0}

    def xs_tile(par):
        return x_s[:, par * D:(par + 1) * D]

    for t in range(TT):
        par = t % 2
        waits = []
        if t >= 2:
            waits.append(("sv", sv_xfree1[t - 2]))
        sxr_cnt[par] = p.dma("sync", xs_tile(par), x_in[t * 128:(t + 1) * 128, :],
                             f"sxr{par}", waits=waits)
        w = [(f"sxr{par}", sxr_cnt[par])]
        p.op("vector", lambda v, _p=par, _t=t: v.reduce_sum(
            stats[:, _t:_t + 1], xs_tile(_p), AX.X), waits=w)
        sv_xfree1[t] = p.op(
            "vector",
            lambda v, _p=par, _t=t: v.scalar_tensor_tensor(
                sqscr, xs_tile(_p), 1.0, xs_tile(_p), ALU.mult, ALU.mult,
                accum_out=stats[:, 8 + _t:9 + _t]),
            inc="sv")

    p.op("vector", lambda v: v.tensor_scalar(
        stats[:, 16:24], stats[:, 0:8], 1.0 / D, None, ALU.mult),
        waits=[("sv", sv_xfree1[TT - 1])])
    c1 = p.op("vector", lambda v: v.scalar_tensor_tensor(
        stats[:, 48:56], stats[:, 0:8], 1.0 / (D * D), stats[:, 0:8],
        ALU.mult, ALU.mult), inc="sv")
    c2 = p.op("vector", lambda v: v.scalar_tensor_tensor(
        stats[:, 56:64], stats[:, 8:16], 1.0 / D, stats[:, 48:56],
        ALU.mult, ALU.subtract), waits=[("sv", c1)], inc="sv")
    a1 = p.op("scalar", lambda s: s.activation(
        stats[:, 48:56], stats[:, 56:64], AF.Ln, bias=acst[:, 0:1]),
        waits=[("sv", c2), ("sg", sg_id)], inc="sa")
    a2 = p.op("scalar", lambda s: s.activation(
        stats[:, 24:32], stats[:, 48:56], AF.Exp, scale=acst[:, 1:2]),
        waits=[("sa", a1)], inc="sa")
    p.op("vector", lambda v: v.scalar_tensor_tensor(
        stats[:, 32:40], stats[:, 16:24], -1.0, stats[:, 24:32],
        ALU.mult, ALU.mult), waits=[("sa", a2)])
    c3 = p.op("vector", lambda v: v.tensor_scalar(
        stats[:, 40:48], stats[:, 24:32], 1.0, None, ALU.add), inc="sv")

    st_tr = [[None] * 4 for _ in range(TT)]
    sv_evac = [[None] * 4 for _ in range(TT)]
    sv_nrm = [None] * TT
    ln1_fm3 = ln1_fm.rearrange("q (dc tok) -> q dc tok", dc=DC)
    for t in range(TT):
        par = t % 2
        waits = [("sv", c3 if t < 2 else sv_nrm[t - 2])]
        sxr_cnt[par] = p.dma("sync", xs_tile(par), x_in[t * 128:(t + 1) * 128, :],
                             f"sxr{par}", waits=waits)
        w = [(f"sxr{par}", sxr_cnt[par]), ("sv", c3)]
        if t >= 1:
            w.append(("st", st_tr[t - 1][3]))
        sv_nrm[t] = p.op(
            "vector",
            lambda v, _p=par, _t=t: v.tensor_scalar(
                ln1_tm, xs_tile(_p), stats[:, 24 + _t:25 + _t],
                stats[:, 32 + _t:33 + _t], ALU.mult, ALU.add),
            waits=w, inc="sv")
        for g in range(4):
            gpar = (t * 4 + g) % 2
            psv = PS_B[:, 1024 + gpar * 512:1024 + gpar * 512 + 512]
            w = [("sv", sv_nrm[t]), ("sg", sg_id)]
            prev = (t * 4 + g) - 2
            if prev >= 0:
                w.append(("sv", sv_evac[prev // 4][prev % 4]))
            for i in range(4):
                inc = "st" if i == 3 else None
                r = p.op(
                    "tensor",
                    lambda tn, _i=i, _g=g, _t=t, _ps=psv: tn.transpose(
                        _ps[:, _i * 128:(_i + 1) * 128],
                        ln1_tm[:, (_g * 4 + _i) * 128:(_g * 4 + _i + 1) * 128],
                        ident),
                    waits=w if i == 0 else (), inc=inc)
            st_tr[t][g] = r
            sv_evac[t][g] = p.op(
                "vector",
                lambda v, _g=g, _t=t, _ps=psv: v.tensor_copy(
                    ln1_fm3[:, _g * 4:(_g + 1) * 4, _t * 128:(_t + 1) * 128],
                    _ps.rearrange("q (i f) -> q i f", i=4)),
                waits=[("st", st_tr[t][g])], inc="sv")

    pg[6] = ("sv", sv_evac[TT - 1][2])
    pg[7] = ("sv", sv_evac[TT - 1][3])

    wv3 = wv_sb.rearrange("q (b dc h) -> q b dc h", b=2, dc=16)
    swv = {0: 0, 1: 0}
    for c0 in range(2):
        for half in range(2):
            swv[c0] = p.dma(
                "sync",
                wv3[:, c0, half * 8:(half + 1) * 8, :],
                whid_bf[half * 1024:(half + 1) * 1024,
                        c0 * 1024:(c0 + 1) * 1024].rearrange("(dc q) h -> q dc h", q=128),
                f"swv{c0}",
                waits=[("scwb0", 16), ("scwb1", 16)] if (c0 == 0 and half == 0) else ())

    qk_banks = [0, 1]
    w = [("swqk", sw_wqk), ("sv", sv_evac[TT - 1][3]), ("scst", scst)] + bank_waits(qk_banks)
    st_qk = None
    for dc in range(DC):
        for s in range(2):
            inc = "st" if (dc == DC - 1 and s == 1) else None
            st_qk = p.op(
                "tensor",
                lambda tn, _dc=dc, _s=s: tn.matmul(
                    PS_A[:, _s * 512:(_s + 1) * 512],
                    wqk_b[:, _dc * 128:(_dc + 1) * 128],
                    ln1_fm[:, _dc * 1024 + _s * 512:_dc * 1024 + (_s + 1) * 512],
                    start=(_dc == 0), stop=(_dc == DC - 1)),
                waits=w if (dc == 0 and s == 0) else (), inc=inc) or st_qk
    sa_qk = p.op("scalar", lambda s: s.activation(
        qk_f32, PS_A[:, 0:1024], AF.Silu, bias=nrm_pp[:, 0:1]),
        waits=[("st", st_qk)], inc="sa")
    free_banks(qk_banks, "sa", sa_qk)
    p.op("vector", lambda v: v.tensor_scalar(
        q_bf, qk_f32, nrm_pp[:, 1:2], nrm_pp[:, 2:3], ALU.mult, ALU.add),
        waits=[("sa", sa_qk)])
    sv_qk = p.op("vector", lambda v: v.tensor_scalar(
        k_own, qk_f32, nrm_pp[:, 3:4], nrm_pp[:, 4:5], ALU.mult, ALU.add), inc="sv")
    skw = p.dma("scalar", k_cin, k_own, "skw", waits=[("sv", sv_qk)])
    sck = p.op(
        "gpsimd",
        lambda g: g.collective_compute(
            "AllGather", ALU.bypass, replica_groups=[list(range(NCORES))],
            ins=[k_cin.opt()], outs=[k_call.opt()]),
        waits=[("skw", skw)], inc="sck")
    skf = None
    for r in range(NCORES):
        skf = p.dma("scalar", k_full[:, r * T:(r + 1) * T],
                    k_call[r * 128:(r + 1) * 128, :], "skf",
                    waits=[("sck", sck)] if r == 0 else ())

    st_vmm_c = [None] * 4
    sa_vsl = {}
    svw_cnt = [0] * 4
    scv = [None] * 4
    sv_sb2 = v_sb.rearrange("q (b f) -> q b f", b=2)

    nvs = 0
    sa_sil = {}
    for c in range(4):
        par = c % 2
        if c >= 2:
            w = [("st", st_vmm_c[c - 2])]
            for half in range(2):
                swv[par] = p.dma(
                    "sync",
                    wv3[:, par, half * 8:(half + 1) * 8, :],
                    whid_bf[half * 1024:(half + 1) * 1024,
                            c * 1024:(c + 1) * 1024].rearrange("(dc q) h -> q dc h", q=128),
                    f"swv{par}", waits=w if half == 0 else ())
        for t in range(TT):
            tpar = t % 2
            for sl_ in range(2):
                banks = [tpar * 2 + sl_]
                w = [(f"swv{par}", swv[par])] + bank_waits(banks)
                ps = bankap(banks[0])
                for dc in range(DC):
                    inc = "st" if dc == DC - 1 else None
                    st_v = p.op(
                        "tensor",
                        lambda tn, _dc=dc, _t=t, _s=sl_, _c=c, _par=par, _ps=ps: tn.matmul(
                            _ps,
                            ln1_fm[:, _dc * 1024 + _t * 128:_dc * 1024 + (_t + 1) * 128],
                            wv_sb[:, _par * 16384 + (_dc // 8) * 8192 + (_dc % 8) * 1024 + _s * 512:
                                  _par * 16384 + (_dc // 8) * 8192 + (_dc % 8) * 1024 + (_s + 1) * 512],
                            start=(_dc == 0), stop=(_dc == DC - 1)),
                        waits=w if dc == 0 else (), inc=inc)
                vpar = nvs % 2
                w2 = [("st", st_v), ("sv", sv_setup)]
                if nvs >= 2:
                    w2.append(("sa", sa_sil[nvs - 2]))
                sv_vb = p.op(
                    "vector",
                    lambda v, _vp=vpar, _c=c, _s=sl_, _ps=ps: v.tensor_tensor(
                        simscr[:, _vp * 512:(_vp + 1) * 512], _ps,
                        bhv_bc[:, _c * 1024 + _s * 512:_c * 1024 + (_s + 1) * 512],
                        ALU.add),
                    waits=w2, inc="sv")
                free_banks(banks, "sv", sv_vb)
                w2 = [("sv", sv_vb)]
                if t >= 2:
                    w2.append((f"svw{c}p{tpar}", ((t - 2) // 2 + 1) * 16))
                sa_v = p.op(
                    "scalar",
                    lambda sc, _vp=vpar, _tp=tpar, _s=sl_: sc.activation(
                        sv_sb2[:, _tp, _s * 512:(_s + 1) * 512],
                        simscr[:, _vp * 512:(_vp + 1) * 512], AF.Silu),
                    waits=w2, inc="sa")
                sa_sil[nvs] = sa_v
                sa_vsl[(c, t, sl_)] = sa_v
                nvs += 1
            svw_cnt[c] = p.dma(
                "sync", v_cin[c][t * 128:(t + 1) * 128, :], sv_sb2[:, tpar, :],
                f"svw{c}p{tpar}", waits=[("sa", sa_vsl[(c, t, 1)])])
        st_vmm_c[c] = st_v
        scv[c] = p.op(
            "gpsimd",
            lambda g, _c=c: g.collective_compute(
                "AllGather", ALU.bypass, replica_groups=[list(range(NCORES))],
                ins=[v_cin[_c].opt()], outs=[v_call[_c].opt()]),
            waits=[(f"svw{c}p0", 64), (f"svw{c}p1", 64)], inc=f"scv{c}")

    swg_cnt = p.dma(
        "sync", wg_sb.rearrange("q (dc h) -> q dc h", dc=DC),
        whid_bf[:, HV:HV + 512].rearrange("(dc q) h -> q dc h", q=128),
        "swg", waits=[("scwb2", 16), ("scwb3", 16), ("st", st_setup)])

    if stage < 1:
        raise ValueError
    if stage == 1:
        _emit_debug_stage1(p, nc, dbg, locals())
        return _finalize(nc, p, stack)

    scwo = None
    for j in range(4):
        scwo = p.dma("gpsimd", wout_bf[j * 1024:(j + 1) * 1024, :],
                     wout_in[j * 1024:(j + 1) * 1024, :], "scwo")
    scdw = None
    for j in range(4):
        scdw = p.dma("gpsimd", dwt_bf[j * 1408:(j + 1) * 1408, :],
                     dwt_in[j * 1408:(j + 1) * 1408, :], "scdw")

    vst3 = vst.rearrange("q (b kt h) -> q b kt h", b=2, kt=8)
    svst = {0: 0, 1: 0}
    st_avm = {}
    sv_om_end = {}
    sa_gsl = {}
    st_gmm = {}
    swo = {0: 0, 1: 0}
    st_womm = {}
    sv_x2a = {}
    sv_l2n = {}
    sx2w_cnt = 0
    sl2w_cnt = 0
    sa_t1 = {}
    vst_free = {}
    st_simm_last = None

    for qb in range(QBN):
        sv_relu = [None] * KT
        sa_sqr = [None] * KT
        for kt in range(KT):
            b = kt % 4
            w = [("skf", skf)] + bank_waits([b])
            st_s = p.op(
                "tensor",
                lambda tn, _kt=kt, _qb=qb, _b=b: tn.matmul(
                    bankap(_b), k_full[:, _kt * 128:(_kt + 1) * 128],
                    q_bf[:, _qb * QW:(_qb + 1) * QW],
                    start=True, stop=True),
                waits=w, inc="st")
            st_simm_last = st_s
            w = [("st", st_s)]
            if kt >= 2:
                w.append(("sa", sa_sqr[kt - 2]))
            sv_relu[kt] = p.op(
                "vector",
                lambda v, _kt=kt, _b=b: v.tensor_scalar(
                    simscr[:, (_kt % 2) * 512:(_kt % 2) * 512 + 512],
                    bankap(_b), 0.0, None, ALU.max),
                waits=w, inc="sv")
            pg[b] = ("sv", sv_relu[kt])
            w = [("sv", sv_relu[kt])]
            if kt == 0:
                if qb == 0:
                    w += [(f"svw{c}p{pp}", 64) for c in range(4) for pp in range(2)]
                else:
                    w += [("sx2w0", 32), ("sx2w1", 32), ("sl2w0", 32), ("sl2w1", 32)]
            sa_sqr[kt] = p.op(
                "scalar",
                lambda sc, _kt=kt: sc.activation(
                    attnT[:, _kt * 512:(_kt + 1) * 512],
                    simscr[:, (_kt % 2) * 512:(_kt % 2) * 512 + 512],
                    AF.Square, scale=acst[:, 2:3]),
                waits=w, inc="sa")

        for hg in range(8):
            c, half = hg // 2, hg % 2
            if (qb, hg) != (0, 0):
                prev = (qb, hg - 1) if hg > 0 else (qb - 1, 7)
                w = [("scwb2", 16), ("scwb3", 16), ("st", st_gmm[prev + (3,)])]
                swg_cnt = p.dma(
                    "sync",
                    wg_sb.rearrange("q (dc h) -> q dc h", dc=DC),
                    whid_bf[:, HV + hg * 512:HV + (hg + 1) * 512].rearrange(
                        "(dc q) h -> q dc h", q=128),
                    "swg", waits=w)
            for hc in range(4):
                bnk = 4 + hc
                w = [("swg", swg_cnt)] + bank_waits([bnk])
                for dc in range(DC):
                    inc = "st" if dc == DC - 1 else None
                    r = p.op(
                        "tensor",
                        lambda tn, _dc=dc, _hc=hc, _qb=qb: tn.matmul(
                            PS_B[:, _hc * 512:(_hc + 1) * 512],
                            wg_sb[:, _dc * 512 + _hc * 128:_dc * 512 + (_hc + 1) * 128],
                            ln1_fm[:, _dc * 1024 + _qb * QW:_dc * 1024 + _qb * QW + QW],
                            start=(_dc == 0), stop=(_dc == DC - 1)),
                        waits=w if dc == 0 else (), inc=inc)
                st_gmm[(qb, hg, hc)] = r
                w = [("st", r)]
                if (qb, hg) != (0, 0):
                    prevhg = (qb, hg - 1) if hg > 0 else (qb - 1, 7)
                    w.append(("sv", sv_om_end[prevhg]))
                if qb == 1 and hg == 0:
                    w.append(("sv", sv_l2n[(qb - 1) * 4 + 3]))
                sa_gsl[(qb, hg, hc)] = p.op(
                    "scalar",
                    lambda sc, _hc=hc, _hg=hg: sc.activation(
                        gate_f[:, _hc * 512:(_hc + 1) * 512],
                        PS_B[:, _hc * 512:(_hc + 1) * 512],
                        AF.Silu, bias=bhid_g[:, _hg * 4 + _hc:_hg * 4 + _hc + 1]),
                    waits=w, inc="sa")
                pg[4 + hc] = ("sa", sa_gsl[(qb, hg, hc)])
            for ks in range(8):
                par = ks % 2
                w = [(f"scv{c}", scv[c])]
                if par in vst_free:
                    w.append(("st", vst_free[par]))
                if qb == 0 and hg == 0 and ks < 2:
                    w.append(("sv", sv_nrm[TT - 1]))
                if qb == 1 and hg == 0 and ks < 2:
                    w.append(("sv", sa_t1[(qb - 1) * 4 + 3]))
                svst[par] = p.dma(
                    "sync",
                    vst3[:, par, :, :],
                    v_call[c][ks * 1024:(ks + 1) * 1024,
                              half * 512:(half + 1) * 512].rearrange(
                                  "(kt q) h -> q kt h", q=128),
                    f"svst{par}", waits=w)
                w = [(f"svst{par}", svst[par]), ("sa", sa_sqr[min(ks * 8 + 7, KT - 1)])]
                if ks == 0:
                    w += bank_waits([0, 1, 2, 3])
                first = True
                for kt in range(8):
                    for hc in range(4):
                        inc = "st" if (kt == 7 and hc == 3) else None
                        r = p.op(
                            "tensor",
                            lambda tn, _ks=ks, _kt=kt, _hc=hc, _par=par: tn.matmul(
                                bankap(_hc),
                                vst[:, _par * 4096 + _kt * 512 + _hc * 128:
                                    _par * 4096 + _kt * 512 + (_hc + 1) * 128],
                                attnT[:, (_ks * 8 + _kt) * 512:(_ks * 8 + _kt + 1) * 512],
                                start=(_ks == 0 and _kt == 0), stop=(_ks == 7 and _kt == 7)),
                            waits=w if first else (), inc=inc)
                        first = False
                st_avm[(qb, hg, ks)] = r
                vst_free[par] = r
            last = None
            for hc in range(4):
                w = [("st", st_avm[(qb, hg, 7)]), ("sa", sa_gsl[(qb, hg, hc)])]
                last = p.op(
                    "vector",
                    lambda v, _hc=hc, _hg=hg: v.tensor_tensor(
                        om[:, (_hg * 4 + _hc) * 512:(_hg * 4 + _hc + 1) * 512],
                        bankap(_hc), gate_f[:, _hc * 512:(_hc + 1) * 512], ALU.mult),
                    waits=w, inc="sv")
                pg[hc] = ("sv", last)
            sv_om_end[(qb, hg)] = last

        sbo = p.dma("sync", bout_row, bout_in, f"sbo{qb}",
                    waits=[("st", st_avm[(qb, 7, 7)])])
        st_bo = None
        w = [(f"sbo{qb}", sbo)] + bank_waits([0, 1, 2, 3])
        for sl_ in range(4):
            st_bo = p.op(
                "tensor",
                lambda tn, _sl=sl_: tn.matmul(
                    PS_A[:, _sl * 512:(_sl + 1) * 512], ones_s,
                    bout_row[:, _sl * 512:(_sl + 1) * 512], start=True, stop=True),
                waits=w if sl_ == 0 else (), inc="st" if sl_ == 3 else None)
        sv_bo = p.op("vector", lambda v: v.tensor_copy(bout_bc, PS_A),
                     waits=[("st", st_bo)], inc="sv")
        free_banks([0, 1, 2, 3], "sv", sv_bo)
        for qtp in range(2):
            qts = [qb * 4 + qtp * 2, qb * 4 + qtp * 2 + 1]
            first_w = bank_waits(list(range(8)))
            for hc in range(32):
                wpar = hc % 2
                w = [("scwo", 64)]
                key = (qb, qtp, hc - 2)
                if (qb, qtp) != (0, 0) or hc >= 2:
                    if hc >= 2:
                        w.append(("st", st_womm.get(("mm", qb, qtp, hc - 2), 0)))
                    else:
                        prev = (qb, qtp - 1) if qtp > 0 else (qb - 1, 1)
                        w.append(("st", st_womm.get(("mm", prev[0], prev[1], 30 + hc), 0)))
                if qtp == 0 and hc < 2:
                    w.append(("st", st_avm[(qb, 7, 7)]))
                swo[wpar] = p.dma(
                    "sync",
                    wo_sb[:, wpar * D:(wpar + 1) * D],
                    wout_bf[hc * 128:(hc + 1) * 128, :],
                    f"swo{wpar}", waits=w)
                w2 = [(f"swo{wpar}", swo[wpar])]
                if hc == 0:
                    w2 += first_w
                firstmm = True
                for j in range(2):
                    pst = PS_A if j == 0 else PS_B
                    for sl in range(4):
                        r = p.op(
                            "tensor",
                            lambda tn, _hc=hc, _j=j, _sl=sl, _qtp=qtp, _qb=qb, _wp=wpar, _ps=pst: tn.matmul(
                                _ps[:, _sl * 512:(_sl + 1) * 512],
                                om[:, _hc * 512 + (_qtp * 2 + _j) * 128:
                                   _hc * 512 + (_qtp * 2 + _j) * 128 + 128],
                                wo_sb[:, _wp * D + _sl * 512:_wp * D + (_sl + 1) * 512],
                                start=(_hc == 0), stop=(_hc == 31)),
                            waits=w2 if firstmm else (), inc="st" if (j == 1 and sl == 3) else None)
                        firstmm = False
                st_womm[("mm", qb, qtp, hc)] = r
            for j in range(2):
                st_womm[qts[j]] = st_womm[("mm", qb, qtp, 31)]
            for j in range(2):
                qt = qts[j]
                par = qt % 2
                pst = PS_A if j == 0 else PS_B
                w = [("st", st_avm[(qb, 7, 7)])]
                if qt >= 2:
                    w.append(("sv", sa_t1[qt - 2]))
                sxr_cnt[par] = p.dma("sync", xs_tile(par),
                                     x_in[(qt % 4) * 128 + qb * QW:(qt % 4) * 128 + qb * QW + 128, :],
                                     f"sxr{par}", waits=w)
                w = [(f"sxr{par}", sxr_cnt[par]), ("sv", c3)]
                if qb == 0 and qtp == 0 and j == 0:
                    w.append(("sv", sv_om_end[(qb, 7)]))
                sa_t1[qt] = p.op(
                    "vector",
                    lambda v, _p=par, _qt=qt: v.tensor_scalar(
                        t1_s, xs_tile(_p), stats[:, 40 + _qt:41 + _qt],
                        stats[:, 32 + _qt:33 + _qt], ALU.mult, ALU.add),
                    waits=w, inc="sv")
                w = [("st", st_womm[qt]), ("sv", sa_t1[qt])]
                if qt >= 2:
                    w.append((f"sx2w{qt % 2}", 16 * ((qt - 2) // 2 + 1)))
                p.op(
                    "vector",
                    lambda v, _p=par, _ps=pst: v.tensor_tensor(
                        x2_sb[:, _p * D:(_p + 1) * D], _ps, bout_bc, ALU.add),
                    waits=w + [("sv", sv_bo)])
                sv_x2a[qt] = p.op(
                    "vector",
                    lambda v, _p=par: v.tensor_tensor(
                        x2_sb[:, _p * D:(_p + 1) * D], x2_sb[:, _p * D:(_p + 1) * D],
                        t1_s, ALU.add),
                    inc="sv")
                free_banks(list(range(4)) if j == 0 else [4, 5, 6, 7], "sv", sv_x2a[qt])
                p.op("vector", lambda v, _p=par, _qt=qt: v.reduce_sum(
                    stats[:, 64 + _qt:65 + _qt], x2_sb[:, _p * D:(_p + 1) * D], AX.X))
                s1 = p.op("vector", lambda v, _p=par, _qt=qt: v.scalar_tensor_tensor(
                    sqscr, x2_sb[:, _p * D:(_p + 1) * D], 1.0,
                    x2_sb[:, _p * D:(_p + 1) * D], ALU.mult, ALU.mult,
                    accum_out=stats[:, 72 + _qt:73 + _qt]), inc="sv")
                p.op("vector", lambda v, _qt=qt: v.tensor_scalar(
                    stats[:, 88:89], stats[:, 64 + _qt:65 + _qt], 1.0 / D, None, ALU.mult),
                    waits=[("sv", s1)])
                s2 = p.op("vector", lambda v, _qt=qt: v.scalar_tensor_tensor(
                    stats[:, 89:90], stats[:, 64 + _qt:65 + _qt], 1.0 / (D * D),
                    stats[:, 64 + _qt:65 + _qt], ALU.mult, ALU.mult), inc="sv")
                s3 = p.op("vector", lambda v, _qt=qt: v.scalar_tensor_tensor(
                    stats[:, 90:91], stats[:, 72 + _qt:73 + _qt], 1.0 / D,
                    stats[:, 89:90], ALU.mult, ALU.subtract),
                    waits=[("sv", s2)], inc="sv")
                a3 = p.op("scalar", lambda s: s.activation(
                    stats[:, 91:92], stats[:, 90:91], AF.Ln, bias=acst[:, 0:1]),
                    waits=[("sv", s3)], inc="sa")
                a4 = p.op("scalar", lambda s, _qt=qt: s.activation(
                    stats[:, 80 + _qt:81 + _qt], stats[:, 91:92], AF.Exp,
                    scale=acst[:, 1:2]), waits=[("sa", a3)], inc="sa")
                s4 = p.op("vector", lambda v, _qt=qt: v.scalar_tensor_tensor(
                    stats[:, 92:93], stats[:, 88:89], -1.0,
                    stats[:, 80 + _qt:81 + _qt], ALU.mult, ALU.mult),
                    waits=[("sa", a4)], inc="sv")
                w = [("sv", s4)]
                if qt >= 2:
                    w.append((f"sl2w{qt % 2}", 16 * ((qt - 2) // 2 + 1)))
                sv_l2n[qt] = p.op(
                    "vector",
                    lambda v, _p=par, _qt=qt: v.tensor_scalar(
                        ln2_tm[:, _p * D:(_p + 1) * D], x2_sb[:, _p * D:(_p + 1) * D],
                        stats[:, 80 + _qt:81 + _qt], stats[:, 92:93],
                        ALU.mult, ALU.add),
                    waits=w, inc="sv")
                sx2w_cnt = p.dma("scalar", x2_dram[qt * 128:(qt + 1) * 128, :],
                                 x2_sb[:, par * D:(par + 1) * D], f"sx2w{qt % 2}",
                                 waits=[("sv", sv_x2a[qt])])
                sl2w_cnt = p.dma("scalar", ln2_dram[qt * 128:(qt + 1) * 128, :],
                                 ln2_tm[:, par * D:(par + 1) * D], f"sl2w{qt % 2}",
                                 waits=[("sv", sv_l2n[qt])])

    if stage == 2:
        _emit_debug_stage2(p, nc, dbg, locals())
        return _finalize(nc, p, stack)

    w = [("sl2w0", 64), ("sl2w1", 64), ("st", st_gmm[(1, 7, 3)])]
    for dc in range(DC):
        p.dma("sync", ln2_fm[:, dc * 1024:(dc + 1) * 1024],
              ln2_dram[:, dc * 128:(dc + 1) * 128], f"sl2t{dc}",
              waits=w if dc == 0 else (), transpose=True)

    gw3 = gw_sb.rearrange("q (b dc i) -> q b dc i", b=2, dc=DC)
    uw3 = uw_sb.rearrange("q (b dc i) -> q b dc i", b=2, dc=DC)
    sfg = {0: 0, 1: 0}
    sfu = {0: 0, 1: 0}
    st_gumm = [None] * ICN
    sa_gs2 = [None] * ICN
    sv_hdn = [None] * ICN
    npairs = ICN // 2
    for pr in range(npairs):
        par = pr % 2
        w = []
        if pr >= 2:
            w.append(("st", st_gumm[(pr - 2) * 2 + 1]))
        if pr < 2:
            w.append(("st", st_simm_last))
            w.append(("st", st_womm[7]))
        sfg[par] = p.dma(
            "gpsimd", gw3[:, par, :, :],
            gwt_in[:, pr * 256:(pr + 1) * 256].rearrange("(dc q) i -> q dc i", q=128),
            f"sfg{par}", waits=w)
        sfu[par] = p.dma(
            "gpsimd", uw3[:, par, :, :],
            uwt_in[:, pr * 256:(pr + 1) * 256].rearrange("(dc q) i -> q dc i", q=128),
            f"sfu{par}")
        for ic in (2 * pr, 2 * pr + 1):
            icp = ic % 2
            psg = (PS_A if icp == 0 else PS_B)[:, 0:1024]
            psu = (PS_A if icp == 0 else PS_B)[:, 1024:2048]
            banks = [0, 1, 2, 3] if icp == 0 else [4, 5, 6, 7]
            w = [(f"sfg{par}", sfg[par]), (f"sfu{par}", sfu[par])] + bank_waits(banks)
            first = True
            for dc in range(DC):
                for sl_ in range(2):
                    wdc = []
                    if ic == 0 and sl_ == 0:
                        wdc = [(f"sl2t{dc}", 16)]
                    p.op(
                        "tensor",
                        lambda tn, _dc=dc, _s=sl_, _ic=ic, _par=par, _ps=psg: tn.matmul(
                            _ps[:, _s * 512:(_s + 1) * 512],
                            gw_sb[:, _par * 4096 + _dc * 256 + (_ic % 2) * 128:
                                  _par * 4096 + _dc * 256 + (_ic % 2) * 128 + 128],
                            ln2_fm[:, _dc * 1024 + _s * 512:_dc * 1024 + (_s + 1) * 512],
                            start=(_dc == 0), stop=(_dc == DC - 1)),
                        waits=(w + wdc) if first else wdc)
                    first = False
                    inc = "st" if (dc == DC - 1 and sl_ == 1) else None
                    r = p.op(
                        "tensor",
                        lambda tn, _dc=dc, _s=sl_, _ic=ic, _par=par, _ps=psu: tn.matmul(
                            _ps[:, _s * 512:(_s + 1) * 512],
                            uw_sb[:, _par * 4096 + _dc * 256 + (_ic % 2) * 128:
                                  _par * 4096 + _dc * 256 + (_ic % 2) * 128 + 128],
                            ln2_fm[:, _dc * 1024 + _s * 512:_dc * 1024 + (_s + 1) * 512],
                            start=(_dc == 0), stop=(_dc == DC - 1)),
                        inc=inc)
            st_gumm[ic] = r
            w = [("st", st_gumm[ic])]
            if ic >= 2:
                w.append(("sv", sv_hdn[ic - 2]))
            if ic == 0:
                w.append(("sv", sv_l2n[7]))
            sa_gs2[ic] = p.op(
                "scalar",
                lambda sc, _ic=ic, _ps=psg: sc.activation(
                    gs_scr[:, (_ic % 2) * 1024:(_ic % 2) * 1024 + 1024], _ps,
                    AF.Silu, scale=gs_pp[:, _ic:_ic + 1]),
                waits=w, inc="sa")
            w = [("sa", sa_gs2[ic])]
            if ic == 0:
                w += [("sx2w0", 64), ("sx2w1", 64), ("sl2w0", 64), ("sl2w1", 64),
                      ("st", st_womm[7])]
            sv_hdn[ic] = p.op(
                "vector",
                lambda v, _ic=ic, _ps=psu: v.scalar_tensor_tensor(
                    hdn[:, _ic * 1024:(_ic + 1) * 1024], _ps, us_pp[:, _ic:_ic + 1],
                    gs_scr[:, (_ic % 2) * 1024:(_ic % 2) * 1024 + 1024],
                    ALU.mult, ALU.mult),
                waits=w, inc="sv")
            free_banks(banks, "sv", sv_hdn[ic])

    sdsr = p.dma("sync", ds_row, ds_in, "sdsr", waits=[("st", st_gumm[ICN - 1])])
    w = [("sdsr", sdsr)] + bank_waits([0, 1, 2, 3])
    st_dsb = None
    for sl in range(4):
        st_dsb = p.op(
            "tensor",
            lambda tn, _sl=sl: tn.matmul(
                PS_A[:, _sl * 512:(_sl + 1) * 512], ones_s,
                ds_row[:, _sl * 512:(_sl + 1) * 512], start=True, stop=True),
            waits=w if sl == 0 else (), inc="st" if sl == 3 else None)
    sv_dsb = p.op("vector", lambda v: v.tensor_copy(ds_bc, PS_A),
                  waits=[("st", st_dsb)], inc="sv")
    free_banks([0, 1, 2, 3], "sv", sv_dsb)

    dw3 = dw_sb.rearrange("q (b ic d) -> q b ic d", b=2, ic=4)
    sfd = {0: 0, 1: 0}
    sxf = {0: 0, 1: 0}
    syw_cnt = 0
    st_dn = {}
    sv_out = {}
    outp = out_sb.rearrange("q (b f) -> q b f", b=2)
    nouts = 0
    for quad in range(2):
        for dh in range(2):
            gdh = quad * 2 + dh
            for bt in range(11):
                par = (gdh * 11 + bt) % 2
                w = [("scdw", 64)]
                gi = gdh * 11 + bt
                if bt >= 2:
                    w.append(("st", st_dn[("mm", quad, dh, min(ICN - 1, (bt - 2) * 4 + 3))]))
                elif gdh > 0:
                    pq, pd = divmod(gdh - 1, 2)
                    w.append(("st", st_dn[("mm", pq, pd, min(ICN - 1, (bt + 9) * 4 + 3))]))
                if gi < 2:
                    w.append(("sv", sv_hdn[ICN - 1]))
                sfd[par] = p.dma(
                    "sync", dw3[:, par, :, :],
                    dwt_bf[bt * 512:(bt + 1) * 512,
                           dh * 1024:(dh + 1) * 1024].rearrange("(ic q) d -> q ic d", q=128),
                    f"sfd{par}", waits=w)
                for ic in range(bt * 4, min(ICN, (bt + 1) * 4)):
                    w2 = [(f"sfd{par}", sfd[par])]
                    if ic == 0:
                        w2 += bank_waits(list(range(8)))
                    first = True
                    for j in range(4):
                        qt = quad * 4 + j
                        psq = (PS_A if j < 2 else PS_B)[:, (j % 2) * 1024:(j % 2) * 1024 + 1024]
                        for s_ in range(2):
                            inc = "st" if (j == 3 and s_ == 1) else None
                            r = p.op(
                                "tensor",
                                lambda tn, _ic=ic, _j=j, _s=s_, _qt=qt, _par=par, _ps=psq: tn.matmul(
                                    _ps[:, _s * 512:(_s + 1) * 512],
                                    hdn[:, _ic * 1024 + _qt * 128:_ic * 1024 + (_qt + 1) * 128],
                                    dw_sb[:, _par * 4096 + (_ic % 4) * 1024 + _s * 512:
                                          _par * 4096 + (_ic % 4) * 1024 + (_s + 1) * 512],
                                    start=(_ic == 0), stop=(_ic == ICN - 1)),
                                waits=w2 if first else (), inc=inc)
                            first = False
                    st_dn[("mm", quad, dh, ic)] = r
            for j in range(2):
                qt = quad * 4 + j
                opar = (nouts + j) % 2
                w = []
                if nouts + j >= 2:
                    w.append(("sv", sv_out[nouts + j - 2]))
                else:
                    w.append(("sv", sv_l2n[7]))
                sxf[opar] = p.dma(
                    "sync", x2h[:, opar * 1024:(opar + 1) * 1024],
                    x2_dram[qt * 128:(qt + 1) * 128, dh * 1024:(dh + 1) * 1024],
                    f"sxf{opar}", waits=w)
            for j in range(4):
                qt = quad * 4 + j
                psq = (PS_A if j < 2 else PS_B)[:, (j % 2) * 1024:(j % 2) * 1024 + 1024]
                opar = nouts % 2
                if j >= 2:
                    sxf[opar] = p.dma(
                        "sync", x2h[:, opar * 1024:(opar + 1) * 1024],
                        x2_dram[qt * 128:(qt + 1) * 128, dh * 1024:(dh + 1) * 1024],
                        f"sxf{opar}", waits=[("sv", sv_out[nouts - 2])])
                w = [("st", st_dn[("mm", quad, dh, ICN - 1)]), ("sv", sv_dsb)]
                if nouts >= 2:
                    w.append((f"syw{nouts % 2}", 16 * ((nouts - 2) // 2 + 1)))
                p.op(
                    "vector",
                    lambda v, _o=opar, _dh=dh, _ps=psq: v.tensor_tensor(
                        outp[:, _o, :], _ps, ds_bc[:, _dh * 1024:(_dh + 1) * 1024],
                        ALU.mult),
                    waits=w, inc=None)
                sv_out[nouts] = p.op(
                    "vector",
                    lambda v, _o=opar: v.tensor_tensor(
                        outp[:, _o, :], outp[:, _o, :], x2h[:, _o * 1024:(_o + 1) * 1024],
                        ALU.add),
                    waits=[(f"sxf{opar}", sxf[opar])], inc="sv")
                free_banks([0, 1] if j == 0 else [2, 3] if j == 1 else [4, 5] if j == 2 else [6, 7],
                           "sv", sv_out[nouts])
                syw_cnt = p.dma("sync", y_out[qt * 128:(qt + 1) * 128, dh * 1024:(dh + 1) * 1024],
                                outp[:, opar, :], f"syw{nouts % 2}",
                                waits=[("sv", sv_out[nouts])])
                nouts += 1

    p.op("sync", lambda e: None, waits=[("syw0", 128), ("syw1", 128)])
    if dbg:
        d_ln2fm = ap(nc.dram_tensor("d_ln2fm", [128, DC * 1024], BF16, kind="ExternalOutput"))
        d_hdn = ap(nc.dram_tensor("d_hdn", [128, ICN * 1024], BF16, kind="ExternalOutput"))
        d_dsb = ap(nc.dram_tensor("d_dsb", [128, D], F32, kind="ExternalOutput"))
        so = p.dma("sync", d_ln2fm, ln2_fm, "sout", waits=[(f"sl2t{DC-1}", 16)])
        so = p.dma("sync", d_hdn, hdn, "sout", waits=[("sv", sv_hdn[ICN - 1])])
        so = p.dma("sync", d_dsb, ds_bc, "sout", waits=[("sv", sv_dsb)])
        p.op("sync", lambda e: None, waits=[("sout", so)])
    return _finalize(nc, p, stack)


def _emit_debug_stage1(p, nc, dbg, L):
    outs = {}
    for name, shape, dt in (("d_ln1fm", [128, DC * 1024], BF16),
                            ("d_qk", [128, T], F32),
                            ("d_q", [128, T], BF16),
                            ("d_kfull", [128, S], BF16),
                            ("d_v0", [S, 1024], BF16),
                            ("d_v1", [S, 1024], BF16),
                            ("d_v2", [S, 1024], BF16),
                            ("d_v3", [S, 1024], BF16),
                            ("d_vc3", [T, 1024], BF16),
                            ("d_stats", [128, 96], F32)):
        outs[name] = nc.dram_tensor(name, shape, dt, kind="ExternalOutput").ap()
    w = [("sv", L["sv_qk"]), ("sv", L["sv_evac"][TT - 1][3]), ("skf", L["skf"]),
         ("scv0", L["scv"][0]), ("scv1", L["scv"][1]), ("scv2", L["scv"][2]),
         ("scv3", L["scv"][3])]
    so = p.dma("sync", outs["d_ln1fm"], L["ln1_fm"], "sout", waits=w)
    so = p.dma("sync", outs["d_qk"], L["qk_f32"], "sout")
    so = p.dma("sync", outs["d_q"], L["q_bf"], "sout")
    so = p.dma("sync", outs["d_kfull"], L["k_full"], "sout")
    so = p.dma("sync", outs["d_v0"], L["v_call"][0], "sout")
    so = p.dma("sync", outs["d_v1"], L["v_call"][1], "sout")
    so = p.dma("sync", outs["d_v2"], L["v_call"][2], "sout")
    so = p.dma("sync", outs["d_v3"], L["v_call"][3], "sout")
    so = p.dma("sync", outs["d_vc3"], L["v_cin"][3], "sout")
    so = p.dma("sync", outs["d_stats"], L["stats"], "sout")
    p.op("sync", lambda e: None, waits=[("sout", so)])


def _emit_debug_stage2(p, nc, dbg, L):
    outs = {}
    for name, shape, dt in (("d_x2", [T, D], F32),
                            ("d_ln2", [T, D], BF16),
                            ("d_om", [128, 32 * 512], BF16),
                            ("d_at", [128, KT * 512], BF16)):
        outs[name] = nc.dram_tensor(name, shape, dt, kind="ExternalOutput").ap()
    w = [("sx2w0", 64), ("sx2w1", 64), ("sl2w0", 64), ("sl2w1", 64),
         ("sv", L["sv_om_end"][(1, 7)]), ("sa", L["sa_sqr"][KT - 1])]
    so = p.dma("sync", outs["d_x2"], L["x2_dram"], "sout", waits=w)
    so = p.dma("sync", outs["d_ln2"], L["ln2_dram"], "sout")
    so = p.dma("sync", outs["d_om"], L["om"], "sout")
    so = p.dma("sync", outs["d_at"], L["attnT"], "sout")
    p.op("sync", lambda e: None, waits=[("sout", so)])


def _finalize(nc, p, stack):
    with nc.Block() as block:
        block.tensor(lambda e: [f(e) for f in p.q["tensor"]])
        block.vector(lambda e: [f(e) for f in p.q["vector"]])
        block.scalar(lambda e: [f(e) for f in p.q["scalar"]])
        block.sync(lambda e: [f(e) for f in p.q["sync"]])
        block.gpsimd(lambda e: [f(e) for f in p.q["gpsimd"]])
    stack.close()
    return nc


def make_inputs(inputs):
    f32 = lambda a: np.ascontiguousarray(np.asarray(a, dtype=np.float32))
    x = f32(inputs["L0_attn_x"]).reshape(S, D)
    b_hidden = f32(inputs["L0_attn_b_hidden"]).reshape(-1)
    nrm = np.zeros((128, 8), dtype=np.float32)
    nrm[:, 0] = f32(inputs["L0_attn_b_qk"]).reshape(-1)
    nrm[:, 1] = f32(inputs["L0_attn_gamma_q"]).reshape(-1)
    nrm[:, 2] = f32(inputs["L0_attn_beta_q"]).reshape(-1)
    nrm[:, 3] = f32(inputs["L0_attn_gamma_k"]).reshape(-1)
    nrm[:, 4] = f32(inputs["L0_attn_beta_k"]).reshape(-1)
    shared = {
        "wqk": f32(inputs["L0_attn_W_qk"]),
        "nrm": np.ascontiguousarray(nrm),
        "bhg": np.ascontiguousarray(b_hidden[HV:].reshape(32, 128).T),
        "bhv": np.ascontiguousarray(b_hidden[:HV].reshape(1, HV)),
        "whid": f32(inputs["L0_attn_W_hidden"]),
        "wout": f32(inputs["L0_attn_W_out"]),
        "bout": f32(inputs["L0_attn_b_out"]).reshape(1, D),
        "gwt": np.ascontiguousarray(f32(inputs["L0_ffn_gate_w"]).T),
        "uwt": np.ascontiguousarray(f32(inputs["L0_ffn_up_w"]).T),
        "dwt": np.ascontiguousarray(f32(inputs["L0_ffn_down_w"]).T),
        "gs": np.ascontiguousarray(f32(inputs["L0_ffn_gate_s"]).reshape(ICN, 128).T),
        "us": np.ascontiguousarray(f32(inputs["L0_ffn_up_s"]).reshape(ICN, 128).T),
        "ds": f32(inputs["L0_ffn_down_s"]).reshape(1, D),
    }
    in_maps = []
    for cid in range(NCORES):
        m = dict(shared)
        m["x"] = np.ascontiguousarray(x[cid * T:(cid + 1) * T, :])
        in_maps.append(m)
    return in_maps


def kernel(**inputs):
    in_maps = make_inputs(inputs)
    nc = build_program(stage=3)
    res = run_bass_kernel_spmd(nc, in_maps, core_ids=list(range(NCORES)))
    y = np.concatenate([np.asarray(r["y"], dtype=np.float32) for r in res.results], axis=0)
    return y.reshape(1, S, D)


# version 04
# speedup vs baseline: 1.0274x; 1.0274x over previous
import sys

for _p in ("/opt/trn_rl_repo",):
    if _p not in sys.path:
        sys.path.insert(0, _p)

import numpy as np
import ml_dtypes
import concourse.bass as bass
import concourse.mybir as mybir
from concourse.bass_utils import run_bass_kernel_spmd

F32 = mybir.dt.float32
BF16 = mybir.dt.bfloat16
AF = mybir.ActivationFunctionType
ALU = mybir.AluOpType
AX = mybir.AxisListType

NCORES = 8
T = 1024
TT = 8
D = 2048
DC = 16
E = 128
HV = 4096
S = 8192
KT = 64
QBN = 2
QW = 512
II = 5632
ICN = 44
EPS = 1e-5

ENGINES = ("tensor", "vector", "scalar", "sync", "gpsimd")


class Plan:

    def __init__(self, nc, stack):
        self.nc = nc
        self.stack = stack
        self.q = {e: [] for e in ENGINES}
        self.sems = {}
        self.cnt = {}
        self.last_wait = {e: {} for e in ENGINES}

    def sem(self, name):
        if name not in self.sems:
            self.sems[name] = self.stack.enter_context(self.nc.semaphore(name))
            self.cnt[name] = 0
        return self.sems[name]

    def bump(self, name, k=1):
        self.sem(name)
        self.cnt[name] += k
        return self.cnt[name]

    def op(self, eng, fn, waits=(), inc=None, inck=1):
        real_waits = []
        lw = self.last_wait[eng]
        for (s, v) in waits:
            assert v is not None, f"None wait on {s}"
            if v <= 0:
                continue
            assert v <= self.cnt.get(s, 0), f"wait on future {s}={v} (have {self.cnt.get(s, 0)})"
            if lw.get(s, 0) >= v:
                continue
            lw[s] = v
            real_waits.append((self.sem(s), v))
        incsem = self.sem(inc) if inc else None
        target = self.bump(inc, inck) if inc else None

        def run(e, _w=tuple(real_waits), _fn=fn, _is=incsem, _k=inck):
            for (sh, v) in _w:
                e.wait_ge(sh, v)
            inst = _fn(e)
            if _is is not None:
                inst.then_inc(_is, _k)

        self.q[eng].append(run)
        return target

    def dma(self, eng, out, in_, semname, waits=(), transpose=False):
        def fn(e, _o=out, _i=in_, _t=transpose):
            if _t:
                return e.dma_start(out=_o, in_=_i, transpose=True)
            return e.dma_start(out=_o, in_=_i)

        return self.op(eng, fn, waits=waits, inc=semname, inck=16)


def build_program(stage=3, dbg=False):
    nc = bass.Bass("TRN2", target_bir_lowering=False, debug=False)
    import contextlib

    stack = contextlib.ExitStack()

    ap = lambda h: h.ap()
    x_in = ap(nc.dram_tensor("x", [T, D], F32, kind="ExternalInput"))
    wqk_in = ap(nc.dram_tensor("wqk", [D, E], F32, kind="ExternalInput"))
    nrm_in = ap(nc.dram_tensor("nrm", [128, 8], F32, kind="ExternalInput"))
    bhg_in = ap(nc.dram_tensor("bhg", [128, 32], F32, kind="ExternalInput"))
    bhv_in = ap(nc.dram_tensor("bhv", [1, HV], F32, kind="ExternalInput"))
    whid_in = ap(nc.dram_tensor("whid", [D, 2 * HV], F32, kind="ExternalInput"))
    wout_in = ap(nc.dram_tensor("wout", [HV, D], F32, kind="ExternalInput"))
    bout_in = ap(nc.dram_tensor("bout", [1, D], F32, kind="ExternalInput"))
    gwt_in = ap(nc.dram_tensor("gwt", [D, II], F32, kind="ExternalInput"))
    uwt_in = ap(nc.dram_tensor("uwt", [D, II], F32, kind="ExternalInput"))
    dwt_in = ap(nc.dram_tensor("dwt", [II, D], F32, kind="ExternalInput"))
    gs_in = ap(nc.dram_tensor("gs", [128, ICN], F32, kind="ExternalInput"))
    us_in = ap(nc.dram_tensor("us", [128, ICN], F32, kind="ExternalInput"))
    ds_in = ap(nc.dram_tensor("ds", [1, D], F32, kind="ExternalInput"))

    y_out = ap(nc.dram_tensor("y", [T, D], F32, kind="ExternalOutput")) if stage >= 3 else None

    whid_bf = ap(nc.dram_tensor("whid_bf", [D, 2 * HV], BF16, kind="Internal"))
    wout_bf = ap(nc.dram_tensor("wout_bf", [HV, D], BF16, kind="Internal"))
    dwt_bf = ap(nc.dram_tensor("dwt_bf", [II, D], BF16, kind="Internal"))
    k_cin = ap(nc.dram_tensor("k_cin", [128, T], BF16, kind="Internal"))
    k_call = ap(nc.dram_tensor("k_call", [NCORES * 128, T], BF16, kind="Internal"))
    v_cin = [ap(nc.dram_tensor(f"v_cin{c}", [T, 1024], BF16, kind="Internal")) for c in range(4)]
    v_call = [ap(nc.dram_tensor(f"v_call{c}", [S, 1024], BF16, kind="Internal")) for c in range(4)]
    ln2_dram = ap(nc.dram_tensor("ln2_dram", [T, D], BF16, kind="Internal"))
    x2_dram = ap(nc.dram_tensor("x2_dram", [T, D], F32, kind="Internal"))

    A = lambda name, shape, dt: nc.alloc_sbuf_tensor(name, shape, dt).ap()
    ident = A("ident", [128, 128], F32)
    stats = A("stats", [128, 96], F32)
    acst = A("acst", [128, 4], F32)
    nrm_pp = A("nrm_pp", [128, 8], F32)
    bhid_g = A("bhid_g", [128, 32], F32)
    gs_pp = A("gs_pp", [128, ICN], F32)
    us_pp = A("us_pp", [128, ICN], F32)
    ones_s = A("ones_s", [1, 128], F32)
    wqk_b = A("wqk_b", [128, DC * 128], BF16)
    q_bf = A("q_bf", [128, T], BF16)
    k_own = A("k_own", [128, T], BF16)

    def region(name, nbytes):
        h = nc.alloc_sbuf_tensor(name, [128, nbytes], mybir.dt.uint8)
        return nc.lookup_mloc(h).addr

    def at(name, base, off, shape, dt, transpose_dest=False):
        return nc.alloc_sbuf_tensor_at(name, shape, dt, offset=base + off).ap()

    m8 = region("M8", 8192)
    ln1_tm = at("ln1_tm", m8, 0, [128, D], F32)
    qk_f32 = at("qk_f32", m8, 0, [128, T], F32)
    simscr = at("simscr", m8, 4096, [128, 1024], F32)

    x16 = region("X16", 16384)
    x_s = at("x_s", x16, 0, [128, 2 * D], F32)
    vst = at("vst", x16, 0, [128, 2 * 4096], BF16)
    x2h = at("x2h", x16, 0, [128, 2 * 1024], F32)

    s8 = region("S8", 8192)
    sqscr = at("sqscr", s8, 0, [128, D], F32)
    gate_f = at("gate_f", s8, 0, [128, 4 * 512], F32)
    t1_s = at("t1_s", s8, 0, [128, D], F32)
    gs_scr = at("gs_scr", s8, 0, [128, 2 * 1024], F32)

    r1 = region("R1", 32768)
    ln1_fm = at("ln1_fm", r1, 0, [128, DC * 1024], BF16)
    ln2_fm = at("ln2_fm", r1, 0, [128, DC * 1024], BF16)
    ds_row = at("ds_row", r1, 0, [1, D], F32)
    ds_bc = at("ds_bc", r1, 8192, [128, D], F32)

    r2 = region("R2", 16384)
    k_full = at("k_full", r2, 0, [128, S], BF16)
    gw_sb = at("gw_sb", r2, 0, [128, 2 * 4096], BF16)
    dw_sb = at("dw_sb", r2, 0, [128, 2 * 4096], BF16)

    r3 = region("R3", 16384)
    wg_sb = at("wg_sb", r3, 0, [128, DC * 512], BF16)
    bhv_row = at("bhv_row", r3, 0, [1, HV], F32)
    uw_sb = at("uw_sb", r3, 0, [128, 2 * 4096], BF16)
    out_sb = at("out_sb", r3, 0, [128, 2 * 1024], F32)

    ra = region("RA", 98304)
    v_sb = at("v_sb", ra, 0, [128, 2 * 1024], BF16)
    bhv_bc = at("bhv_bc", ra, 4096, [128, HV], F32)
    wv_sb = at("wv_sb", ra, 20480, [128, 2 * 16384], BF16)
    attnT = at("attnT", ra, 0, [128, KT * 512], BF16)
    om = at("om", ra, 65536, [128, 32 * 512], BF16)
    x2_sb = at("x2_sb", ra, 0, [128, 2 * D], F32)
    ln2_tm = at("ln2_tm", ra, 16384, [128, 2 * D], BF16)
    wo_sb = at("wo_sb", ra, 24576, [128, 2 * D], BF16)
    bout_bc = at("bout_bc", ra, 32768, [128, D], F32)
    bout_row = at("bout_row", ra, 40960, [1, D], F32)
    hdn = at("hdn", ra, 0, [128, ICN * 1024], BF16)

    PS_A = nc.alloc_psum_tensor("PS_A", [128, 2048], F32).ap()
    PS_B = nc.alloc_psum_tensor("PS_B", [128, 2048], F32).ap()

    def bankap(i, width=512):
        t = PS_A if i < 4 else PS_B
        return t[:, (i % 4) * 512:(i % 4) * 512 + width]

    p = Plan(nc, stack)
    pg = {}

    def bank_waits(banks):
        w = {}
        for b in banks:
            if b in pg:
                s, v = pg[b]
                w[s] = max(w.get(s, 0), v)
        return list(w.items())

    def free_banks(banks, semname, v):
        for b in banks:
            pg[b] = (semname, v)

    p.op("gpsimd", lambda g: g.memset(acst[:, 0:1], EPS))
    p.op("gpsimd", lambda g: g.memset(acst[:, 1:2], -0.5))
    p.op("gpsimd", lambda g: g.memset(acst[:, 2:3], 1.0 / S))
    p.op("gpsimd", lambda g: g.memset(ones_s, 1.0))
    p.op("gpsimd", lambda g: g.memset(ident, 0.0))
    sg_id = p.op(
        "gpsimd",
        lambda g: g.affine_select(
            out=ident, in_=ident, compare_op=ALU.not_equal, fill=1.0,
            base=0, pattern=[[-1, 128]], channel_multiplier=1),
        inc="sg")
    sw_wqk = p.dma(
        "gpsimd",
        wqk_b.rearrange("q (c e) -> q c e", c=DC),
        wqk_in.rearrange("(c q) e -> q c e", q=128),
        "swqk")
    scst = None
    for (dst, src) in ((nrm_pp, nrm_in), (bhid_g, bhg_in), (gs_pp, gs_in),
                       (us_pp, us_in)):
        scst = p.dma("sync", dst, src, "scst")
    sbhr = p.dma("sync", bhv_row, bhv_in, "sbhr")

    for j in range(4):
        p.dma("gpsimd", whid_bf[:, j * 2048:(j + 1) * 2048],
              whid_in[:, j * 2048:(j + 1) * 2048], f"scwb{j}")

    scwo = None
    for j in range(4):
        scwo = p.dma("gpsimd", wout_bf[j * 1024:(j + 1) * 1024, :],
                     wout_in[j * 1024:(j + 1) * 1024, :], "scwo")
    scdw = None
    for j in range(4):
        scdw = p.dma("gpsimd", dwt_bf[j * 1408:(j + 1) * 1408, :],
                     dwt_in[j * 1408:(j + 1) * 1408, :], "scdw")

    st_setup = None
    for j in range(8):
        st_setup = p.op(
            "tensor",
            lambda tn, _j=j: tn.matmul(
                bankap(_j % 4) if _j < 4 else bankap(4 + _j % 4),
                ones_s, bhv_row[:, _j * 512:(_j + 1) * 512],
                start=True, stop=True),
            waits=[("sbhr", sbhr)] if j == 0 else (),
            inc="st" if j == 7 else None)
    sv_setup = p.op("vector", lambda v: v.tensor_copy(bhv_bc[:, 0:2048], PS_A),
                    waits=[("st", st_setup)])
    sv_setup = p.op("vector", lambda v: v.tensor_copy(bhv_bc[:, 2048:4096], PS_B),
                    inc="sv")
    free_banks(list(range(8)), "sv", sv_setup)

    sxr_cnt = {0: 0, 1: 0}

    def xs_tile(par):
        return x_s[:, par * D:(par + 1) * D]

    st_tr = [[None] * 4 for _ in range(TT)]
    sv_evac = [[None] * 4 for _ in range(TT)]
    sv_nrm = [None] * TT
    sa_rstd = [None] * TT
    ln1_fm3 = ln1_fm.rearrange("q (dc tok) -> q dc tok", dc=DC)
    c3 = None
    for t in range(TT):
        par = t % 2
        waits = []
        if t >= 2:
            waits.append(("sv", sv_nrm[t - 2]))
        sxr_cnt[par] = p.dma("sync", xs_tile(par), x_in[t * 128:(t + 1) * 128, :],
                             f"sxr{par}", waits=waits)
        w = [(f"sxr{par}", sxr_cnt[par])]
        sa_sum = p.op(
            "scalar",
            lambda sc, _p=par, _t=t: sc.activation(
                sqscr, xs_tile(_p), AF.Identity, accum_out=stats[:, _t:_t + 1]),
            waits=w, inc="sa")
        c_ssq = p.op(
            "vector",
            lambda v, _p=par, _t=t: v.scalar_tensor_tensor(
                sqscr, xs_tile(_p), 1.0, xs_tile(_p), ALU.mult, ALU.mult,
                accum_out=stats[:, 8 + _t:9 + _t]),
            waits=w, inc="sv")
        p.op("vector", lambda v, _t=t: v.tensor_scalar(
            stats[:, 16 + _t:17 + _t], stats[:, _t:_t + 1], 1.0 / D, None, ALU.mult),
            waits=[("sa", sa_sum)])
        c_m2 = p.op("vector", lambda v, _t=t: v.scalar_tensor_tensor(
            stats[:, 48:49], stats[:, _t:_t + 1], 1.0 / (D * D), stats[:, _t:_t + 1],
            ALU.mult, ALU.mult), inc="sv")
        wv_ = [("sv", c_m2)]
        if t >= 1:
            wv_.append(("sa", a1_prev))
        c_var = p.op("vector", lambda v, _t=t: v.scalar_tensor_tensor(
            stats[:, 49:50], stats[:, 8 + _t:9 + _t], 1.0 / D, stats[:, 48:49],
            ALU.mult, ALU.subtract), waits=wv_, inc="sv")
        a1 = p.op("scalar", lambda sc, _t=t: sc.activation(
            stats[:, 50:51], stats[:, 49:50], AF.Ln, bias=acst[:, 0:1]),
            waits=[("sv", c_var), ("sg", sg_id)], inc="sa")
        a1_prev = a1
        sa_rstd[t] = p.op("scalar", lambda sc, _t=t: sc.activation(
            stats[:, 24 + _t:25 + _t], stats[:, 50:51], AF.Exp, scale=acst[:, 1:2]),
            waits=[("sa", a1)], inc="sa")
        p.op("vector", lambda v, _t=t: v.scalar_tensor_tensor(
            stats[:, 32 + _t:33 + _t], stats[:, 16 + _t:17 + _t], -1.0,
            stats[:, 24 + _t:25 + _t], ALU.mult, ALU.mult),
            waits=[("sa", sa_rstd[t])])
        c_pre = p.op("vector", lambda v, _t=t: v.tensor_scalar(
            stats[:, 40 + _t:41 + _t], stats[:, 24 + _t:25 + _t], 1.0, None, ALU.add),
            inc="sv")
        w = [("sv", c_pre)]
        if t >= 1:
            w.append(("st", st_tr[t - 1][3]))
        sv_nrm[t] = p.op(
            "vector",
            lambda v, _p=par, _t=t: v.tensor_scalar(
                ln1_tm, xs_tile(_p), stats[:, 24 + _t:25 + _t],
                stats[:, 32 + _t:33 + _t], ALU.mult, ALU.add),
            waits=w, inc="sv")
        c3 = sv_nrm[t]
        for g in range(4):
            gpar = (t * 4 + g) % 2
            psv = PS_B[:, 1024 + gpar * 512:1024 + gpar * 512 + 512]
            w = [("sv", sv_nrm[t]), ("sg", sg_id)]
            prev = (t * 4 + g) - 2
            if prev >= 0:
                w.append(("sv", sv_evac[prev // 4][prev % 4]))
            for i in range(4):
                inc = "st" if i == 3 else None
                r = p.op(
                    "tensor",
                    lambda tn, _i=i, _g=g, _t=t, _ps=psv: tn.transpose(
                        _ps[:, _i * 128:(_i + 1) * 128],
                        ln1_tm[:, (_g * 4 + _i) * 128:(_g * 4 + _i + 1) * 128],
                        ident),
                    waits=w if i == 0 else (), inc=inc)
            st_tr[t][g] = r
            sv_evac[t][g] = p.op(
                "vector",
                lambda v, _g=g, _t=t, _ps=psv: v.tensor_copy(
                    ln1_fm3[:, _g * 4:(_g + 1) * 4, _t * 128:(_t + 1) * 128],
                    _ps.rearrange("q (i f) -> q i f", i=4)),
                waits=[("st", st_tr[t][g])], inc="sv")

    pg[6] = ("sv", sv_evac[TT - 1][2])
    pg[7] = ("sv", sv_evac[TT - 1][3])

    wv3 = wv_sb.rearrange("q (b dc h) -> q b dc h", b=2, dc=16)
    swv = {0: 0, 1: 0}
    for c0 in range(2):
        for half in range(2):
            swv[c0] = p.dma(
                "sync",
                wv3[:, c0, half * 8:(half + 1) * 8, :],
                whid_bf[half * 1024:(half + 1) * 1024,
                        c0 * 1024:(c0 + 1) * 1024].rearrange("(dc q) h -> q dc h", q=128),
                f"swv{c0}",
                waits=[("scwb0", 16), ("scwb1", 16)] if (c0 == 0 and half == 0) else ())

    qk_banks = [0, 1]
    w = [("swqk", sw_wqk), ("sv", sv_evac[TT - 1][3]), ("scst", scst)] + bank_waits(qk_banks)
    st_qk = None
    for dc in range(DC):
        for s in range(2):
            inc = "st" if (dc == DC - 1 and s == 1) else None
            st_qk = p.op(
                "tensor",
                lambda tn, _dc=dc, _s=s: tn.matmul(
                    PS_A[:, _s * 512:(_s + 1) * 512],
                    wqk_b[:, _dc * 128:(_dc + 1) * 128],
                    ln1_fm[:, _dc * 1024 + _s * 512:_dc * 1024 + (_s + 1) * 512],
                    start=(_dc == 0), stop=(_dc == DC - 1)),
                waits=w if (dc == 0 and s == 0) else (), inc=inc) or st_qk
    sa_qk = p.op("scalar", lambda s: s.activation(
        qk_f32, PS_A[:, 0:1024], AF.Silu, bias=nrm_pp[:, 0:1]),
        waits=[("st", st_qk)], inc="sa")
    free_banks(qk_banks, "sa", sa_qk)
    p.op("vector", lambda v: v.tensor_scalar(
        q_bf, qk_f32, nrm_pp[:, 1:2], nrm_pp[:, 2:3], ALU.mult, ALU.add),
        waits=[("sa", sa_qk)])
    sv_qk = p.op("vector", lambda v: v.tensor_scalar(
        k_own, qk_f32, nrm_pp[:, 3:4], nrm_pp[:, 4:5], ALU.mult, ALU.add), inc="sv")
    skw = p.dma("scalar", k_cin, k_own, "skw", waits=[("sv", sv_qk)])
    sck = p.op(
        "gpsimd",
        lambda g: g.collective_compute(
            "AllGather", ALU.bypass, replica_groups=[list(range(NCORES))],
            ins=[k_cin.opt()], outs=[k_call.opt()]),
        waits=[("skw", skw)], inc="sck")
    skf = None
    for r in range(NCORES):
        skf = p.dma("scalar", k_full[:, r * T:(r + 1) * T],
                    k_call[r * 128:(r + 1) * 128, :], "skf",
                    waits=[("sck", sck)] if r == 0 else ())

    st_vmm_c = [None] * 4
    sa_vsl = {}
    svw_cnt = [0] * 4
    scv = [None] * 4
    sv_sb2 = v_sb.rearrange("q (b f) -> q b f", b=2)

    nvs = 0
    sa_sil = {}
    for c in range(4):
        par = c % 2
        if c >= 2:
            w = [("st", st_vmm_c[c - 2])]
            for half in range(2):
                swv[par] = p.dma(
                    "sync",
                    wv3[:, par, half * 8:(half + 1) * 8, :],
                    whid_bf[half * 1024:(half + 1) * 1024,
                            c * 1024:(c + 1) * 1024].rearrange("(dc q) h -> q dc h", q=128),
                    f"swv{par}", waits=w if half == 0 else ())
        for t in range(TT):
            tpar = t % 2
            for sl_ in range(2):
                banks = [tpar * 2 + sl_]
                w = [(f"swv{par}", swv[par])] + bank_waits(banks)
                ps = bankap(banks[0])
                for dc in range(DC):
                    inc = "st" if dc == DC - 1 else None
                    st_v = p.op(
                        "tensor",
                        lambda tn, _dc=dc, _t=t, _s=sl_, _c=c, _par=par, _ps=ps: tn.matmul(
                            _ps,
                            ln1_fm[:, _dc * 1024 + _t * 128:_dc * 1024 + (_t + 1) * 128],
                            wv_sb[:, _par * 16384 + (_dc // 8) * 8192 + (_dc % 8) * 1024 + _s * 512:
                                  _par * 16384 + (_dc // 8) * 8192 + (_dc % 8) * 1024 + (_s + 1) * 512],
                            start=(_dc == 0), stop=(_dc == DC - 1)),
                        waits=w if dc == 0 else (), inc=inc)
                vpar = nvs % 2
                w2 = [("st", st_v), ("sv", sv_setup)]
                if nvs >= 2:
                    w2.append(("sa", sa_sil[nvs - 2]))
                sv_vb = p.op(
                    "vector",
                    lambda v, _vp=vpar, _c=c, _s=sl_, _ps=ps: v.tensor_tensor(
                        simscr[:, _vp * 512:(_vp + 1) * 512], _ps,
                        bhv_bc[:, _c * 1024 + _s * 512:_c * 1024 + (_s + 1) * 512],
                        ALU.add),
                    waits=w2, inc="sv")
                free_banks(banks, "sv", sv_vb)
                w2 = [("sv", sv_vb)]
                if t >= 2:
                    w2.append((f"svw{c}p{tpar}", ((t - 2) // 2 + 1) * 16))
                sa_v = p.op(
                    "scalar",
                    lambda sc, _vp=vpar, _tp=tpar, _s=sl_: sc.activation(
                        sv_sb2[:, _tp, _s * 512:(_s + 1) * 512],
                        simscr[:, _vp * 512:(_vp + 1) * 512], AF.Silu),
                    waits=w2, inc="sa")
                sa_sil[nvs] = sa_v
                sa_vsl[(c, t, sl_)] = sa_v
                nvs += 1
            svw_cnt[c] = p.dma(
                "sync", v_cin[c][t * 128:(t + 1) * 128, :], sv_sb2[:, tpar, :],
                f"svw{c}p{tpar}", waits=[("sa", sa_vsl[(c, t, 1)])])
        st_vmm_c[c] = st_v
        scv[c] = p.op(
            "gpsimd",
            lambda g, _c=c: g.collective_compute(
                "AllGather", ALU.bypass, replica_groups=[list(range(NCORES))],
                ins=[v_cin[_c].opt()], outs=[v_call[_c].opt()]),
            waits=[(f"svw{c}p0", 64), (f"svw{c}p1", 64)], inc=f"scv{c}")

    swg_cnt = p.dma(
        "sync", wg_sb.rearrange("q (dc h) -> q dc h", dc=DC),
        whid_bf[:, HV:HV + 512].rearrange("(dc q) h -> q dc h", q=128),
        "swg", waits=[("scwb2", 16), ("scwb3", 16), ("st", st_setup)])

    if stage < 1:
        raise ValueError
    if stage == 1:
        _emit_debug_stage1(p, nc, dbg, locals())
        return _finalize(nc, p, stack)


    vst3 = vst.rearrange("q (b kt h) -> q b kt h", b=2, kt=8)
    svst = {0: 0, 1: 0}
    st_avm = {}
    sv_om_end = {}
    sa_gsl = {}
    st_gmm = {}
    swo = {0: 0, 1: 0}
    st_womm = {}
    sv_x2a = {}
    sv_l2n = {}
    sx2w_cnt = 0
    sl2w_cnt = 0
    sa_t1 = {}
    vst_free = {}
    st_simm_last = None

    for qb in range(QBN):
        sv_relu = [None] * KT
        sa_sqr = [None] * KT
        for kt in range(KT):
            b = kt % 4
            w = [("skf", skf)] + bank_waits([b])
            st_s = p.op(
                "tensor",
                lambda tn, _kt=kt, _qb=qb, _b=b: tn.matmul(
                    bankap(_b), k_full[:, _kt * 128:(_kt + 1) * 128],
                    q_bf[:, _qb * QW:(_qb + 1) * QW],
                    start=True, stop=True),
                waits=w, inc="st")
            st_simm_last = st_s
            w = [("st", st_s)]
            if kt >= 2:
                w.append(("sa", sa_sqr[kt - 2]))
            sv_relu[kt] = p.op(
                "vector",
                lambda v, _kt=kt, _b=b: v.tensor_scalar(
                    simscr[:, (_kt % 2) * 512:(_kt % 2) * 512 + 512],
                    bankap(_b), 0.0, None, ALU.max),
                waits=w, inc="sv")
            pg[b] = ("sv", sv_relu[kt])
            w = [("sv", sv_relu[kt])]
            if kt == 0:
                if qb == 0:
                    w += [(f"svw{c}p{pp}", 64) for c in range(4) for pp in range(2)]
                else:
                    w += [("sx2w0", 32), ("sx2w1", 32), ("sl2w0", 32), ("sl2w1", 32)]
            sa_sqr[kt] = p.op(
                "scalar",
                lambda sc, _kt=kt: sc.activation(
                    attnT[:, _kt * 512:(_kt + 1) * 512],
                    simscr[:, (_kt % 2) * 512:(_kt % 2) * 512 + 512],
                    AF.Square, scale=acst[:, 2:3]),
                waits=w, inc="sa")

        for hg in range(8):
            c, half = hg // 2, hg % 2
            if (qb, hg) != (0, 0):
                prev = (qb, hg - 1) if hg > 0 else (qb - 1, 7)
                w = [("scwb2", 16), ("scwb3", 16), ("st", st_gmm[prev + (3,)])]
                swg_cnt = p.dma(
                    "sync",
                    wg_sb.rearrange("q (dc h) -> q dc h", dc=DC),
                    whid_bf[:, HV + hg * 512:HV + (hg + 1) * 512].rearrange(
                        "(dc q) h -> q dc h", q=128),
                    "swg", waits=w)
            for hc in range(4):
                bnk = 4 + hc
                w = [("swg", swg_cnt)] + bank_waits([bnk])
                for dc in range(DC):
                    inc = "st" if dc == DC - 1 else None
                    r = p.op(
                        "tensor",
                        lambda tn, _dc=dc, _hc=hc, _qb=qb: tn.matmul(
                            PS_B[:, _hc * 512:(_hc + 1) * 512],
                            wg_sb[:, _dc * 512 + _hc * 128:_dc * 512 + (_hc + 1) * 128],
                            ln1_fm[:, _dc * 1024 + _qb * QW:_dc * 1024 + _qb * QW + QW],
                            start=(_dc == 0), stop=(_dc == DC - 1)),
                        waits=w if dc == 0 else (), inc=inc)
                st_gmm[(qb, hg, hc)] = r
                w = [("st", r)]
                if (qb, hg) != (0, 0):
                    prevhg = (qb, hg - 1) if hg > 0 else (qb - 1, 7)
                    w.append(("sv", sv_om_end[prevhg]))
                if qb == 1 and hg == 0:
                    w.append(("sv", sv_l2n[(qb - 1) * 4 + 3]))
                sa_gsl[(qb, hg, hc)] = p.op(
                    "scalar",
                    lambda sc, _hc=hc, _hg=hg: sc.activation(
                        gate_f[:, _hc * 512:(_hc + 1) * 512],
                        PS_B[:, _hc * 512:(_hc + 1) * 512],
                        AF.Silu, bias=bhid_g[:, _hg * 4 + _hc:_hg * 4 + _hc + 1]),
                    waits=w, inc="sa")
                pg[4 + hc] = ("sa", sa_gsl[(qb, hg, hc)])
            for ks in range(8):
                par = ks % 2
                w = [(f"scv{c}", scv[c])]
                if par in vst_free:
                    w.append(("st", vst_free[par]))
                if qb == 0 and hg == 0 and ks < 2:
                    w.append(("sv", sv_nrm[TT - 1]))
                if qb == 1 and hg == 0 and ks < 2:
                    w.append(("sv", sa_t1[(qb - 1) * 4 + 3]))
                svst[par] = p.dma(
                    "sync",
                    vst3[:, par, :, :],
                    v_call[c][ks * 1024:(ks + 1) * 1024,
                              half * 512:(half + 1) * 512].rearrange(
                                  "(kt q) h -> q kt h", q=128),
                    f"svst{par}", waits=w)
                w = [(f"svst{par}", svst[par]), ("sa", sa_sqr[min(ks * 8 + 7, KT - 1)])]
                if ks == 0:
                    w += bank_waits([0, 1, 2, 3])
                first = True
                for kt in range(8):
                    for hc in range(4):
                        inc = "st" if (kt == 7 and hc == 3) else None
                        r = p.op(
                            "tensor",
                            lambda tn, _ks=ks, _kt=kt, _hc=hc, _par=par: tn.matmul(
                                bankap(_hc),
                                vst[:, _par * 4096 + _kt * 512 + _hc * 128:
                                    _par * 4096 + _kt * 512 + (_hc + 1) * 128],
                                attnT[:, (_ks * 8 + _kt) * 512:(_ks * 8 + _kt + 1) * 512],
                                start=(_ks == 0 and _kt == 0), stop=(_ks == 7 and _kt == 7)),
                            waits=w if first else (), inc=inc)
                        first = False
                st_avm[(qb, hg, ks)] = r
                vst_free[par] = r
            last = None
            for hc in range(4):
                w = [("st", st_avm[(qb, hg, 7)]), ("sa", sa_gsl[(qb, hg, hc)])]
                last = p.op(
                    "vector",
                    lambda v, _hc=hc, _hg=hg: v.tensor_tensor(
                        om[:, (_hg * 4 + _hc) * 512:(_hg * 4 + _hc + 1) * 512],
                        bankap(_hc), gate_f[:, _hc * 512:(_hc + 1) * 512], ALU.mult),
                    waits=w, inc="sv")
                pg[hc] = ("sv", last)
            sv_om_end[(qb, hg)] = last

        sbo = p.dma("sync", bout_row, bout_in, f"sbo{qb}",
                    waits=[("st", st_avm[(qb, 7, 7)])])
        st_bo = None
        w = [(f"sbo{qb}", sbo)] + bank_waits([0, 1, 2, 3])
        for sl_ in range(4):
            st_bo = p.op(
                "tensor",
                lambda tn, _sl=sl_: tn.matmul(
                    PS_A[:, _sl * 512:(_sl + 1) * 512], ones_s,
                    bout_row[:, _sl * 512:(_sl + 1) * 512], start=True, stop=True),
                waits=w if sl_ == 0 else (), inc="st" if sl_ == 3 else None)
        sv_bo = p.op("vector", lambda v: v.tensor_copy(bout_bc, PS_A),
                     waits=[("st", st_bo)], inc="sv")
        free_banks([0, 1, 2, 3], "sv", sv_bo)
        for qtp in range(2):
            qts = [qb * 4 + qtp * 2, qb * 4 + qtp * 2 + 1]
            first_w = bank_waits(list(range(8)))
            for hc in range(32):
                wpar = hc % 2
                w = [("scwo", 64)]
                key = (qb, qtp, hc - 2)
                if (qb, qtp) != (0, 0) or hc >= 2:
                    if hc >= 2:
                        w.append(("st", st_womm.get(("mm", qb, qtp, hc - 2), 0)))
                    else:
                        prev = (qb, qtp - 1) if qtp > 0 else (qb - 1, 1)
                        w.append(("st", st_womm.get(("mm", prev[0], prev[1], 30 + hc), 0)))
                if qtp == 0 and hc < 2:
                    w.append(("st", st_avm[(qb, 7, 7)]))
                swo[wpar] = p.dma(
                    "sync",
                    wo_sb[:, wpar * D:(wpar + 1) * D],
                    wout_bf[hc * 128:(hc + 1) * 128, :],
                    f"swo{wpar}", waits=w)
                w2 = [(f"swo{wpar}", swo[wpar])]
                if hc == 0:
                    w2 += first_w
                firstmm = True
                for j in range(2):
                    pst = PS_A if j == 0 else PS_B
                    for sl in range(4):
                        r = p.op(
                            "tensor",
                            lambda tn, _hc=hc, _j=j, _sl=sl, _qtp=qtp, _qb=qb, _wp=wpar, _ps=pst: tn.matmul(
                                _ps[:, _sl * 512:(_sl + 1) * 512],
                                om[:, _hc * 512 + (_qtp * 2 + _j) * 128:
                                   _hc * 512 + (_qtp * 2 + _j) * 128 + 128],
                                wo_sb[:, _wp * D + _sl * 512:_wp * D + (_sl + 1) * 512],
                                start=(_hc == 0), stop=(_hc == 31)),
                            waits=w2 if firstmm else (), inc="st" if (j == 1 and sl == 3) else None)
                        firstmm = False
                st_womm[("mm", qb, qtp, hc)] = r
            for j in range(2):
                st_womm[qts[j]] = st_womm[("mm", qb, qtp, 31)]
            for j in range(2):
                qt = qts[j]
                par = qt % 2
                pst = PS_A if j == 0 else PS_B
                w = [("st", st_avm[(qb, 7, 7)])]
                if qt >= 2:
                    w.append(("sv", sa_t1[qt - 2]))
                sxr_cnt[par] = p.dma("sync", xs_tile(par),
                                     x_in[(qt % 4) * 128 + qb * QW:(qt % 4) * 128 + qb * QW + 128, :],
                                     f"sxr{par}", waits=w)
                w = [(f"sxr{par}", sxr_cnt[par]), ("sv", c3)]
                if qb == 0 and qtp == 0 and j == 0:
                    w.append(("sv", sv_om_end[(qb, 7)]))
                sa_t1[qt] = p.op(
                    "vector",
                    lambda v, _p=par, _qt=qt: v.tensor_scalar(
                        t1_s, xs_tile(_p), stats[:, 40 + _qt:41 + _qt],
                        stats[:, 32 + _qt:33 + _qt], ALU.mult, ALU.add),
                    waits=w, inc="sv")
                w = [("st", st_womm[qt]), ("sv", sa_t1[qt])]
                if qt >= 2:
                    w.append((f"sx2w{qt % 2}", 16 * ((qt - 2) // 2 + 1)))
                p.op(
                    "vector",
                    lambda v, _p=par, _ps=pst: v.tensor_tensor(
                        x2_sb[:, _p * D:(_p + 1) * D], _ps, bout_bc, ALU.add),
                    waits=w + [("sv", sv_bo)])
                sv_x2a[qt] = p.op(
                    "vector",
                    lambda v, _p=par: v.tensor_tensor(
                        x2_sb[:, _p * D:(_p + 1) * D], x2_sb[:, _p * D:(_p + 1) * D],
                        t1_s, ALU.add),
                    inc="sv")
                free_banks(list(range(4)) if j == 0 else [4, 5, 6, 7], "sv", sv_x2a[qt])
                p.op("vector", lambda v, _p=par, _qt=qt: v.reduce_sum(
                    stats[:, 64 + _qt:65 + _qt], x2_sb[:, _p * D:(_p + 1) * D], AX.X))
                s1 = p.op("vector", lambda v, _p=par, _qt=qt: v.scalar_tensor_tensor(
                    sqscr, x2_sb[:, _p * D:(_p + 1) * D], 1.0,
                    x2_sb[:, _p * D:(_p + 1) * D], ALU.mult, ALU.mult,
                    accum_out=stats[:, 72 + _qt:73 + _qt]), inc="sv")
                p.op("vector", lambda v, _qt=qt: v.tensor_scalar(
                    stats[:, 88:89], stats[:, 64 + _qt:65 + _qt], 1.0 / D, None, ALU.mult),
                    waits=[("sv", s1)])
                s2 = p.op("vector", lambda v, _qt=qt: v.scalar_tensor_tensor(
                    stats[:, 89:90], stats[:, 64 + _qt:65 + _qt], 1.0 / (D * D),
                    stats[:, 64 + _qt:65 + _qt], ALU.mult, ALU.mult), inc="sv")
                s3 = p.op("vector", lambda v, _qt=qt: v.scalar_tensor_tensor(
                    stats[:, 90:91], stats[:, 72 + _qt:73 + _qt], 1.0 / D,
                    stats[:, 89:90], ALU.mult, ALU.subtract),
                    waits=[("sv", s2)], inc="sv")
                a3 = p.op("scalar", lambda s: s.activation(
                    stats[:, 91:92], stats[:, 90:91], AF.Ln, bias=acst[:, 0:1]),
                    waits=[("sv", s3)], inc="sa")
                a4 = p.op("scalar", lambda s, _qt=qt: s.activation(
                    stats[:, 80 + _qt:81 + _qt], stats[:, 91:92], AF.Exp,
                    scale=acst[:, 1:2]), waits=[("sa", a3)], inc="sa")
                s4 = p.op("vector", lambda v, _qt=qt: v.scalar_tensor_tensor(
                    stats[:, 92:93], stats[:, 88:89], -1.0,
                    stats[:, 80 + _qt:81 + _qt], ALU.mult, ALU.mult),
                    waits=[("sa", a4)], inc="sv")
                w = [("sv", s4)]
                if qt >= 2:
                    w.append((f"sl2w{qt % 2}", 16 * ((qt - 2) // 2 + 1)))
                sv_l2n[qt] = p.op(
                    "vector",
                    lambda v, _p=par, _qt=qt: v.tensor_scalar(
                        ln2_tm[:, _p * D:(_p + 1) * D], x2_sb[:, _p * D:(_p + 1) * D],
                        stats[:, 80 + _qt:81 + _qt], stats[:, 92:93],
                        ALU.mult, ALU.add),
                    waits=w, inc="sv")
                sx2w_cnt = p.dma("scalar", x2_dram[qt * 128:(qt + 1) * 128, :],
                                 x2_sb[:, par * D:(par + 1) * D], f"sx2w{qt % 2}",
                                 waits=[("sv", sv_x2a[qt])])
                sl2w_cnt = p.dma("scalar", ln2_dram[qt * 128:(qt + 1) * 128, :],
                                 ln2_tm[:, par * D:(par + 1) * D], f"sl2w{qt % 2}",
                                 waits=[("sv", sv_l2n[qt])])

    if stage == 2:
        _emit_debug_stage2(p, nc, dbg, locals())
        return _finalize(nc, p, stack)

    w = [("sl2w0", 64), ("sl2w1", 64), ("st", st_gmm[(1, 7, 3)])]
    for dc in range(DC):
        p.dma("sync", ln2_fm[:, dc * 1024:(dc + 1) * 1024],
              ln2_dram[:, dc * 128:(dc + 1) * 128], f"sl2t{dc}",
              waits=w if dc == 0 else (), transpose=True)

    gw3 = gw_sb.rearrange("q (b dc i) -> q b dc i", b=2, dc=DC)
    uw3 = uw_sb.rearrange("q (b dc i) -> q b dc i", b=2, dc=DC)
    sfg = {0: 0, 1: 0}
    sfu = {0: 0, 1: 0}
    st_gumm = [None] * ICN
    sa_gs2 = [None] * ICN
    sv_hdn = [None] * ICN
    npairs = ICN // 2
    for pr in range(npairs):
        par = pr % 2
        w = []
        if pr >= 2:
            w.append(("st", st_gumm[(pr - 2) * 2 + 1]))
        if pr < 2:
            w.append(("st", st_simm_last))
            w.append(("st", st_womm[7]))
        sfg[par] = p.dma(
            "gpsimd", gw3[:, par, :, :],
            gwt_in[:, pr * 256:(pr + 1) * 256].rearrange("(dc q) i -> q dc i", q=128),
            f"sfg{par}", waits=w)
        sfu[par] = p.dma(
            "gpsimd", uw3[:, par, :, :],
            uwt_in[:, pr * 256:(pr + 1) * 256].rearrange("(dc q) i -> q dc i", q=128),
            f"sfu{par}")
        for ic in (2 * pr, 2 * pr + 1):
            icp = ic % 2
            psg = (PS_A if icp == 0 else PS_B)[:, 0:1024]
            psu = (PS_A if icp == 0 else PS_B)[:, 1024:2048]
            banks = [0, 1, 2, 3] if icp == 0 else [4, 5, 6, 7]
            w = [(f"sfg{par}", sfg[par]), (f"sfu{par}", sfu[par])] + bank_waits(banks)
            first = True
            for dc in range(DC):
                for sl_ in range(2):
                    wdc = []
                    if ic == 0 and sl_ == 0:
                        wdc = [(f"sl2t{dc}", 16)]
                    p.op(
                        "tensor",
                        lambda tn, _dc=dc, _s=sl_, _ic=ic, _par=par, _ps=psg: tn.matmul(
                            _ps[:, _s * 512:(_s + 1) * 512],
                            gw_sb[:, _par * 4096 + _dc * 256 + (_ic % 2) * 128:
                                  _par * 4096 + _dc * 256 + (_ic % 2) * 128 + 128],
                            ln2_fm[:, _dc * 1024 + _s * 512:_dc * 1024 + (_s + 1) * 512],
                            start=(_dc == 0), stop=(_dc == DC - 1)),
                        waits=(w + wdc) if first else wdc)
                    first = False
                    inc = "st" if (dc == DC - 1 and sl_ == 1) else None
                    r = p.op(
                        "tensor",
                        lambda tn, _dc=dc, _s=sl_, _ic=ic, _par=par, _ps=psu: tn.matmul(
                            _ps[:, _s * 512:(_s + 1) * 512],
                            uw_sb[:, _par * 4096 + _dc * 256 + (_ic % 2) * 128:
                                  _par * 4096 + _dc * 256 + (_ic % 2) * 128 + 128],
                            ln2_fm[:, _dc * 1024 + _s * 512:_dc * 1024 + (_s + 1) * 512],
                            start=(_dc == 0), stop=(_dc == DC - 1)),
                        inc=inc)
            st_gumm[ic] = r
            w = [("st", st_gumm[ic])]
            if ic >= 2:
                w.append(("sv", sv_hdn[ic - 2]))
            if ic == 0:
                w.append(("sv", sv_l2n[7]))
            sa_gs2[ic] = p.op(
                "scalar",
                lambda sc, _ic=ic, _ps=psg: sc.activation(
                    gs_scr[:, (_ic % 2) * 1024:(_ic % 2) * 1024 + 1024], _ps,
                    AF.Silu, scale=gs_pp[:, _ic:_ic + 1]),
                waits=w, inc="sa")
            w = [("sa", sa_gs2[ic])]
            if ic == 0:
                w += [("sx2w0", 64), ("sx2w1", 64), ("sl2w0", 64), ("sl2w1", 64),
                      ("st", st_womm[7])]
            sv_hdn[ic] = p.op(
                "vector",
                lambda v, _ic=ic, _ps=psu: v.scalar_tensor_tensor(
                    hdn[:, _ic * 1024:(_ic + 1) * 1024], _ps, us_pp[:, _ic:_ic + 1],
                    gs_scr[:, (_ic % 2) * 1024:(_ic % 2) * 1024 + 1024],
                    ALU.mult, ALU.mult),
                waits=w, inc="sv")
            free_banks(banks, "sv", sv_hdn[ic])

    sdsr = p.dma("sync", ds_row, ds_in, "sdsr", waits=[("st", st_gumm[ICN - 1])])
    w = [("sdsr", sdsr)] + bank_waits([0, 1, 2, 3])
    st_dsb = None
    for sl in range(4):
        st_dsb = p.op(
            "tensor",
            lambda tn, _sl=sl: tn.matmul(
                PS_A[:, _sl * 512:(_sl + 1) * 512], ones_s,
                ds_row[:, _sl * 512:(_sl + 1) * 512], start=True, stop=True),
            waits=w if sl == 0 else (), inc="st" if sl == 3 else None)
    sv_dsb = p.op("vector", lambda v: v.tensor_copy(ds_bc, PS_A),
                  waits=[("st", st_dsb)], inc="sv")
    free_banks([0, 1, 2, 3], "sv", sv_dsb)

    dw3 = dw_sb.rearrange("q (b ic d) -> q b ic d", b=2, ic=4)
    sfd = {0: 0, 1: 0}
    sxf = {0: 0, 1: 0}
    syw_cnt = 0
    st_dn = {}
    sv_out = {}
    outp = out_sb.rearrange("q (b f) -> q b f", b=2)
    nouts = 0
    for quad in range(2):
        for dh in range(2):
            gdh = quad * 2 + dh
            for bt in range(11):
                par = (gdh * 11 + bt) % 2
                w = [("scdw", 64)]
                gi = gdh * 11 + bt
                if bt >= 2:
                    w.append(("st", st_dn[("mm", quad, dh, min(ICN - 1, (bt - 2) * 4 + 3))]))
                elif gdh > 0:
                    pq, pd = divmod(gdh - 1, 2)
                    w.append(("st", st_dn[("mm", pq, pd, min(ICN - 1, (bt + 9) * 4 + 3))]))
                if gi < 2:
                    w.append(("sv", sv_hdn[ICN - 1]))
                sfd[par] = p.dma(
                    "sync", dw3[:, par, :, :],
                    dwt_bf[bt * 512:(bt + 1) * 512,
                           dh * 1024:(dh + 1) * 1024].rearrange("(ic q) d -> q ic d", q=128),
                    f"sfd{par}", waits=w)
                for ic in range(bt * 4, min(ICN, (bt + 1) * 4)):
                    w2 = [(f"sfd{par}", sfd[par])]
                    if ic == 0:
                        w2 += bank_waits(list(range(8)))
                    first = True
                    for j in range(4):
                        qt = quad * 4 + j
                        psq = (PS_A if j < 2 else PS_B)[:, (j % 2) * 1024:(j % 2) * 1024 + 1024]
                        for s_ in range(2):
                            inc = "st" if (j == 3 and s_ == 1) else None
                            r = p.op(
                                "tensor",
                                lambda tn, _ic=ic, _j=j, _s=s_, _qt=qt, _par=par, _ps=psq: tn.matmul(
                                    _ps[:, _s * 512:(_s + 1) * 512],
                                    hdn[:, _ic * 1024 + _qt * 128:_ic * 1024 + (_qt + 1) * 128],
                                    dw_sb[:, _par * 4096 + (_ic % 4) * 1024 + _s * 512:
                                          _par * 4096 + (_ic % 4) * 1024 + (_s + 1) * 512],
                                    start=(_ic == 0), stop=(_ic == ICN - 1)),
                                waits=w2 if first else (), inc=inc)
                            first = False
                    st_dn[("mm", quad, dh, ic)] = r
            for j in range(2):
                qt = quad * 4 + j
                opar = (nouts + j) % 2
                w = []
                if nouts + j >= 2:
                    w.append(("sv", sv_out[nouts + j - 2]))
                else:
                    w.append(("sv", sv_l2n[7]))
                sxf[opar] = p.dma(
                    "sync", x2h[:, opar * 1024:(opar + 1) * 1024],
                    x2_dram[qt * 128:(qt + 1) * 128, dh * 1024:(dh + 1) * 1024],
                    f"sxf{opar}", waits=w)
            for j in range(4):
                qt = quad * 4 + j
                psq = (PS_A if j < 2 else PS_B)[:, (j % 2) * 1024:(j % 2) * 1024 + 1024]
                opar = nouts % 2
                if j >= 2:
                    sxf[opar] = p.dma(
                        "sync", x2h[:, opar * 1024:(opar + 1) * 1024],
                        x2_dram[qt * 128:(qt + 1) * 128, dh * 1024:(dh + 1) * 1024],
                        f"sxf{opar}", waits=[("sv", sv_out[nouts - 2])])
                w = [("st", st_dn[("mm", quad, dh, ICN - 1)]), ("sv", sv_dsb)]
                if nouts >= 2:
                    w.append((f"syw{nouts % 2}", 16 * ((nouts - 2) // 2 + 1)))
                p.op(
                    "vector",
                    lambda v, _o=opar, _dh=dh, _ps=psq: v.tensor_tensor(
                        outp[:, _o, :], _ps, ds_bc[:, _dh * 1024:(_dh + 1) * 1024],
                        ALU.mult),
                    waits=w, inc=None)
                sv_out[nouts] = p.op(
                    "vector",
                    lambda v, _o=opar: v.tensor_tensor(
                        outp[:, _o, :], outp[:, _o, :], x2h[:, _o * 1024:(_o + 1) * 1024],
                        ALU.add),
                    waits=[(f"sxf{opar}", sxf[opar])], inc="sv")
                free_banks([0, 1] if j == 0 else [2, 3] if j == 1 else [4, 5] if j == 2 else [6, 7],
                           "sv", sv_out[nouts])
                syw_cnt = p.dma("sync", y_out[qt * 128:(qt + 1) * 128, dh * 1024:(dh + 1) * 1024],
                                outp[:, opar, :], f"syw{nouts % 2}",
                                waits=[("sv", sv_out[nouts])])
                nouts += 1

    p.op("sync", lambda e: None, waits=[("syw0", 128), ("syw1", 128)])
    if dbg:
        d_ln2fm = ap(nc.dram_tensor("d_ln2fm", [128, DC * 1024], BF16, kind="ExternalOutput"))
        d_hdn = ap(nc.dram_tensor("d_hdn", [128, ICN * 1024], BF16, kind="ExternalOutput"))
        d_dsb = ap(nc.dram_tensor("d_dsb", [128, D], F32, kind="ExternalOutput"))
        so = p.dma("sync", d_ln2fm, ln2_fm, "sout", waits=[(f"sl2t{DC-1}", 16)])
        so = p.dma("sync", d_hdn, hdn, "sout", waits=[("sv", sv_hdn[ICN - 1])])
        so = p.dma("sync", d_dsb, ds_bc, "sout", waits=[("sv", sv_dsb)])
        p.op("sync", lambda e: None, waits=[("sout", so)])
    return _finalize(nc, p, stack)


def _emit_debug_stage1(p, nc, dbg, L):
    outs = {}
    for name, shape, dt in (("d_ln1fm", [128, DC * 1024], BF16),
                            ("d_qk", [128, T], F32),
                            ("d_q", [128, T], BF16),
                            ("d_kfull", [128, S], BF16),
                            ("d_v0", [S, 1024], BF16),
                            ("d_v1", [S, 1024], BF16),
                            ("d_v2", [S, 1024], BF16),
                            ("d_v3", [S, 1024], BF16),
                            ("d_vc3", [T, 1024], BF16),
                            ("d_stats", [128, 96], F32)):
        outs[name] = nc.dram_tensor(name, shape, dt, kind="ExternalOutput").ap()
    w = [("sv", L["sv_qk"]), ("sv", L["sv_evac"][TT - 1][3]), ("skf", L["skf"]),
         ("scv0", L["scv"][0]), ("scv1", L["scv"][1]), ("scv2", L["scv"][2]),
         ("scv3", L["scv"][3])]
    so = p.dma("sync", outs["d_ln1fm"], L["ln1_fm"], "sout", waits=w)
    so = p.dma("sync", outs["d_qk"], L["qk_f32"], "sout")
    so = p.dma("sync", outs["d_q"], L["q_bf"], "sout")
    so = p.dma("sync", outs["d_kfull"], L["k_full"], "sout")
    so = p.dma("sync", outs["d_v0"], L["v_call"][0], "sout")
    so = p.dma("sync", outs["d_v1"], L["v_call"][1], "sout")
    so = p.dma("sync", outs["d_v2"], L["v_call"][2], "sout")
    so = p.dma("sync", outs["d_v3"], L["v_call"][3], "sout")
    so = p.dma("sync", outs["d_vc3"], L["v_cin"][3], "sout")
    so = p.dma("sync", outs["d_stats"], L["stats"], "sout")
    p.op("sync", lambda e: None, waits=[("sout", so)])


def _emit_debug_stage2(p, nc, dbg, L):
    outs = {}
    for name, shape, dt in (("d_x2", [T, D], F32),
                            ("d_ln2", [T, D], BF16),
                            ("d_om", [128, 32 * 512], BF16),
                            ("d_at", [128, KT * 512], BF16)):
        outs[name] = nc.dram_tensor(name, shape, dt, kind="ExternalOutput").ap()
    w = [("sx2w0", 64), ("sx2w1", 64), ("sl2w0", 64), ("sl2w1", 64),
         ("sv", L["sv_om_end"][(1, 7)]), ("sa", L["sa_sqr"][KT - 1])]
    so = p.dma("sync", outs["d_x2"], L["x2_dram"], "sout", waits=w)
    so = p.dma("sync", outs["d_ln2"], L["ln2_dram"], "sout")
    so = p.dma("sync", outs["d_om"], L["om"], "sout")
    so = p.dma("sync", outs["d_at"], L["attnT"], "sout")
    p.op("sync", lambda e: None, waits=[("sout", so)])


def _finalize(nc, p, stack):
    with nc.Block() as block:
        block.tensor(lambda e: [f(e) for f in p.q["tensor"]])
        block.vector(lambda e: [f(e) for f in p.q["vector"]])
        block.scalar(lambda e: [f(e) for f in p.q["scalar"]])
        block.sync(lambda e: [f(e) for f in p.q["sync"]])
        block.gpsimd(lambda e: [f(e) for f in p.q["gpsimd"]])
    stack.close()
    return nc


def make_inputs(inputs):
    f32 = lambda a: np.ascontiguousarray(np.asarray(a, dtype=np.float32))
    x = f32(inputs["L0_attn_x"]).reshape(S, D)
    b_hidden = f32(inputs["L0_attn_b_hidden"]).reshape(-1)
    nrm = np.zeros((128, 8), dtype=np.float32)
    nrm[:, 0] = f32(inputs["L0_attn_b_qk"]).reshape(-1)
    nrm[:, 1] = f32(inputs["L0_attn_gamma_q"]).reshape(-1)
    nrm[:, 2] = f32(inputs["L0_attn_beta_q"]).reshape(-1)
    nrm[:, 3] = f32(inputs["L0_attn_gamma_k"]).reshape(-1)
    nrm[:, 4] = f32(inputs["L0_attn_beta_k"]).reshape(-1)
    shared = {
        "wqk": f32(inputs["L0_attn_W_qk"]),
        "nrm": np.ascontiguousarray(nrm),
        "bhg": np.ascontiguousarray(b_hidden[HV:].reshape(32, 128).T),
        "bhv": np.ascontiguousarray(b_hidden[:HV].reshape(1, HV)),
        "whid": f32(inputs["L0_attn_W_hidden"]),
        "wout": f32(inputs["L0_attn_W_out"]),
        "bout": f32(inputs["L0_attn_b_out"]).reshape(1, D),
        "gwt": np.ascontiguousarray(f32(inputs["L0_ffn_gate_w"]).T),
        "uwt": np.ascontiguousarray(f32(inputs["L0_ffn_up_w"]).T),
        "dwt": np.ascontiguousarray(f32(inputs["L0_ffn_down_w"]).T),
        "gs": np.ascontiguousarray(f32(inputs["L0_ffn_gate_s"]).reshape(ICN, 128).T),
        "us": np.ascontiguousarray(f32(inputs["L0_ffn_up_s"]).reshape(ICN, 128).T),
        "ds": f32(inputs["L0_ffn_down_s"]).reshape(1, D),
    }
    in_maps = []
    for cid in range(NCORES):
        m = dict(shared)
        m["x"] = np.ascontiguousarray(x[cid * T:(cid + 1) * T, :])
        in_maps.append(m)
    return in_maps


def kernel(**inputs):
    in_maps = make_inputs(inputs)
    nc = build_program(stage=3)
    res = run_bass_kernel_spmd(nc, in_maps, core_ids=list(range(NCORES)))
    y = np.concatenate([np.asarray(r["y"], dtype=np.float32) for r in res.results], axis=0)
    return y.reshape(1, S, D)


# version 05
# speedup vs baseline: 1.0439x; 1.0160x over previous
import sys

for _p in ("/opt/trn_rl_repo",):
    if _p not in sys.path:
        sys.path.insert(0, _p)

import numpy as np
import ml_dtypes
import concourse.bass as bass
import concourse.mybir as mybir
from concourse.bass_utils import run_bass_kernel_spmd

F32 = mybir.dt.float32
BF16 = mybir.dt.bfloat16
AF = mybir.ActivationFunctionType
ALU = mybir.AluOpType
AX = mybir.AxisListType

NCORES = 8
T = 1024
TT = 8
D = 2048
DC = 16
E = 128
HV = 4096
S = 8192
KT = 64
QBN = 2
QW = 512
II = 5632
ICN = 44
EPS = 1e-5

ENGINES = ("tensor", "vector", "scalar", "sync", "gpsimd")


class Plan:

    def __init__(self, nc, stack):
        self.nc = nc
        self.stack = stack
        self.q = {e: [] for e in ENGINES}
        self.sems = {}
        self.cnt = {}
        self.last_wait = {e: {} for e in ENGINES}

    def sem(self, name):
        if name not in self.sems:
            self.sems[name] = self.stack.enter_context(self.nc.semaphore(name))
            self.cnt[name] = 0
        return self.sems[name]

    def bump(self, name, k=1):
        self.sem(name)
        self.cnt[name] += k
        return self.cnt[name]

    def op(self, eng, fn, waits=(), inc=None, inck=1):
        real_waits = []
        lw = self.last_wait[eng]
        for (s, v) in waits:
            assert v is not None, f"None wait on {s}"
            if v <= 0:
                continue
            assert v <= self.cnt.get(s, 0), f"wait on future {s}={v} (have {self.cnt.get(s, 0)})"
            if lw.get(s, 0) >= v:
                continue
            lw[s] = v
            real_waits.append((self.sem(s), v))
        incsem = self.sem(inc) if inc else None
        target = self.bump(inc, inck) if inc else None

        def run(e, _w=tuple(real_waits), _fn=fn, _is=incsem, _k=inck):
            for (sh, v) in _w:
                e.wait_ge(sh, v)
            inst = _fn(e)
            if _is is not None:
                inst.then_inc(_is, _k)

        self.q[eng].append(run)
        return target

    def dma(self, eng, out, in_, semname, waits=(), transpose=False):
        def fn(e, _o=out, _i=in_, _t=transpose):
            if _t:
                return e.dma_start(out=_o, in_=_i, transpose=True)
            return e.dma_start(out=_o, in_=_i)

        return self.op(eng, fn, waits=waits, inc=semname, inck=16)


def build_program(stage=3, dbg=False):
    nc = bass.Bass("TRN2", target_bir_lowering=False, debug=False)
    import contextlib

    stack = contextlib.ExitStack()

    ap = lambda h: h.ap()
    x_in = ap(nc.dram_tensor("x", [T, D], F32, kind="ExternalInput"))
    wqk_in = ap(nc.dram_tensor("wqk", [D, E], F32, kind="ExternalInput"))
    nrm_in = ap(nc.dram_tensor("nrm", [128, 8], F32, kind="ExternalInput"))
    bhg_in = ap(nc.dram_tensor("bhg", [128, 32], F32, kind="ExternalInput"))
    bhv_in = ap(nc.dram_tensor("bhv", [1, HV], F32, kind="ExternalInput"))
    whid_in = ap(nc.dram_tensor("whid", [D, 2 * HV], F32, kind="ExternalInput"))
    wout_in = ap(nc.dram_tensor("wout", [HV, D], F32, kind="ExternalInput"))
    bout_in = ap(nc.dram_tensor("bout", [1, D], F32, kind="ExternalInput"))
    gwt_in = ap(nc.dram_tensor("gwt", [D, II], F32, kind="ExternalInput"))
    uwt_in = ap(nc.dram_tensor("uwt", [D, II], F32, kind="ExternalInput"))
    dwt_in = ap(nc.dram_tensor("dwt", [II, D], F32, kind="ExternalInput"))
    gs_in = ap(nc.dram_tensor("gs", [128, ICN], F32, kind="ExternalInput"))
    us_in = ap(nc.dram_tensor("us", [128, ICN], F32, kind="ExternalInput"))
    ds_in = ap(nc.dram_tensor("ds", [1, D], F32, kind="ExternalInput"))

    y_out = ap(nc.dram_tensor("y", [T, D], F32, kind="ExternalOutput")) if stage >= 3 else None

    whid_bf = ap(nc.dram_tensor("whid_bf", [D, HV], BF16, kind="Internal"))
    wout_bf = ap(nc.dram_tensor("wout_bf", [HV, D], BF16, kind="Internal"))
    dwt_bf = ap(nc.dram_tensor("dwt_bf", [II, D], BF16, kind="Internal"))
    k_cin = ap(nc.dram_tensor("k_cin", [128, T], BF16, kind="Internal"))
    k_call = ap(nc.dram_tensor("k_call", [NCORES * 128, T], BF16, kind="Internal"))
    v_cin = [ap(nc.dram_tensor(f"v_cin{c}", [T, 1024], BF16, kind="Internal")) for c in range(4)]
    v_call = [ap(nc.dram_tensor(f"v_call{c}", [S, 1024], BF16, kind="Internal")) for c in range(4)]
    ln2_dram = ap(nc.dram_tensor("ln2_dram", [T, D], BF16, kind="Internal"))
    x2_dram = ap(nc.dram_tensor("x2_dram", [T, D], F32, kind="Internal"))

    A = lambda name, shape, dt: nc.alloc_sbuf_tensor(name, shape, dt).ap()
    ident = A("ident", [128, 128], F32)
    stats = A("stats", [128, 96], F32)
    acst = A("acst", [128, 4], F32)
    nrm_pp = A("nrm_pp", [128, 8], F32)
    bhid_g = A("bhid_g", [128, 32], F32)
    gs_pp = A("gs_pp", [128, ICN], F32)
    us_pp = A("us_pp", [128, ICN], F32)
    ones_s = A("ones_s", [1, 128], F32)
    wqk_b = A("wqk_b", [128, DC * 128], BF16)
    q_bf = A("q_bf", [128, T], BF16)
    k_own = A("k_own", [128, T], BF16)

    def region(name, nbytes):
        h = nc.alloc_sbuf_tensor(name, [128, nbytes], mybir.dt.uint8)
        return nc.lookup_mloc(h).addr

    def at(name, base, off, shape, dt, transpose_dest=False):
        return nc.alloc_sbuf_tensor_at(name, shape, dt, offset=base + off).ap()

    m8 = region("M8", 8192)
    ln1_tm = at("ln1_tm", m8, 0, [128, D], F32)
    qk_f32 = at("qk_f32", m8, 0, [128, T], F32)
    simscr = at("simscr", m8, 4096, [128, 1024], F32)

    x16 = region("X16", 16384)
    x_s = at("x_s", x16, 0, [128, 2 * D], F32)
    vst = at("vst", x16, 0, [128, 2 * 4096], BF16)
    x2h = at("x2h", x16, 0, [128, 2 * 1024], F32)

    s8 = region("S8", 8192)
    sqscr = at("sqscr", s8, 0, [128, D], F32)
    gate_f = at("gate_f", s8, 0, [128, 4 * 512], F32)
    t1_s = at("t1_s", s8, 0, [128, D], F32)
    gs_scr = at("gs_scr", s8, 0, [128, 2 * 1024], F32)

    r1 = region("R1", 32768)
    ln1_fm = at("ln1_fm", r1, 0, [128, DC * 1024], BF16)
    ln2_fm = at("ln2_fm", r1, 0, [128, DC * 1024], BF16)
    ds_row = at("ds_row", r1, 0, [1, D], F32)
    ds_bc = at("ds_bc", r1, 8192, [128, D], F32)

    r2 = region("R2", 16384)
    k_full = at("k_full", r2, 0, [128, S], BF16)
    gw_sb = at("gw_sb", r2, 0, [128, 2 * 4096], BF16)
    dw_sb = at("dw_sb", r2, 0, [128, 2 * 4096], BF16)

    r3 = region("R3", 16384)
    wg_sb = at("wg_sb", r3, 0, [128, DC * 512], BF16)
    bhv_row = at("bhv_row", r3, 0, [1, HV], F32)
    uw_sb = at("uw_sb", r3, 0, [128, 2 * 4096], BF16)
    out_sb = at("out_sb", r3, 0, [128, 2 * 1024], F32)

    ra = region("RA", 98304)
    v_sb = at("v_sb", ra, 0, [128, 2 * 1024], BF16)
    bhv_bc = at("bhv_bc", ra, 4096, [128, HV], F32)
    wv_sb = at("wv_sb", ra, 20480, [128, 2 * 16384], BF16)
    attnT = at("attnT", ra, 0, [128, KT * 512], BF16)
    om = at("om", ra, 65536, [128, 32 * 512], BF16)
    x2_sb = at("x2_sb", ra, 0, [128, 2 * D], F32)
    ln2_tm = at("ln2_tm", ra, 16384, [128, 2 * D], BF16)
    wo_sb = at("wo_sb", ra, 24576, [128, 2 * D], BF16)
    bout_bc = at("bout_bc", ra, 32768, [128, D], F32)
    bout_row = at("bout_row", ra, 40960, [1, D], F32)
    hdn = at("hdn", ra, 0, [128, ICN * 1024], BF16)

    PS_A = nc.alloc_psum_tensor("PS_A", [128, 2048], F32).ap()
    PS_B = nc.alloc_psum_tensor("PS_B", [128, 2048], F32).ap()

    def bankap(i, width=512):
        t = PS_A if i < 4 else PS_B
        return t[:, (i % 4) * 512:(i % 4) * 512 + width]

    p = Plan(nc, stack)
    pg = {}

    def bank_waits(banks):
        w = {}
        for b in banks:
            if b in pg:
                s, v = pg[b]
                w[s] = max(w.get(s, 0), v)
        return list(w.items())

    def free_banks(banks, semname, v):
        for b in banks:
            pg[b] = (semname, v)

    p.op("gpsimd", lambda g: g.memset(acst[:, 0:1], EPS))
    p.op("gpsimd", lambda g: g.memset(acst[:, 1:2], -0.5))
    p.op("gpsimd", lambda g: g.memset(acst[:, 2:3], 1.0 / S))
    p.op("gpsimd", lambda g: g.memset(ones_s, 1.0))
    p.op("gpsimd", lambda g: g.memset(ident, 0.0))
    sg_id = p.op(
        "gpsimd",
        lambda g: g.affine_select(
            out=ident, in_=ident, compare_op=ALU.not_equal, fill=1.0,
            base=0, pattern=[[-1, 128]], channel_multiplier=1),
        inc="sg")
    sw_wqk = p.dma(
        "gpsimd",
        wqk_b.rearrange("q (c e) -> q c e", c=DC),
        wqk_in.rearrange("(c q) e -> q c e", q=128),
        "swqk")
    scst = None
    for (dst, src) in ((nrm_pp, nrm_in), (bhid_g, bhg_in), (gs_pp, gs_in),
                       (us_pp, us_in)):
        scst = p.dma("sync", dst, src, "scst")
    sbhr = p.dma("sync", bhv_row, bhv_in, "sbhr")

    wv3 = wv_sb.rearrange("q (b dc h) -> q b dc h", b=2, dc=16)
    swv = {0: 0, 1: 0}
    for c0 in range(2):
        for half in range(2):
            swv[c0] = p.dma(
                "gpsimd",
                wv3[:, c0, half * 8:(half + 1) * 8, :],
                whid_in[half * 1024:(half + 1) * 1024,
                        c0 * 1024:(c0 + 1) * 1024].rearrange("(dc q) h -> q dc h", q=128),
                f"swv{c0}")
    for j in range(2):
        p.dma("gpsimd", whid_bf[:, j * 2048:(j + 1) * 2048],
              whid_in[:, HV + j * 2048:HV + (j + 1) * 2048], f"scwb{2 + j}")

    st_setup = None
    for j in range(8):
        st_setup = p.op(
            "tensor",
            lambda tn, _j=j: tn.matmul(
                bankap(_j % 4) if _j < 4 else bankap(4 + _j % 4),
                ones_s, bhv_row[:, _j * 512:(_j + 1) * 512],
                start=True, stop=True),
            waits=[("sbhr", sbhr)] if j == 0 else (),
            inc="st" if j == 7 else None)
    sv_setup = p.op("vector", lambda v: v.tensor_copy(bhv_bc[:, 0:2048], PS_A),
                    waits=[("st", st_setup)])
    sv_setup = p.op("vector", lambda v: v.tensor_copy(bhv_bc[:, 2048:4096], PS_B),
                    inc="sv")
    free_banks(list(range(8)), "sv", sv_setup)

    sxr_cnt = {0: 0, 1: 0}

    def xs_tile(par):
        return x_s[:, par * D:(par + 1) * D]

    st_tr = [[None] * 4 for _ in range(TT)]
    sv_evac = [[None] * 4 for _ in range(TT)]
    sv_nrm = [None] * TT
    sa_rstd = [None] * TT
    ln1_fm3 = ln1_fm.rearrange("q (dc tok) -> q dc tok", dc=DC)
    c3 = None
    for t in range(TT):
        par = t % 2
        waits = []
        if t >= 2:
            waits.append(("sv", sv_nrm[t - 2]))
        sxr_cnt[par] = p.dma("sync", xs_tile(par), x_in[t * 128:(t + 1) * 128, :],
                             f"sxr{par}", waits=waits)
        w = [(f"sxr{par}", sxr_cnt[par])]
        sa_sum = p.op(
            "scalar",
            lambda sc, _p=par, _t=t: sc.activation(
                sqscr, xs_tile(_p), AF.Identity, accum_out=stats[:, _t:_t + 1]),
            waits=w, inc="sa")
        c_ssq = p.op(
            "vector",
            lambda v, _p=par, _t=t: v.scalar_tensor_tensor(
                sqscr, xs_tile(_p), 1.0, xs_tile(_p), ALU.mult, ALU.mult,
                accum_out=stats[:, 8 + _t:9 + _t]),
            waits=w, inc="sv")
        p.op("vector", lambda v, _t=t: v.tensor_scalar(
            stats[:, 16 + _t:17 + _t], stats[:, _t:_t + 1], 1.0 / D, None, ALU.mult),
            waits=[("sa", sa_sum)])
        c_m2 = p.op("vector", lambda v, _t=t: v.scalar_tensor_tensor(
            stats[:, 48:49], stats[:, _t:_t + 1], 1.0 / (D * D), stats[:, _t:_t + 1],
            ALU.mult, ALU.mult), inc="sv")
        wv_ = [("sv", c_m2)]
        if t >= 1:
            wv_.append(("sa", a1_prev))
        c_var = p.op("vector", lambda v, _t=t: v.scalar_tensor_tensor(
            stats[:, 49:50], stats[:, 8 + _t:9 + _t], 1.0 / D, stats[:, 48:49],
            ALU.mult, ALU.subtract), waits=wv_, inc="sv")
        a1 = p.op("scalar", lambda sc, _t=t: sc.activation(
            stats[:, 50:51], stats[:, 49:50], AF.Ln, bias=acst[:, 0:1]),
            waits=[("sv", c_var), ("sg", sg_id)], inc="sa")
        a1_prev = a1
        sa_rstd[t] = p.op("scalar", lambda sc, _t=t: sc.activation(
            stats[:, 24 + _t:25 + _t], stats[:, 50:51], AF.Exp, scale=acst[:, 1:2]),
            waits=[("sa", a1)], inc="sa")
        p.op("vector", lambda v, _t=t: v.scalar_tensor_tensor(
            stats[:, 32 + _t:33 + _t], stats[:, 16 + _t:17 + _t], -1.0,
            stats[:, 24 + _t:25 + _t], ALU.mult, ALU.mult),
            waits=[("sa", sa_rstd[t])])
        c_pre = p.op("vector", lambda v, _t=t: v.tensor_scalar(
            stats[:, 40 + _t:41 + _t], stats[:, 24 + _t:25 + _t], 1.0, None, ALU.add),
            inc="sv")
        w = [("sv", c_pre)]
        if t >= 1:
            w.append(("st", st_tr[t - 1][3]))
        sv_nrm[t] = p.op(
            "vector",
            lambda v, _p=par, _t=t: v.tensor_scalar(
                ln1_tm, xs_tile(_p), stats[:, 24 + _t:25 + _t],
                stats[:, 32 + _t:33 + _t], ALU.mult, ALU.add),
            waits=w, inc="sv")
        c3 = sv_nrm[t]
        for g in range(4):
            gpar = (t * 4 + g) % 2
            psv = PS_B[:, 1024 + gpar * 512:1024 + gpar * 512 + 512]
            w = [("sv", sv_nrm[t]), ("sg", sg_id)]
            prev = (t * 4 + g) - 2
            if prev >= 0:
                w.append(("sv", sv_evac[prev // 4][prev % 4]))
            for i in range(4):
                inc = "st" if i == 3 else None
                r = p.op(
                    "tensor",
                    lambda tn, _i=i, _g=g, _t=t, _ps=psv: tn.transpose(
                        _ps[:, _i * 128:(_i + 1) * 128],
                        ln1_tm[:, (_g * 4 + _i) * 128:(_g * 4 + _i + 1) * 128],
                        ident),
                    waits=w if i == 0 else (), inc=inc)
            st_tr[t][g] = r
            sv_evac[t][g] = p.op(
                "vector",
                lambda v, _g=g, _t=t, _ps=psv: v.tensor_copy(
                    ln1_fm3[:, _g * 4:(_g + 1) * 4, _t * 128:(_t + 1) * 128],
                    _ps.rearrange("q (i f) -> q i f", i=4)),
                waits=[("st", st_tr[t][g])], inc="sv")

    pg[6] = ("sv", sv_evac[TT - 1][2])
    pg[7] = ("sv", sv_evac[TT - 1][3])

    qk_banks = [0, 1]
    w = [("swqk", sw_wqk), ("sv", sv_evac[TT - 1][3]), ("scst", scst)] + bank_waits(qk_banks)
    st_qk = None
    for dc in range(DC):
        for s in range(2):
            inc = "st" if (dc == DC - 1 and s == 1) else None
            st_qk = p.op(
                "tensor",
                lambda tn, _dc=dc, _s=s: tn.matmul(
                    PS_A[:, _s * 512:(_s + 1) * 512],
                    wqk_b[:, _dc * 128:(_dc + 1) * 128],
                    ln1_fm[:, _dc * 1024 + _s * 512:_dc * 1024 + (_s + 1) * 512],
                    start=(_dc == 0), stop=(_dc == DC - 1)),
                waits=w if (dc == 0 and s == 0) else (), inc=inc) or st_qk
    sa_qk = p.op("scalar", lambda s: s.activation(
        qk_f32, PS_A[:, 0:1024], AF.Silu, bias=nrm_pp[:, 0:1]),
        waits=[("st", st_qk)], inc="sa")
    free_banks(qk_banks, "sa", sa_qk)
    p.op("vector", lambda v: v.tensor_scalar(
        q_bf, qk_f32, nrm_pp[:, 1:2], nrm_pp[:, 2:3], ALU.mult, ALU.add),
        waits=[("sa", sa_qk)])
    sv_qk = p.op("vector", lambda v: v.tensor_scalar(
        k_own, qk_f32, nrm_pp[:, 3:4], nrm_pp[:, 4:5], ALU.mult, ALU.add), inc="sv")
    skw = p.dma("scalar", k_cin, k_own, "skw", waits=[("sv", sv_qk)])
    sck = p.op(
        "gpsimd",
        lambda g: g.collective_compute(
            "AllGather", ALU.bypass, replica_groups=[list(range(NCORES))],
            ins=[k_cin.opt()], outs=[k_call.opt()]),
        waits=[("skw", skw)], inc="sck")
    skf = None
    for r in range(NCORES):
        skf = p.dma("scalar", k_full[:, r * T:(r + 1) * T],
                    k_call[r * 128:(r + 1) * 128, :], "skf",
                    waits=[("sck", sck)] if r == 0 else ())

    st_vmm_c = [None] * 4
    sa_vsl = {}
    svw_cnt = [0] * 4
    scv = [None] * 4
    sv_sb2 = v_sb.rearrange("q (b f) -> q b f", b=2)

    nvs = 0
    sa_sil = {}
    for c in range(4):
        par = c % 2
        for t in range(TT):
            tpar = t % 2
            for sl_ in range(2):
                banks = [tpar * 2 + sl_]
                w = [(f"swv{par}", swv[par])] + bank_waits(banks)
                ps = bankap(banks[0])
                for dc in range(DC):
                    inc = "st" if dc == DC - 1 else None
                    st_v = p.op(
                        "tensor",
                        lambda tn, _dc=dc, _t=t, _s=sl_, _c=c, _par=par, _ps=ps: tn.matmul(
                            _ps,
                            ln1_fm[:, _dc * 1024 + _t * 128:_dc * 1024 + (_t + 1) * 128],
                            wv_sb[:, _par * 16384 + (_dc // 8) * 8192 + (_dc % 8) * 1024 + _s * 512:
                                  _par * 16384 + (_dc // 8) * 8192 + (_dc % 8) * 1024 + (_s + 1) * 512],
                            start=(_dc == 0), stop=(_dc == DC - 1)),
                        waits=w if dc == 0 else (), inc=inc)
                vpar = nvs % 2
                w2 = [("st", st_v), ("sv", sv_setup)]
                if nvs >= 2:
                    w2.append(("sa", sa_sil[nvs - 2]))
                sv_vb = p.op(
                    "vector",
                    lambda v, _vp=vpar, _c=c, _s=sl_, _ps=ps: v.tensor_tensor(
                        simscr[:, _vp * 512:(_vp + 1) * 512], _ps,
                        bhv_bc[:, _c * 1024 + _s * 512:_c * 1024 + (_s + 1) * 512],
                        ALU.add),
                    waits=w2, inc="sv")
                free_banks(banks, "sv", sv_vb)
                w2 = [("sv", sv_vb)]
                if t >= 2:
                    w2.append((f"svw{c}p{tpar}", ((t - 2) // 2 + 1) * 16))
                sa_v = p.op(
                    "scalar",
                    lambda sc, _vp=vpar, _tp=tpar, _s=sl_: sc.activation(
                        sv_sb2[:, _tp, _s * 512:(_s + 1) * 512],
                        simscr[:, _vp * 512:(_vp + 1) * 512], AF.Silu),
                    waits=w2, inc="sa")
                sa_sil[nvs] = sa_v
                sa_vsl[(c, t, sl_)] = sa_v
                nvs += 1
            svw_cnt[c] = p.dma(
                "sync", v_cin[c][t * 128:(t + 1) * 128, :], sv_sb2[:, tpar, :],
                f"svw{c}p{tpar}", waits=[("sa", sa_vsl[(c, t, 1)])])
        st_vmm_c[c] = st_v
        if c + 2 <= 3:
            cn = c + 2
            for half in range(2):
                swv[cn % 2] = p.dma(
                    "gpsimd",
                    wv3[:, cn % 2, half * 8:(half + 1) * 8, :],
                    whid_in[half * 1024:(half + 1) * 1024,
                            cn * 1024:(cn + 1) * 1024].rearrange("(dc q) h -> q dc h", q=128),
                    f"swv{cn % 2}",
                    waits=[("st", st_vmm_c[c])] if half == 0 else ())
        scv[c] = p.op(
            "gpsimd",
            lambda g, _c=c: g.collective_compute(
                "AllGather", ALU.bypass, replica_groups=[list(range(NCORES))],
                ins=[v_cin[_c].opt()], outs=[v_call[_c].opt()]),
            waits=[(f"svw{c}p0", 64), (f"svw{c}p1", 64)], inc=f"scv{c}")

    scwo = None
    for j in range(4):
        scwo = p.dma("gpsimd", wout_bf[j * 1024:(j + 1) * 1024, :],
                     wout_in[j * 1024:(j + 1) * 1024, :], "scwo",
                     waits=[("scv3", 1)] if j == 0 else ())
    scdw = None
    for j in range(4):
        scdw = p.dma("gpsimd", dwt_bf[j * 1408:(j + 1) * 1408, :],
                     dwt_in[j * 1408:(j + 1) * 1408, :], "scdw")

    swg_cnt = p.dma(
        "sync", wg_sb.rearrange("q (dc h) -> q dc h", dc=DC),
        whid_bf[:, 0:512].rearrange("(dc q) h -> q dc h", q=128),
        "swg", waits=[("scwb2", 16), ("scwb3", 16), ("st", st_setup)])

    if stage < 1:
        raise ValueError
    if stage == 1:
        _emit_debug_stage1(p, nc, dbg, locals())
        return _finalize(nc, p, stack)


    vst3 = vst.rearrange("q (b kt h) -> q b kt h", b=2, kt=8)
    svst = {0: 0, 1: 0}
    st_avm = {}
    sv_om_end = {}
    sa_gsl = {}
    st_gmm = {}
    swo = {0: 0, 1: 0}
    st_womm = {}
    sv_x2a = {}
    sv_l2n = {}
    sx2w_cnt = 0
    sl2w_cnt = 0
    sa_t1 = {}
    vst_free = {}
    st_simm_last = None

    for qb in range(QBN):
        sv_relu = [None] * KT
        sa_sqr = [None] * KT
        for kt in range(KT):
            b = kt % 4
            w = [("skf", skf)] + bank_waits([b])
            st_s = p.op(
                "tensor",
                lambda tn, _kt=kt, _qb=qb, _b=b: tn.matmul(
                    bankap(_b), k_full[:, _kt * 128:(_kt + 1) * 128],
                    q_bf[:, _qb * QW:(_qb + 1) * QW],
                    start=True, stop=True),
                waits=w, inc="st")
            st_simm_last = st_s
            w = [("st", st_s)]
            if kt >= 2:
                w.append(("sa", sa_sqr[kt - 2]))
            sv_relu[kt] = p.op(
                "vector",
                lambda v, _kt=kt, _b=b: v.tensor_scalar(
                    simscr[:, (_kt % 2) * 512:(_kt % 2) * 512 + 512],
                    bankap(_b), 0.0, None, ALU.max),
                waits=w, inc="sv")
            pg[b] = ("sv", sv_relu[kt])
            w = [("sv", sv_relu[kt])]
            if kt == 0:
                if qb == 0:
                    w += [(f"svw{c}p{pp}", 64) for c in range(4) for pp in range(2)]
                else:
                    w += [("sx2w0", 32), ("sx2w1", 32), ("sl2w0", 32), ("sl2w1", 32)]
            sa_sqr[kt] = p.op(
                "scalar",
                lambda sc, _kt=kt: sc.activation(
                    attnT[:, _kt * 512:(_kt + 1) * 512],
                    simscr[:, (_kt % 2) * 512:(_kt % 2) * 512 + 512],
                    AF.Square, scale=acst[:, 2:3]),
                waits=w, inc="sa")

        for hg in range(8):
            c, half = hg // 2, hg % 2
            if (qb, hg) != (0, 0):
                prev = (qb, hg - 1) if hg > 0 else (qb - 1, 7)
                w = [("scwb2", 16), ("scwb3", 16), ("st", st_gmm[prev + (3,)])]
                swg_cnt = p.dma(
                    "sync",
                    wg_sb.rearrange("q (dc h) -> q dc h", dc=DC),
                    whid_bf[:, hg * 512:(hg + 1) * 512].rearrange(
                        "(dc q) h -> q dc h", q=128),
                    "swg", waits=w)
            for hc in range(4):
                bnk = 4 + hc
                w = [("swg", swg_cnt)] + bank_waits([bnk])
                for dc in range(DC):
                    inc = "st" if dc == DC - 1 else None
                    r = p.op(
                        "tensor",
                        lambda tn, _dc=dc, _hc=hc, _qb=qb: tn.matmul(
                            PS_B[:, _hc * 512:(_hc + 1) * 512],
                            wg_sb[:, _dc * 512 + _hc * 128:_dc * 512 + (_hc + 1) * 128],
                            ln1_fm[:, _dc * 1024 + _qb * QW:_dc * 1024 + _qb * QW + QW],
                            start=(_dc == 0), stop=(_dc == DC - 1)),
                        waits=w if dc == 0 else (), inc=inc)
                st_gmm[(qb, hg, hc)] = r
                w = [("st", r)]
                if (qb, hg) != (0, 0):
                    prevhg = (qb, hg - 1) if hg > 0 else (qb - 1, 7)
                    w.append(("sv", sv_om_end[prevhg]))
                if qb == 1 and hg == 0:
                    w.append(("sv", sv_l2n[(qb - 1) * 4 + 3]))
                sa_gsl[(qb, hg, hc)] = p.op(
                    "scalar",
                    lambda sc, _hc=hc, _hg=hg: sc.activation(
                        gate_f[:, _hc * 512:(_hc + 1) * 512],
                        PS_B[:, _hc * 512:(_hc + 1) * 512],
                        AF.Silu, bias=bhid_g[:, _hg * 4 + _hc:_hg * 4 + _hc + 1]),
                    waits=w, inc="sa")
                pg[4 + hc] = ("sa", sa_gsl[(qb, hg, hc)])
            for ks in range(8):
                par = ks % 2
                w = [(f"scv{c}", scv[c])]
                if par in vst_free:
                    w.append(("st", vst_free[par]))
                if qb == 0 and hg == 0 and ks < 2:
                    w.append(("sv", sv_nrm[TT - 1]))
                if qb == 1 and hg == 0 and ks < 2:
                    w.append(("sv", sa_t1[(qb - 1) * 4 + 3]))
                svst[par] = p.dma(
                    "sync",
                    vst3[:, par, :, :],
                    v_call[c][ks * 1024:(ks + 1) * 1024,
                              half * 512:(half + 1) * 512].rearrange(
                                  "(kt q) h -> q kt h", q=128),
                    f"svst{par}", waits=w)
                w = [(f"svst{par}", svst[par]), ("sa", sa_sqr[min(ks * 8 + 7, KT - 1)])]
                if ks == 0:
                    w += bank_waits([0, 1, 2, 3])
                first = True
                for kt in range(8):
                    for hc in range(4):
                        inc = "st" if (kt == 7 and hc == 3) else None
                        r = p.op(
                            "tensor",
                            lambda tn, _ks=ks, _kt=kt, _hc=hc, _par=par: tn.matmul(
                                bankap(_hc),
                                vst[:, _par * 4096 + _kt * 512 + _hc * 128:
                                    _par * 4096 + _kt * 512 + (_hc + 1) * 128],
                                attnT[:, (_ks * 8 + _kt) * 512:(_ks * 8 + _kt + 1) * 512],
                                start=(_ks == 0 and _kt == 0), stop=(_ks == 7 and _kt == 7)),
                            waits=w if first else (), inc=inc)
                        first = False
                st_avm[(qb, hg, ks)] = r
                vst_free[par] = r
            last = None
            for hc in range(4):
                w = [("st", st_avm[(qb, hg, 7)]), ("sa", sa_gsl[(qb, hg, hc)])]
                last = p.op(
                    "vector",
                    lambda v, _hc=hc, _hg=hg: v.tensor_tensor(
                        om[:, (_hg * 4 + _hc) * 512:(_hg * 4 + _hc + 1) * 512],
                        bankap(_hc), gate_f[:, _hc * 512:(_hc + 1) * 512], ALU.mult),
                    waits=w, inc="sv")
                pg[hc] = ("sv", last)
            sv_om_end[(qb, hg)] = last

        sbo = p.dma("sync", bout_row, bout_in, f"sbo{qb}",
                    waits=[("st", st_avm[(qb, 7, 7)])])
        st_bo = None
        w = [(f"sbo{qb}", sbo)] + bank_waits([0, 1, 2, 3])
        for sl_ in range(4):
            st_bo = p.op(
                "tensor",
                lambda tn, _sl=sl_: tn.matmul(
                    PS_A[:, _sl * 512:(_sl + 1) * 512], ones_s,
                    bout_row[:, _sl * 512:(_sl + 1) * 512], start=True, stop=True),
                waits=w if sl_ == 0 else (), inc="st" if sl_ == 3 else None)
        sv_bo = p.op("vector", lambda v: v.tensor_copy(bout_bc, PS_A),
                     waits=[("st", st_bo)], inc="sv")
        free_banks([0, 1, 2, 3], "sv", sv_bo)
        for qtp in range(2):
            qts = [qb * 4 + qtp * 2, qb * 4 + qtp * 2 + 1]
            first_w = bank_waits(list(range(8)))
            for hc in range(32):
                wpar = hc % 2
                w = [("scwo", 64)]
                key = (qb, qtp, hc - 2)
                if (qb, qtp) != (0, 0) or hc >= 2:
                    if hc >= 2:
                        w.append(("st", st_womm.get(("mm", qb, qtp, hc - 2), 0)))
                    else:
                        prev = (qb, qtp - 1) if qtp > 0 else (qb - 1, 1)
                        w.append(("st", st_womm.get(("mm", prev[0], prev[1], 30 + hc), 0)))
                if qtp == 0 and hc < 2:
                    w.append(("st", st_avm[(qb, 7, 7)]))
                swo[wpar] = p.dma(
                    "sync",
                    wo_sb[:, wpar * D:(wpar + 1) * D],
                    wout_bf[hc * 128:(hc + 1) * 128, :],
                    f"swo{wpar}", waits=w)
                w2 = [(f"swo{wpar}", swo[wpar])]
                if hc == 0:
                    w2 += first_w
                firstmm = True
                for j in range(2):
                    pst = PS_A if j == 0 else PS_B
                    for sl in range(4):
                        r = p.op(
                            "tensor",
                            lambda tn, _hc=hc, _j=j, _sl=sl, _qtp=qtp, _qb=qb, _wp=wpar, _ps=pst: tn.matmul(
                                _ps[:, _sl * 512:(_sl + 1) * 512],
                                om[:, _hc * 512 + (_qtp * 2 + _j) * 128:
                                   _hc * 512 + (_qtp * 2 + _j) * 128 + 128],
                                wo_sb[:, _wp * D + _sl * 512:_wp * D + (_sl + 1) * 512],
                                start=(_hc == 0), stop=(_hc == 31)),
                            waits=w2 if firstmm else (), inc="st" if (j == 1 and sl == 3) else None)
                        firstmm = False
                st_womm[("mm", qb, qtp, hc)] = r
            for j in range(2):
                st_womm[qts[j]] = st_womm[("mm", qb, qtp, 31)]
            for j in range(2):
                qt = qts[j]
                par = qt % 2
                pst = PS_A if j == 0 else PS_B
                w = [("st", st_avm[(qb, 7, 7)])]
                if qt >= 2:
                    w.append(("sv", sa_t1[qt - 2]))
                sxr_cnt[par] = p.dma("sync", xs_tile(par),
                                     x_in[(qt % 4) * 128 + qb * QW:(qt % 4) * 128 + qb * QW + 128, :],
                                     f"sxr{par}", waits=w)
                w = [(f"sxr{par}", sxr_cnt[par]), ("sv", c3)]
                if qb == 0 and qtp == 0 and j == 0:
                    w.append(("sv", sv_om_end[(qb, 7)]))
                sa_t1[qt] = p.op(
                    "vector",
                    lambda v, _p=par, _qt=qt: v.tensor_scalar(
                        t1_s, xs_tile(_p), stats[:, 40 + _qt:41 + _qt],
                        stats[:, 32 + _qt:33 + _qt], ALU.mult, ALU.add),
                    waits=w, inc="sv")
                w = [("st", st_womm[qt]), ("sv", sa_t1[qt])]
                if qt >= 2:
                    w.append((f"sx2w{qt % 2}", 16 * ((qt - 2) // 2 + 1)))
                p.op(
                    "vector",
                    lambda v, _p=par, _ps=pst: v.tensor_tensor(
                        x2_sb[:, _p * D:(_p + 1) * D], _ps, bout_bc, ALU.add),
                    waits=w + [("sv", sv_bo)])
                sv_x2a[qt] = p.op(
                    "vector",
                    lambda v, _p=par: v.tensor_tensor(
                        x2_sb[:, _p * D:(_p + 1) * D], x2_sb[:, _p * D:(_p + 1) * D],
                        t1_s, ALU.add),
                    inc="sv")
                free_banks(list(range(4)) if j == 0 else [4, 5, 6, 7], "sv", sv_x2a[qt])
                p.op("vector", lambda v, _p=par, _qt=qt: v.reduce_sum(
                    stats[:, 64 + _qt:65 + _qt], x2_sb[:, _p * D:(_p + 1) * D], AX.X))
                s1 = p.op("vector", lambda v, _p=par, _qt=qt: v.scalar_tensor_tensor(
                    sqscr, x2_sb[:, _p * D:(_p + 1) * D], 1.0,
                    x2_sb[:, _p * D:(_p + 1) * D], ALU.mult, ALU.mult,
                    accum_out=stats[:, 72 + _qt:73 + _qt]), inc="sv")
                p.op("vector", lambda v, _qt=qt: v.tensor_scalar(
                    stats[:, 88:89], stats[:, 64 + _qt:65 + _qt], 1.0 / D, None, ALU.mult),
                    waits=[("sv", s1)])
                s2 = p.op("vector", lambda v, _qt=qt: v.scalar_tensor_tensor(
                    stats[:, 89:90], stats[:, 64 + _qt:65 + _qt], 1.0 / (D * D),
                    stats[:, 64 + _qt:65 + _qt], ALU.mult, ALU.mult), inc="sv")
                s3 = p.op("vector", lambda v, _qt=qt: v.scalar_tensor_tensor(
                    stats[:, 90:91], stats[:, 72 + _qt:73 + _qt], 1.0 / D,
                    stats[:, 89:90], ALU.mult, ALU.subtract),
                    waits=[("sv", s2)], inc="sv")
                a3 = p.op("scalar", lambda s: s.activation(
                    stats[:, 91:92], stats[:, 90:91], AF.Ln, bias=acst[:, 0:1]),
                    waits=[("sv", s3)], inc="sa")
                a4 = p.op("scalar", lambda s, _qt=qt: s.activation(
                    stats[:, 80 + _qt:81 + _qt], stats[:, 91:92], AF.Exp,
                    scale=acst[:, 1:2]), waits=[("sa", a3)], inc="sa")
                s4 = p.op("vector", lambda v, _qt=qt: v.scalar_tensor_tensor(
                    stats[:, 92:93], stats[:, 88:89], -1.0,
                    stats[:, 80 + _qt:81 + _qt], ALU.mult, ALU.mult),
                    waits=[("sa", a4)], inc="sv")
                w = [("sv", s4)]
                if qt >= 2:
                    w.append((f"sl2w{qt % 2}", 16 * ((qt - 2) // 2 + 1)))
                sv_l2n[qt] = p.op(
                    "vector",
                    lambda v, _p=par, _qt=qt: v.tensor_scalar(
                        ln2_tm[:, _p * D:(_p + 1) * D], x2_sb[:, _p * D:(_p + 1) * D],
                        stats[:, 80 + _qt:81 + _qt], stats[:, 92:93],
                        ALU.mult, ALU.add),
                    waits=w, inc="sv")
                sx2w_cnt = p.dma("scalar", x2_dram[qt * 128:(qt + 1) * 128, :],
                                 x2_sb[:, par * D:(par + 1) * D], f"sx2w{qt % 2}",
                                 waits=[("sv", sv_x2a[qt])])
                sl2w_cnt = p.dma("scalar", ln2_dram[qt * 128:(qt + 1) * 128, :],
                                 ln2_tm[:, par * D:(par + 1) * D], f"sl2w{qt % 2}",
                                 waits=[("sv", sv_l2n[qt])])

    if stage == 2:
        _emit_debug_stage2(p, nc, dbg, locals())
        return _finalize(nc, p, stack)

    w = [("sl2w0", 64), ("sl2w1", 64), ("st", st_gmm[(1, 7, 3)])]
    for dc in range(DC):
        p.dma("sync", ln2_fm[:, dc * 1024:(dc + 1) * 1024],
              ln2_dram[:, dc * 128:(dc + 1) * 128], f"sl2t{dc}",
              waits=w if dc == 0 else (), transpose=True)

    gw3 = gw_sb.rearrange("q (b dc i) -> q b dc i", b=2, dc=DC)
    uw3 = uw_sb.rearrange("q (b dc i) -> q b dc i", b=2, dc=DC)
    sfg = {0: 0, 1: 0}
    sfu = {0: 0, 1: 0}
    st_gumm = [None] * ICN
    sa_gs2 = [None] * ICN
    sv_hdn = [None] * ICN
    npairs = ICN // 2
    for pr in range(npairs):
        par = pr % 2
        w = []
        if pr >= 2:
            w.append(("st", st_gumm[(pr - 2) * 2 + 1]))
        if pr < 2:
            w.append(("st", st_simm_last))
            w.append(("st", st_womm[7]))
        sfg[par] = p.dma(
            "gpsimd", gw3[:, par, :, :],
            gwt_in[:, pr * 256:(pr + 1) * 256].rearrange("(dc q) i -> q dc i", q=128),
            f"sfg{par}", waits=w)
        sfu[par] = p.dma(
            "gpsimd", uw3[:, par, :, :],
            uwt_in[:, pr * 256:(pr + 1) * 256].rearrange("(dc q) i -> q dc i", q=128),
            f"sfu{par}")
        for ic in (2 * pr, 2 * pr + 1):
            icp = ic % 2
            psg = (PS_A if icp == 0 else PS_B)[:, 0:1024]
            psu = (PS_A if icp == 0 else PS_B)[:, 1024:2048]
            banks = [0, 1, 2, 3] if icp == 0 else [4, 5, 6, 7]
            w = [(f"sfg{par}", sfg[par]), (f"sfu{par}", sfu[par])] + bank_waits(banks)
            first = True
            for dc in range(DC):
                for sl_ in range(2):
                    wdc = []
                    if ic == 0 and sl_ == 0:
                        wdc = [(f"sl2t{dc}", 16)]
                    p.op(
                        "tensor",
                        lambda tn, _dc=dc, _s=sl_, _ic=ic, _par=par, _ps=psg: tn.matmul(
                            _ps[:, _s * 512:(_s + 1) * 512],
                            gw_sb[:, _par * 4096 + _dc * 256 + (_ic % 2) * 128:
                                  _par * 4096 + _dc * 256 + (_ic % 2) * 128 + 128],
                            ln2_fm[:, _dc * 1024 + _s * 512:_dc * 1024 + (_s + 1) * 512],
                            start=(_dc == 0), stop=(_dc == DC - 1)),
                        waits=(w + wdc) if first else wdc)
                    first = False
                    inc = "st" if (dc == DC - 1 and sl_ == 1) else None
                    r = p.op(
                        "tensor",
                        lambda tn, _dc=dc, _s=sl_, _ic=ic, _par=par, _ps=psu: tn.matmul(
                            _ps[:, _s * 512:(_s + 1) * 512],
                            uw_sb[:, _par * 4096 + _dc * 256 + (_ic % 2) * 128:
                                  _par * 4096 + _dc * 256 + (_ic % 2) * 128 + 128],
                            ln2_fm[:, _dc * 1024 + _s * 512:_dc * 1024 + (_s + 1) * 512],
                            start=(_dc == 0), stop=(_dc == DC - 1)),
                        inc=inc)
            st_gumm[ic] = r
            w = [("st", st_gumm[ic])]
            if ic >= 2:
                w.append(("sv", sv_hdn[ic - 2]))
            if ic == 0:
                w.append(("sv", sv_l2n[7]))
            sa_gs2[ic] = p.op(
                "scalar",
                lambda sc, _ic=ic, _ps=psg: sc.activation(
                    gs_scr[:, (_ic % 2) * 1024:(_ic % 2) * 1024 + 1024], _ps,
                    AF.Silu, scale=gs_pp[:, _ic:_ic + 1]),
                waits=w, inc="sa")
            w = [("sa", sa_gs2[ic])]
            if ic == 0:
                w += [("sx2w0", 64), ("sx2w1", 64), ("sl2w0", 64), ("sl2w1", 64),
                      ("st", st_womm[7])]
            sv_hdn[ic] = p.op(
                "vector",
                lambda v, _ic=ic, _ps=psu: v.scalar_tensor_tensor(
                    hdn[:, _ic * 1024:(_ic + 1) * 1024], _ps, us_pp[:, _ic:_ic + 1],
                    gs_scr[:, (_ic % 2) * 1024:(_ic % 2) * 1024 + 1024],
                    ALU.mult, ALU.mult),
                waits=w, inc="sv")
            free_banks(banks, "sv", sv_hdn[ic])

    sdsr = p.dma("sync", ds_row, ds_in, "sdsr", waits=[("st", st_gumm[ICN - 1])])
    w = [("sdsr", sdsr)] + bank_waits([0, 1, 2, 3])
    st_dsb = None
    for sl in range(4):
        st_dsb = p.op(
            "tensor",
            lambda tn, _sl=sl: tn.matmul(
                PS_A[:, _sl * 512:(_sl + 1) * 512], ones_s,
                ds_row[:, _sl * 512:(_sl + 1) * 512], start=True, stop=True),
            waits=w if sl == 0 else (), inc="st" if sl == 3 else None)
    sv_dsb = p.op("vector", lambda v: v.tensor_copy(ds_bc, PS_A),
                  waits=[("st", st_dsb)], inc="sv")
    free_banks([0, 1, 2, 3], "sv", sv_dsb)

    dw3 = dw_sb.rearrange("q (b ic d) -> q b ic d", b=2, ic=4)
    sfd = {0: 0, 1: 0}
    sxf = {0: 0, 1: 0}
    syw_cnt = 0
    st_dn = {}
    sv_out = {}
    outp = out_sb.rearrange("q (b f) -> q b f", b=2)
    nouts = 0
    for quad in range(2):
        for dh in range(2):
            gdh = quad * 2 + dh
            for bt in range(11):
                par = (gdh * 11 + bt) % 2
                w = [("scdw", 64)]
                gi = gdh * 11 + bt
                if bt >= 2:
                    w.append(("st", st_dn[("mm", quad, dh, min(ICN - 1, (bt - 2) * 4 + 3))]))
                elif gdh > 0:
                    pq, pd = divmod(gdh - 1, 2)
                    w.append(("st", st_dn[("mm", pq, pd, min(ICN - 1, (bt + 9) * 4 + 3))]))
                if gi < 2:
                    w.append(("sv", sv_hdn[ICN - 1]))
                sfd[par] = p.dma(
                    "sync", dw3[:, par, :, :],
                    dwt_bf[bt * 512:(bt + 1) * 512,
                           dh * 1024:(dh + 1) * 1024].rearrange("(ic q) d -> q ic d", q=128),
                    f"sfd{par}", waits=w)
                for ic in range(bt * 4, min(ICN, (bt + 1) * 4)):
                    w2 = [(f"sfd{par}", sfd[par])]
                    if ic == 0:
                        w2 += bank_waits(list(range(8)))
                    first = True
                    for j in range(4):
                        qt = quad * 4 + j
                        psq = (PS_A if j < 2 else PS_B)[:, (j % 2) * 1024:(j % 2) * 1024 + 1024]
                        for s_ in range(2):
                            inc = "st" if (j == 3 and s_ == 1) else None
                            r = p.op(
                                "tensor",
                                lambda tn, _ic=ic, _j=j, _s=s_, _qt=qt, _par=par, _ps=psq: tn.matmul(
                                    _ps[:, _s * 512:(_s + 1) * 512],
                                    hdn[:, _ic * 1024 + _qt * 128:_ic * 1024 + (_qt + 1) * 128],
                                    dw_sb[:, _par * 4096 + (_ic % 4) * 1024 + _s * 512:
                                          _par * 4096 + (_ic % 4) * 1024 + (_s + 1) * 512],
                                    start=(_ic == 0), stop=(_ic == ICN - 1)),
                                waits=w2 if first else (), inc=inc)
                            first = False
                    st_dn[("mm", quad, dh, ic)] = r
            for j in range(2):
                qt = quad * 4 + j
                opar = (nouts + j) % 2
                w = []
                if nouts + j >= 2:
                    w.append(("sv", sv_out[nouts + j - 2]))
                else:
                    w.append(("sv", sv_l2n[7]))
                sxf[opar] = p.dma(
                    "sync", x2h[:, opar * 1024:(opar + 1) * 1024],
                    x2_dram[qt * 128:(qt + 1) * 128, dh * 1024:(dh + 1) * 1024],
                    f"sxf{opar}", waits=w)
            for j in range(4):
                qt = quad * 4 + j
                psq = (PS_A if j < 2 else PS_B)[:, (j % 2) * 1024:(j % 2) * 1024 + 1024]
                opar = nouts % 2
                if j >= 2:
                    sxf[opar] = p.dma(
                        "sync", x2h[:, opar * 1024:(opar + 1) * 1024],
                        x2_dram[qt * 128:(qt + 1) * 128, dh * 1024:(dh + 1) * 1024],
                        f"sxf{opar}", waits=[("sv", sv_out[nouts - 2])])
                w = [("st", st_dn[("mm", quad, dh, ICN - 1)]), ("sv", sv_dsb)]
                if nouts >= 2:
                    w.append((f"syw{nouts % 2}", 16 * ((nouts - 2) // 2 + 1)))
                p.op(
                    "vector",
                    lambda v, _o=opar, _dh=dh, _ps=psq: v.tensor_tensor(
                        outp[:, _o, :], _ps, ds_bc[:, _dh * 1024:(_dh + 1) * 1024],
                        ALU.mult),
                    waits=w, inc=None)
                sv_out[nouts] = p.op(
                    "vector",
                    lambda v, _o=opar: v.tensor_tensor(
                        outp[:, _o, :], outp[:, _o, :], x2h[:, _o * 1024:(_o + 1) * 1024],
                        ALU.add),
                    waits=[(f"sxf{opar}", sxf[opar])], inc="sv")
                free_banks([0, 1] if j == 0 else [2, 3] if j == 1 else [4, 5] if j == 2 else [6, 7],
                           "sv", sv_out[nouts])
                syw_cnt = p.dma("sync", y_out[qt * 128:(qt + 1) * 128, dh * 1024:(dh + 1) * 1024],
                                outp[:, opar, :], f"syw{nouts % 2}",
                                waits=[("sv", sv_out[nouts])])
                nouts += 1

    p.op("sync", lambda e: None, waits=[("syw0", 128), ("syw1", 128)])
    if dbg:
        d_ln2fm = ap(nc.dram_tensor("d_ln2fm", [128, DC * 1024], BF16, kind="ExternalOutput"))
        d_hdn = ap(nc.dram_tensor("d_hdn", [128, ICN * 1024], BF16, kind="ExternalOutput"))
        d_dsb = ap(nc.dram_tensor("d_dsb", [128, D], F32, kind="ExternalOutput"))
        so = p.dma("sync", d_ln2fm, ln2_fm, "sout", waits=[(f"sl2t{DC-1}", 16)])
        so = p.dma("sync", d_hdn, hdn, "sout", waits=[("sv", sv_hdn[ICN - 1])])
        so = p.dma("sync", d_dsb, ds_bc, "sout", waits=[("sv", sv_dsb)])
        p.op("sync", lambda e: None, waits=[("sout", so)])
    return _finalize(nc, p, stack)


def _emit_debug_stage1(p, nc, dbg, L):
    outs = {}
    for name, shape, dt in (("d_ln1fm", [128, DC * 1024], BF16),
                            ("d_qk", [128, T], F32),
                            ("d_q", [128, T], BF16),
                            ("d_kfull", [128, S], BF16),
                            ("d_v0", [S, 1024], BF16),
                            ("d_v1", [S, 1024], BF16),
                            ("d_v2", [S, 1024], BF16),
                            ("d_v3", [S, 1024], BF16),
                            ("d_vc3", [T, 1024], BF16),
                            ("d_stats", [128, 96], F32)):
        outs[name] = nc.dram_tensor(name, shape, dt, kind="ExternalOutput").ap()
    w = [("sv", L["sv_qk"]), ("sv", L["sv_evac"][TT - 1][3]), ("skf", L["skf"]),
         ("scv0", L["scv"][0]), ("scv1", L["scv"][1]), ("scv2", L["scv"][2]),
         ("scv3", L["scv"][3])]
    so = p.dma("sync", outs["d_ln1fm"], L["ln1_fm"], "sout", waits=w)
    so = p.dma("sync", outs["d_qk"], L["qk_f32"], "sout")
    so = p.dma("sync", outs["d_q"], L["q_bf"], "sout")
    so = p.dma("sync", outs["d_kfull"], L["k_full"], "sout")
    so = p.dma("sync", outs["d_v0"], L["v_call"][0], "sout")
    so = p.dma("sync", outs["d_v1"], L["v_call"][1], "sout")
    so = p.dma("sync", outs["d_v2"], L["v_call"][2], "sout")
    so = p.dma("sync", outs["d_v3"], L["v_call"][3], "sout")
    so = p.dma("sync", outs["d_vc3"], L["v_cin"][3], "sout")
    so = p.dma("sync", outs["d_stats"], L["stats"], "sout")
    p.op("sync", lambda e: None, waits=[("sout", so)])


def _emit_debug_stage2(p, nc, dbg, L):
    outs = {}
    for name, shape, dt in (("d_x2", [T, D], F32),
                            ("d_ln2", [T, D], BF16),
                            ("d_om", [128, 32 * 512], BF16),
                            ("d_at", [128, KT * 512], BF16)):
        outs[name] = nc.dram_tensor(name, shape, dt, kind="ExternalOutput").ap()
    w = [("sx2w0", 64), ("sx2w1", 64), ("sl2w0", 64), ("sl2w1", 64),
         ("sv", L["sv_om_end"][(1, 7)]), ("sa", L["sa_sqr"][KT - 1])]
    so = p.dma("sync", outs["d_x2"], L["x2_dram"], "sout", waits=w)
    so = p.dma("sync", outs["d_ln2"], L["ln2_dram"], "sout")
    so = p.dma("sync", outs["d_om"], L["om"], "sout")
    so = p.dma("sync", outs["d_at"], L["attnT"], "sout")
    p.op("sync", lambda e: None, waits=[("sout", so)])


def _finalize(nc, p, stack):
    with nc.Block() as block:
        block.tensor(lambda e: [f(e) for f in p.q["tensor"]])
        block.vector(lambda e: [f(e) for f in p.q["vector"]])
        block.scalar(lambda e: [f(e) for f in p.q["scalar"]])
        block.sync(lambda e: [f(e) for f in p.q["sync"]])
        block.gpsimd(lambda e: [f(e) for f in p.q["gpsimd"]])
    stack.close()
    return nc


def make_inputs(inputs):
    f32 = lambda a: np.ascontiguousarray(np.asarray(a, dtype=np.float32))
    x = f32(inputs["L0_attn_x"]).reshape(S, D)
    b_hidden = f32(inputs["L0_attn_b_hidden"]).reshape(-1)
    nrm = np.zeros((128, 8), dtype=np.float32)
    nrm[:, 0] = f32(inputs["L0_attn_b_qk"]).reshape(-1)
    nrm[:, 1] = f32(inputs["L0_attn_gamma_q"]).reshape(-1)
    nrm[:, 2] = f32(inputs["L0_attn_beta_q"]).reshape(-1)
    nrm[:, 3] = f32(inputs["L0_attn_gamma_k"]).reshape(-1)
    nrm[:, 4] = f32(inputs["L0_attn_beta_k"]).reshape(-1)
    shared = {
        "wqk": f32(inputs["L0_attn_W_qk"]),
        "nrm": np.ascontiguousarray(nrm),
        "bhg": np.ascontiguousarray(b_hidden[HV:].reshape(32, 128).T),
        "bhv": np.ascontiguousarray(b_hidden[:HV].reshape(1, HV)),
        "whid": f32(inputs["L0_attn_W_hidden"]),
        "wout": f32(inputs["L0_attn_W_out"]),
        "bout": f32(inputs["L0_attn_b_out"]).reshape(1, D),
        "gwt": np.ascontiguousarray(f32(inputs["L0_ffn_gate_w"]).T),
        "uwt": np.ascontiguousarray(f32(inputs["L0_ffn_up_w"]).T),
        "dwt": np.ascontiguousarray(f32(inputs["L0_ffn_down_w"]).T),
        "gs": np.ascontiguousarray(f32(inputs["L0_ffn_gate_s"]).reshape(ICN, 128).T),
        "us": np.ascontiguousarray(f32(inputs["L0_ffn_up_s"]).reshape(ICN, 128).T),
        "ds": f32(inputs["L0_ffn_down_s"]).reshape(1, D),
    }
    in_maps = []
    for cid in range(NCORES):
        m = dict(shared)
        m["x"] = np.ascontiguousarray(x[cid * T:(cid + 1) * T, :])
        in_maps.append(m)
    return in_maps


def kernel(**inputs):
    in_maps = make_inputs(inputs)
    nc = build_program(stage=3)
    res = run_bass_kernel_spmd(nc, in_maps, core_ids=list(range(NCORES)))
    y = np.concatenate([np.asarray(r["y"], dtype=np.float32) for r in res.results], axis=0)
    return y.reshape(1, S, D)


# version 06
# speedup vs baseline: 1.0653x; 1.0205x over previous
import sys

for _p in ("/opt/trn_rl_repo",):
    if _p not in sys.path:
        sys.path.insert(0, _p)

import numpy as np
import ml_dtypes
import concourse.bass as bass
import concourse.mybir as mybir
from concourse.bass_utils import run_bass_kernel_spmd

F32 = mybir.dt.float32
BF16 = mybir.dt.bfloat16
AF = mybir.ActivationFunctionType
ALU = mybir.AluOpType
AX = mybir.AxisListType

NCORES = 8
T = 1024
TT = 8
D = 2048
DC = 16
E = 128
HV = 4096
S = 8192
KT = 64
QBN = 2
QW = 512
II = 5632
ICN = 44
EPS = 1e-5

ENGINES = ("tensor", "vector", "scalar", "sync", "gpsimd")


class Plan:

    def __init__(self, nc, stack):
        self.nc = nc
        self.stack = stack
        self.q = {e: [] for e in ENGINES}
        self.sems = {}
        self.cnt = {}
        self.last_wait = {e: {} for e in ENGINES}

    def sem(self, name):
        if name not in self.sems:
            self.sems[name] = self.stack.enter_context(self.nc.semaphore(name))
            self.cnt[name] = 0
        return self.sems[name]

    def bump(self, name, k=1):
        self.sem(name)
        self.cnt[name] += k
        return self.cnt[name]

    def op(self, eng, fn, waits=(), inc=None, inck=1):
        real_waits = []
        lw = self.last_wait[eng]
        for (s, v) in waits:
            assert v is not None, f"None wait on {s}"
            if v <= 0:
                continue
            assert v <= self.cnt.get(s, 0), f"wait on future {s}={v} (have {self.cnt.get(s, 0)})"
            if lw.get(s, 0) >= v:
                continue
            lw[s] = v
            real_waits.append((self.sem(s), v))
        incsem = self.sem(inc) if inc else None
        target = self.bump(inc, inck) if inc else None

        def run(e, _w=tuple(real_waits), _fn=fn, _is=incsem, _k=inck):
            for (sh, v) in _w:
                e.wait_ge(sh, v)
            inst = _fn(e)
            if _is is not None:
                inst.then_inc(_is, _k)

        self.q[eng].append(run)
        return target

    def dma(self, eng, out, in_, semname, waits=(), transpose=False):
        def fn(e, _o=out, _i=in_, _t=transpose):
            if _t:
                return e.dma_start(out=_o, in_=_i, transpose=True)
            return e.dma_start(out=_o, in_=_i)

        return self.op(eng, fn, waits=waits, inc=semname, inck=16)


def build_program(stage=3, dbg=False):
    nc = bass.Bass("TRN2", target_bir_lowering=False, debug=False)
    import contextlib

    stack = contextlib.ExitStack()

    ap = lambda h: h.ap()
    x_in = ap(nc.dram_tensor("x", [T, D], F32, kind="ExternalInput"))
    wqk_in = ap(nc.dram_tensor("wqk", [D, E], F32, kind="ExternalInput"))
    nrm_in = ap(nc.dram_tensor("nrm", [128, 8], F32, kind="ExternalInput"))
    bhg_in = ap(nc.dram_tensor("bhg", [128, 32], F32, kind="ExternalInput"))
    bhv_in = ap(nc.dram_tensor("bhv", [1, HV], F32, kind="ExternalInput"))
    whid_in = ap(nc.dram_tensor("whid", [D, 2 * HV], F32, kind="ExternalInput"))
    wout_in = ap(nc.dram_tensor("wout", [HV, D], F32, kind="ExternalInput"))
    bout_in = ap(nc.dram_tensor("bout", [1, D], F32, kind="ExternalInput"))
    gwt_in = ap(nc.dram_tensor("gwt", [D, II], F32, kind="ExternalInput"))
    uwt_in = ap(nc.dram_tensor("uwt", [D, II], F32, kind="ExternalInput"))
    dwt_in = ap(nc.dram_tensor("dwt", [II, D], F32, kind="ExternalInput"))
    gs_in = ap(nc.dram_tensor("gs", [128, ICN], F32, kind="ExternalInput"))
    us_in = ap(nc.dram_tensor("us", [128, ICN], F32, kind="ExternalInput"))
    ds_in = ap(nc.dram_tensor("ds", [1, D], F32, kind="ExternalInput"))

    y_out = ap(nc.dram_tensor("y", [T, D], F32, kind="ExternalOutput")) if stage >= 3 else None

    whid_bf = ap(nc.dram_tensor("whid_bf", [D, HV], BF16, kind="Internal"))
    wout_bf = ap(nc.dram_tensor("wout_bf", [HV, D], BF16, kind="Internal"))
    dwt_bf = ap(nc.dram_tensor("dwt_bf", [II, D], BF16, kind="Internal"))
    k_cin = ap(nc.dram_tensor("k_cin", [128, T], BF16, kind="Internal"))
    k_call = ap(nc.dram_tensor("k_call", [NCORES * 128, T], BF16, kind="Internal"))
    v_cin = [ap(nc.dram_tensor(f"v_cin{c}", [T, 1024], BF16, kind="Internal")) for c in range(4)]
    v_call = [ap(nc.dram_tensor(f"v_call{c}", [S, 1024], BF16, kind="Internal")) for c in range(4)]
    ln2_dram = ap(nc.dram_tensor("ln2_dram", [T, D], BF16, kind="Internal"))
    x2_dram = ap(nc.dram_tensor("x2_dram", [T, D], F32, kind="Internal"))

    A = lambda name, shape, dt: nc.alloc_sbuf_tensor(name, shape, dt).ap()
    ident = A("ident", [128, 128], F32)
    stats = A("stats", [128, 96], F32)
    acst = A("acst", [128, 4], F32)
    nrm_pp = A("nrm_pp", [128, 8], F32)
    bhid_g = A("bhid_g", [128, 32], F32)
    gs_pp = A("gs_pp", [128, ICN], F32)
    us_pp = A("us_pp", [128, ICN], F32)
    ones_s = A("ones_s", [1, 128], F32)
    wqk_b = A("wqk_b", [128, DC * 128], BF16)
    q_bf = A("q_bf", [128, T], BF16)
    k_own = A("k_own", [128, T], BF16)

    def region(name, nbytes):
        h = nc.alloc_sbuf_tensor(name, [128, nbytes], mybir.dt.uint8)
        return nc.lookup_mloc(h).addr

    def at(name, base, off, shape, dt, transpose_dest=False):
        return nc.alloc_sbuf_tensor_at(name, shape, dt, offset=base + off).ap()

    m8 = region("M8", 8192)
    ln1_tm = at("ln1_tm", m8, 0, [128, D], F32)
    qk_f32 = at("qk_f32", m8, 0, [128, T], F32)
    simscr = at("simscr", m8, 4096, [128, 1024], F32)

    x16 = region("X16", 16384)
    x_s = at("x_s", x16, 0, [128, 2 * D], F32)
    vst = at("vst", x16, 0, [128, 2 * 4096], BF16)
    x2h = at("x2h", x16, 0, [128, 2 * 1024], F32)

    s8 = region("S8", 8192)
    sqscr = at("sqscr", s8, 0, [128, D], F32)
    gate_f = at("gate_f", s8, 0, [128, 4 * 512], F32)
    t1_s = at("t1_s", s8, 0, [128, D], F32)
    gs_scr = at("gs_scr", s8, 0, [128, 2 * 1024], F32)

    r1 = region("R1", 32768)
    ln1_fm = at("ln1_fm", r1, 0, [128, DC * 1024], BF16)
    ln2_fm = at("ln2_fm", r1, 0, [128, DC * 1024], BF16)
    ds_row = at("ds_row", r1, 0, [1, D], F32)
    ds_bc = at("ds_bc", r1, 8192, [128, D], F32)

    r2 = region("R2", 16384)
    k_full = at("k_full", r2, 0, [128, S], BF16)
    gw_sb = at("gw_sb", r2, 0, [128, 2 * 4096], BF16)
    dw_sb = at("dw_sb", r2, 0, [128, 2 * 4096], BF16)

    r3 = region("R3", 16384)
    wg_sb = at("wg_sb", r3, 0, [128, DC * 512], BF16)
    bhv_row = at("bhv_row", r3, 0, [1, HV], F32)
    uw_sb = at("uw_sb", r3, 0, [128, 2 * 4096], BF16)
    out_sb = at("out_sb", r3, 0, [128, 2 * 1024], F32)

    ra = region("RA", 98304)
    v_sb = at("v_sb", ra, 0, [128, 2 * 1024], BF16)
    bhv_bc = at("bhv_bc", ra, 4096, [128, HV], F32)
    wv_sb = at("wv_sb", ra, 20480, [128, 2 * 16384], BF16)
    attnT = at("attnT", ra, 0, [128, KT * 512], BF16)
    om = at("om", ra, 65536, [128, 32 * 512], BF16)
    x2_sb = at("x2_sb", ra, 0, [128, 2 * D], F32)
    ln2_tm = at("ln2_tm", ra, 16384, [128, 2 * D], BF16)
    wo_sb = at("wo_sb", ra, 24576, [128, 2 * D], BF16)
    bout_bc = at("bout_bc", ra, 32768, [128, D], F32)
    bout_row = at("bout_row", ra, 40960, [1, D], F32)
    hdn = at("hdn", ra, 0, [128, ICN * 1024], BF16)

    PS_A = nc.alloc_psum_tensor("PS_A", [128, 2048], F32).ap()
    PS_B = nc.alloc_psum_tensor("PS_B", [128, 2048], F32).ap()

    def bankap(i, width=512):
        t = PS_A if i < 4 else PS_B
        return t[:, (i % 4) * 512:(i % 4) * 512 + width]

    p = Plan(nc, stack)
    pg = {}

    def bank_waits(banks):
        w = {}
        for b in banks:
            if b in pg:
                s, v = pg[b]
                w[s] = max(w.get(s, 0), v)
        return list(w.items())

    def free_banks(banks, semname, v):
        for b in banks:
            pg[b] = (semname, v)

    p.op("gpsimd", lambda g: g.memset(acst[:, 0:1], EPS))
    p.op("gpsimd", lambda g: g.memset(acst[:, 1:2], -0.5))
    p.op("gpsimd", lambda g: g.memset(acst[:, 2:3], 1.0 / S))
    p.op("gpsimd", lambda g: g.memset(ones_s, 1.0))
    p.op("gpsimd", lambda g: g.memset(ident, 0.0))
    sg_id = p.op(
        "gpsimd",
        lambda g: g.affine_select(
            out=ident, in_=ident, compare_op=ALU.not_equal, fill=1.0,
            base=0, pattern=[[-1, 128]], channel_multiplier=1),
        inc="sg")
    sw_wqk = p.dma(
        "gpsimd",
        wqk_b.rearrange("q (c e) -> q c e", c=DC),
        wqk_in.rearrange("(c q) e -> q c e", q=128),
        "swqk")
    scst = None
    for (dst, src) in ((nrm_pp, nrm_in), (bhid_g, bhg_in), (gs_pp, gs_in),
                       (us_pp, us_in)):
        scst = p.dma("sync", dst, src, "scst")
    sbhr = p.dma("sync", bhv_row, bhv_in, "sbhr")

    wv3 = wv_sb.rearrange("q (b dc h) -> q b dc h", b=2, dc=16)
    swv = {0: 0, 1: 0}
    for c0 in range(2):
        for half in range(2):
            swv[c0] = p.dma(
                "gpsimd",
                wv3[:, c0, half * 8:(half + 1) * 8, :],
                whid_in[half * 1024:(half + 1) * 1024,
                        c0 * 1024:(c0 + 1) * 1024].rearrange("(dc q) h -> q dc h", q=128),
                f"swv{c0}")
    for j in range(2):
        p.dma("gpsimd", whid_bf[:, j * 2048:(j + 1) * 2048],
              whid_in[:, HV + j * 2048:HV + (j + 1) * 2048], f"scwb{2 + j}")

    st_setup = None
    for j in range(8):
        st_setup = p.op(
            "tensor",
            lambda tn, _j=j: tn.matmul(
                bankap(_j % 4) if _j < 4 else bankap(4 + _j % 4),
                ones_s, bhv_row[:, _j * 512:(_j + 1) * 512],
                start=True, stop=True),
            waits=[("sbhr", sbhr)] if j == 0 else (),
            inc="st" if j == 7 else None)
    sv_setup = p.op("vector", lambda v: v.tensor_copy(bhv_bc[:, 0:2048], PS_A),
                    waits=[("st", st_setup)])
    sv_setup = p.op("vector", lambda v: v.tensor_copy(bhv_bc[:, 2048:4096], PS_B),
                    inc="sv")
    free_banks(list(range(8)), "sv", sv_setup)

    sxr_cnt = {0: 0, 1: 0}

    def xs_tile(par):
        return x_s[:, par * D:(par + 1) * D]

    st_tr = [[None] * 4 for _ in range(TT)]
    sv_evac = [[None] * 4 for _ in range(TT)]
    sv_nrm = [None] * TT
    sa_rstd = [None] * TT
    ln1_fm3 = ln1_fm.rearrange("q (dc tok) -> q dc tok", dc=DC)
    c3 = None
    for t in range(TT):
        par = t % 2
        waits = []
        if t >= 2:
            waits.append(("sv", sv_nrm[t - 2]))
        sxr_cnt[par] = p.dma("sync", xs_tile(par), x_in[t * 128:(t + 1) * 128, :],
                             f"sxr{par}", waits=waits)
        w = [(f"sxr{par}", sxr_cnt[par])]
        sa_sum = p.op(
            "scalar",
            lambda sc, _p=par, _t=t: sc.activation(
                sqscr, xs_tile(_p), AF.Identity, accum_out=stats[:, _t:_t + 1]),
            waits=w, inc="sa")
        c_ssq = p.op(
            "vector",
            lambda v, _p=par, _t=t: v.scalar_tensor_tensor(
                sqscr, xs_tile(_p), 1.0, xs_tile(_p), ALU.mult, ALU.mult,
                accum_out=stats[:, 8 + _t:9 + _t]),
            waits=w, inc="sv")
        p.op("vector", lambda v, _t=t: v.tensor_scalar(
            stats[:, 16 + _t:17 + _t], stats[:, _t:_t + 1], 1.0 / D, None, ALU.mult),
            waits=[("sa", sa_sum)])
        c_m2 = p.op("vector", lambda v, _t=t: v.scalar_tensor_tensor(
            stats[:, 48:49], stats[:, _t:_t + 1], 1.0 / (D * D), stats[:, _t:_t + 1],
            ALU.mult, ALU.mult), inc="sv")
        wv_ = [("sv", c_m2)]
        if t >= 1:
            wv_.append(("sa", a1_prev))
        c_var = p.op("vector", lambda v, _t=t: v.scalar_tensor_tensor(
            stats[:, 49:50], stats[:, 8 + _t:9 + _t], 1.0 / D, stats[:, 48:49],
            ALU.mult, ALU.subtract), waits=wv_, inc="sv")
        a1 = p.op("scalar", lambda sc, _t=t: sc.activation(
            stats[:, 50:51], stats[:, 49:50], AF.Ln, bias=acst[:, 0:1]),
            waits=[("sv", c_var), ("sg", sg_id)], inc="sa")
        a1_prev = a1
        sa_rstd[t] = p.op("scalar", lambda sc, _t=t: sc.activation(
            stats[:, 24 + _t:25 + _t], stats[:, 50:51], AF.Exp, scale=acst[:, 1:2]),
            waits=[("sa", a1)], inc="sa")
        p.op("vector", lambda v, _t=t: v.scalar_tensor_tensor(
            stats[:, 32 + _t:33 + _t], stats[:, 16 + _t:17 + _t], -1.0,
            stats[:, 24 + _t:25 + _t], ALU.mult, ALU.mult),
            waits=[("sa", sa_rstd[t])])
        c_pre = p.op("vector", lambda v, _t=t: v.tensor_scalar(
            stats[:, 40 + _t:41 + _t], stats[:, 24 + _t:25 + _t], 1.0, None, ALU.add),
            inc="sv")
        w = [("sv", c_pre)]
        if t >= 1:
            w.append(("st", st_tr[t - 1][3]))
        sv_nrm[t] = p.op(
            "vector",
            lambda v, _p=par, _t=t: v.tensor_scalar(
                ln1_tm, xs_tile(_p), stats[:, 24 + _t:25 + _t],
                stats[:, 32 + _t:33 + _t], ALU.mult, ALU.add),
            waits=w, inc="sv")
        c3 = sv_nrm[t]
        for g in range(4):
            gpar = (t * 4 + g) % 2
            psv = PS_B[:, 1024 + gpar * 512:1024 + gpar * 512 + 512]
            w = [("sv", sv_nrm[t]), ("sg", sg_id)]
            prev = (t * 4 + g) - 2
            if prev >= 0:
                w.append(("sv", sv_evac[prev // 4][prev % 4]))
            for i in range(4):
                inc = "st" if i == 3 else None
                r = p.op(
                    "tensor",
                    lambda tn, _i=i, _g=g, _t=t, _ps=psv: tn.transpose(
                        _ps[:, _i * 128:(_i + 1) * 128],
                        ln1_tm[:, (_g * 4 + _i) * 128:(_g * 4 + _i + 1) * 128],
                        ident),
                    waits=w if i == 0 else (), inc=inc)
            st_tr[t][g] = r
            sv_evac[t][g] = p.op(
                "vector",
                lambda v, _g=g, _t=t, _ps=psv: v.tensor_copy(
                    ln1_fm3[:, _g * 4:(_g + 1) * 4, _t * 128:(_t + 1) * 128],
                    _ps.rearrange("q (i f) -> q i f", i=4)),
                waits=[("st", st_tr[t][g])], inc="sv")

    pg[6] = ("sv", sv_evac[TT - 1][2])
    pg[7] = ("sv", sv_evac[TT - 1][3])

    qk_banks = [0, 1]
    w = [("swqk", sw_wqk), ("sv", sv_evac[TT - 1][3]), ("scst", scst)] + bank_waits(qk_banks)
    st_qk = None
    for dc in range(DC):
        for s in range(2):
            inc = "st" if (dc == DC - 1 and s == 1) else None
            st_qk = p.op(
                "tensor",
                lambda tn, _dc=dc, _s=s: tn.matmul(
                    PS_A[:, _s * 512:(_s + 1) * 512],
                    wqk_b[:, _dc * 128:(_dc + 1) * 128],
                    ln1_fm[:, _dc * 1024 + _s * 512:_dc * 1024 + (_s + 1) * 512],
                    start=(_dc == 0), stop=(_dc == DC - 1)),
                waits=w if (dc == 0 and s == 0) else (), inc=inc) or st_qk
    sa_qk = p.op("scalar", lambda s: s.activation(
        qk_f32, PS_A[:, 0:1024], AF.Silu, bias=nrm_pp[:, 0:1]),
        waits=[("st", st_qk)], inc="sa")
    free_banks(qk_banks, "sa", sa_qk)
    p.op("vector", lambda v: v.tensor_scalar(
        q_bf, qk_f32, nrm_pp[:, 1:2], nrm_pp[:, 2:3], ALU.mult, ALU.add),
        waits=[("sa", sa_qk)])
    sv_qk = p.op("vector", lambda v: v.tensor_scalar(
        k_own, qk_f32, nrm_pp[:, 3:4], nrm_pp[:, 4:5], ALU.mult, ALU.add), inc="sv")
    skw = p.dma("scalar", k_cin, k_own, "skw", waits=[("sv", sv_qk)])
    sck = p.op(
        "gpsimd",
        lambda g: g.collective_compute(
            "AllGather", ALU.bypass, replica_groups=[list(range(NCORES))],
            ins=[k_cin.opt()], outs=[k_call.opt()]),
        waits=[("skw", skw)], inc="sck")
    skf = None

    st_vmm_c = [None] * 4
    sa_vsl = {}
    svw_cnt = [0] * 4
    scv = [None] * 4
    sv_sb2 = v_sb.rearrange("q (b f) -> q b f", b=2)

    nvs = 0
    sa_sil = {}
    for c in range(4):
        par = c % 2
        for t in range(TT):
            tpar = t % 2
            for sl_ in range(2):
                banks = [tpar * 2 + sl_]
                w = [(f"swv{par}", swv[par])] + bank_waits(banks)
                ps = bankap(banks[0])
                for dc in range(DC):
                    inc = "st" if dc == DC - 1 else None
                    st_v = p.op(
                        "tensor",
                        lambda tn, _dc=dc, _t=t, _s=sl_, _c=c, _par=par, _ps=ps: tn.matmul(
                            _ps,
                            ln1_fm[:, _dc * 1024 + _t * 128:_dc * 1024 + (_t + 1) * 128],
                            wv_sb[:, _par * 16384 + (_dc // 8) * 8192 + (_dc % 8) * 1024 + _s * 512:
                                  _par * 16384 + (_dc // 8) * 8192 + (_dc % 8) * 1024 + (_s + 1) * 512],
                            start=(_dc == 0), stop=(_dc == DC - 1)),
                        waits=w if dc == 0 else (), inc=inc)
                vpar = nvs % 2
                w2 = [("st", st_v), ("sv", sv_setup)]
                if nvs >= 2:
                    w2.append(("sa", sa_sil[nvs - 2]))
                sv_vb = p.op(
                    "vector",
                    lambda v, _vp=vpar, _c=c, _s=sl_, _ps=ps: v.tensor_tensor(
                        simscr[:, _vp * 512:(_vp + 1) * 512], _ps,
                        bhv_bc[:, _c * 1024 + _s * 512:_c * 1024 + (_s + 1) * 512],
                        ALU.add),
                    waits=w2, inc="sv")
                free_banks(banks, "sv", sv_vb)
                w2 = [("sv", sv_vb)]
                if t >= 2:
                    w2.append((f"svw{c}p{tpar}", ((t - 2) // 2 + 1) * 16))
                sa_v = p.op(
                    "scalar",
                    lambda sc, _vp=vpar, _tp=tpar, _s=sl_: sc.activation(
                        sv_sb2[:, _tp, _s * 512:(_s + 1) * 512],
                        simscr[:, _vp * 512:(_vp + 1) * 512], AF.Silu),
                    waits=w2, inc="sa")
                sa_sil[nvs] = sa_v
                sa_vsl[(c, t, sl_)] = sa_v
                nvs += 1
            svw_cnt[c] = p.dma(
                "sync", v_cin[c][t * 128:(t + 1) * 128, :], sv_sb2[:, tpar, :],
                f"svw{c}p{tpar}", waits=[("sa", sa_vsl[(c, t, 1)])])
        st_vmm_c[c] = st_v
        if c == 1:
            for r_ in range(NCORES):
                skf = p.dma("scalar", k_full[:, r_ * T:(r_ + 1) * T],
                            k_call[r_ * 128:(r_ + 1) * 128, :], "skf",
                            waits=[("sck", sck)] if r_ == 0 else ())
        if c + 2 <= 3:
            cn = c + 2
            for half in range(2):
                swv[cn % 2] = p.dma(
                    "gpsimd",
                    wv3[:, cn % 2, half * 8:(half + 1) * 8, :],
                    whid_in[half * 1024:(half + 1) * 1024,
                            cn * 1024:(cn + 1) * 1024].rearrange("(dc q) h -> q dc h", q=128),
                    f"swv{cn % 2}",
                    waits=[("st", st_vmm_c[c])] if half == 0 else ())
        scv[c] = p.op(
            "gpsimd",
            lambda g, _c=c: g.collective_compute(
                "AllGather", ALU.bypass, replica_groups=[list(range(NCORES))],
                ins=[v_cin[_c].opt()], outs=[v_call[_c].opt()]),
            waits=[(f"svw{c}p0", 64), (f"svw{c}p1", 64)], inc=f"scv{c}")

    scwo = None
    for j in range(4):
        scwo = p.dma("gpsimd", wout_bf[j * 1024:(j + 1) * 1024, :],
                     wout_in[j * 1024:(j + 1) * 1024, :], "scwo",
                     waits=[("scv3", 1)] if j == 0 else ())
    scdw = None
    for j in range(4):
        scdw = p.dma("gpsimd", dwt_bf[j * 1408:(j + 1) * 1408, :],
                     dwt_in[j * 1408:(j + 1) * 1408, :], "scdw")

    swg_cnt = p.dma(
        "sync", wg_sb.rearrange("q (dc h) -> q dc h", dc=DC),
        whid_bf[:, 0:512].rearrange("(dc q) h -> q dc h", q=128),
        "swg", waits=[("scwb2", 16), ("scwb3", 16), ("st", st_setup)])

    if stage < 1:
        raise ValueError
    if stage == 1:
        _emit_debug_stage1(p, nc, dbg, locals())
        return _finalize(nc, p, stack)


    vst3 = vst.rearrange("q (b kt h) -> q b kt h", b=2, kt=8)
    svst = {0: 0, 1: 0}
    st_avm = {}
    sv_om_end = {}
    sa_gsl = {}
    st_gmm = {}
    swo = {0: 0, 1: 0}
    st_womm = {}
    sv_x2a = {}
    sv_l2n = {}
    sx2w_cnt = 0
    sl2w_cnt = 0
    sa_t1 = {}
    vst_free = {}
    st_simm_last = None

    for qb in range(QBN):
        sv_relu = [None] * KT
        sa_sqr = [None] * KT
        for kt in range(KT):
            b = kt % 4
            w = [("skf", skf)] + bank_waits([b])
            st_s = p.op(
                "tensor",
                lambda tn, _kt=kt, _qb=qb, _b=b: tn.matmul(
                    bankap(_b), k_full[:, _kt * 128:(_kt + 1) * 128],
                    q_bf[:, _qb * QW:(_qb + 1) * QW],
                    start=True, stop=True),
                waits=w, inc="st")
            st_simm_last = st_s
            w = [("st", st_s)]
            if kt >= 2:
                w.append(("sa", sa_sqr[kt - 2]))
            sv_relu[kt] = p.op(
                "vector",
                lambda v, _kt=kt, _b=b: v.tensor_scalar(
                    simscr[:, (_kt % 2) * 512:(_kt % 2) * 512 + 512],
                    bankap(_b), 0.0, None, ALU.max),
                waits=w, inc="sv")
            pg[b] = ("sv", sv_relu[kt])
            w = [("sv", sv_relu[kt])]
            if kt == 0:
                if qb == 0:
                    w += [(f"svw{c}p{pp}", 64) for c in range(4) for pp in range(2)]
                else:
                    w += [("sx2w0", 32), ("sx2w1", 32), ("sl2w0", 32), ("sl2w1", 32)]
            sa_sqr[kt] = p.op(
                "scalar",
                lambda sc, _kt=kt: sc.activation(
                    attnT[:, _kt * 512:(_kt + 1) * 512],
                    simscr[:, (_kt % 2) * 512:(_kt % 2) * 512 + 512],
                    AF.Square, scale=acst[:, 2:3]),
                waits=w, inc="sa")

        for hg in range(8):
            c, half = hg // 2, hg % 2
            if (qb, hg) != (0, 0):
                prev = (qb, hg - 1) if hg > 0 else (qb - 1, 7)
                w = [("scwb2", 16), ("scwb3", 16), ("st", st_gmm[prev + (3,)])]
                swg_cnt = p.dma(
                    "sync",
                    wg_sb.rearrange("q (dc h) -> q dc h", dc=DC),
                    whid_bf[:, hg * 512:(hg + 1) * 512].rearrange(
                        "(dc q) h -> q dc h", q=128),
                    "swg", waits=w)
            for hc in range(4):
                bnk = 4 + hc
                w = [("swg", swg_cnt)] + bank_waits([bnk])
                for dc in range(DC):
                    inc = "st" if dc == DC - 1 else None
                    r = p.op(
                        "tensor",
                        lambda tn, _dc=dc, _hc=hc, _qb=qb: tn.matmul(
                            PS_B[:, _hc * 512:(_hc + 1) * 512],
                            wg_sb[:, _dc * 512 + _hc * 128:_dc * 512 + (_hc + 1) * 128],
                            ln1_fm[:, _dc * 1024 + _qb * QW:_dc * 1024 + _qb * QW + QW],
                            start=(_dc == 0), stop=(_dc == DC - 1)),
                        waits=w if dc == 0 else (), inc=inc)
                st_gmm[(qb, hg, hc)] = r
                w = [("st", r)]
                if (qb, hg) != (0, 0):
                    prevhg = (qb, hg - 1) if hg > 0 else (qb - 1, 7)
                    w.append(("sv", sv_om_end[prevhg]))
                if qb == 1 and hg == 0:
                    w.append(("sv", sv_l2n[(qb - 1) * 4 + 3]))
                sa_gsl[(qb, hg, hc)] = p.op(
                    "scalar",
                    lambda sc, _hc=hc, _hg=hg: sc.activation(
                        gate_f[:, _hc * 512:(_hc + 1) * 512],
                        PS_B[:, _hc * 512:(_hc + 1) * 512],
                        AF.Silu, bias=bhid_g[:, _hg * 4 + _hc:_hg * 4 + _hc + 1]),
                    waits=w, inc="sa")
                pg[4 + hc] = ("sa", sa_gsl[(qb, hg, hc)])
            for ks in range(8):
                par = ks % 2
                w = [(f"scv{c}", scv[c])]
                if par in vst_free:
                    w.append(("st", vst_free[par]))
                if qb == 0 and hg == 0 and ks < 2:
                    w.append(("sv", sv_nrm[TT - 1]))
                if qb == 1 and hg == 0 and ks < 2:
                    w.append(("sv", sa_t1[(qb - 1) * 4 + 3]))
                svst[par] = p.dma(
                    "sync",
                    vst3[:, par, :, :],
                    v_call[c][ks * 1024:(ks + 1) * 1024,
                              half * 512:(half + 1) * 512].rearrange(
                                  "(kt q) h -> q kt h", q=128),
                    f"svst{par}", waits=w)
                w = [(f"svst{par}", svst[par]), ("sa", sa_sqr[min(ks * 8 + 7, KT - 1)])]
                if ks == 0:
                    w += bank_waits([0, 1, 2, 3])
                first = True
                for kt in range(8):
                    for hc in range(4):
                        inc = "st" if (kt == 7 and hc == 3) else None
                        r = p.op(
                            "tensor",
                            lambda tn, _ks=ks, _kt=kt, _hc=hc, _par=par: tn.matmul(
                                bankap(_hc),
                                vst[:, _par * 4096 + _kt * 512 + _hc * 128:
                                    _par * 4096 + _kt * 512 + (_hc + 1) * 128],
                                attnT[:, (_ks * 8 + _kt) * 512:(_ks * 8 + _kt + 1) * 512],
                                start=(_ks == 0 and _kt == 0), stop=(_ks == 7 and _kt == 7)),
                            waits=w if first else (), inc=inc)
                        first = False
                st_avm[(qb, hg, ks)] = r
                vst_free[par] = r
            last = None
            for hc in range(4):
                w = [("st", st_avm[(qb, hg, 7)]), ("sa", sa_gsl[(qb, hg, hc)])]
                last = p.op(
                    "vector",
                    lambda v, _hc=hc, _hg=hg: v.tensor_tensor(
                        om[:, (_hg * 4 + _hc) * 512:(_hg * 4 + _hc + 1) * 512],
                        bankap(_hc), gate_f[:, _hc * 512:(_hc + 1) * 512], ALU.mult),
                    waits=w, inc="sv")
                pg[hc] = ("sv", last)
            sv_om_end[(qb, hg)] = last

        sbo = p.dma("sync", bout_row, bout_in, f"sbo{qb}",
                    waits=[("st", st_avm[(qb, 7, 7)])])
        st_bo = None
        w = [(f"sbo{qb}", sbo)] + bank_waits([0, 1, 2, 3])
        for sl_ in range(4):
            st_bo = p.op(
                "tensor",
                lambda tn, _sl=sl_: tn.matmul(
                    PS_A[:, _sl * 512:(_sl + 1) * 512], ones_s,
                    bout_row[:, _sl * 512:(_sl + 1) * 512], start=True, stop=True),
                waits=w if sl_ == 0 else (), inc="st" if sl_ == 3 else None)
        sv_bo = p.op("vector", lambda v: v.tensor_copy(bout_bc, PS_A),
                     waits=[("st", st_bo)], inc="sv")
        free_banks([0, 1, 2, 3], "sv", sv_bo)
        for qtp in range(2):
            qts = [qb * 4 + qtp * 2, qb * 4 + qtp * 2 + 1]
            first_w = bank_waits(list(range(8)))
            for hc in range(32):
                wpar = hc % 2
                w = [("scwo", 64)]
                key = (qb, qtp, hc - 2)
                if (qb, qtp) != (0, 0) or hc >= 2:
                    if hc >= 2:
                        w.append(("st", st_womm.get(("mm", qb, qtp, hc - 2), 0)))
                    else:
                        prev = (qb, qtp - 1) if qtp > 0 else (qb - 1, 1)
                        w.append(("st", st_womm.get(("mm", prev[0], prev[1], 30 + hc), 0)))
                if qtp == 0 and hc < 2:
                    w.append(("st", st_avm[(qb, 7, 7)]))
                swo[wpar] = p.dma(
                    "sync",
                    wo_sb[:, wpar * D:(wpar + 1) * D],
                    wout_bf[hc * 128:(hc + 1) * 128, :],
                    f"swo{wpar}", waits=w)
                w2 = [(f"swo{wpar}", swo[wpar])]
                if hc == 0:
                    w2 += first_w
                firstmm = True
                for j in range(2):
                    pst = PS_A if j == 0 else PS_B
                    for sl in range(4):
                        r = p.op(
                            "tensor",
                            lambda tn, _hc=hc, _j=j, _sl=sl, _qtp=qtp, _qb=qb, _wp=wpar, _ps=pst: tn.matmul(
                                _ps[:, _sl * 512:(_sl + 1) * 512],
                                om[:, _hc * 512 + (_qtp * 2 + _j) * 128:
                                   _hc * 512 + (_qtp * 2 + _j) * 128 + 128],
                                wo_sb[:, _wp * D + _sl * 512:_wp * D + (_sl + 1) * 512],
                                start=(_hc == 0), stop=(_hc == 31)),
                            waits=w2 if firstmm else (), inc="st" if (j == 1 and sl == 3) else None)
                        firstmm = False
                st_womm[("mm", qb, qtp, hc)] = r
            for j in range(2):
                st_womm[qts[j]] = st_womm[("mm", qb, qtp, 31)]
            for j in range(2):
                qt = qts[j]
                par = qt % 2
                pst = PS_A if j == 0 else PS_B
                w = [("st", st_avm[(qb, 7, 7)])]
                if qt >= 2:
                    w.append(("sv", sa_t1[qt - 2]))
                sxr_cnt[par] = p.dma("sync", xs_tile(par),
                                     x_in[(qt % 4) * 128 + qb * QW:(qt % 4) * 128 + qb * QW + 128, :],
                                     f"sxr{par}", waits=w)
                w = [(f"sxr{par}", sxr_cnt[par]), ("sv", c3)]
                if qb == 0 and qtp == 0 and j == 0:
                    w.append(("sv", sv_om_end[(qb, 7)]))
                sa_t1[qt] = p.op(
                    "vector",
                    lambda v, _p=par, _qt=qt: v.tensor_scalar(
                        t1_s, xs_tile(_p), stats[:, 40 + _qt:41 + _qt],
                        stats[:, 32 + _qt:33 + _qt], ALU.mult, ALU.add),
                    waits=w, inc="sv")
                w = [("st", st_womm[qt]), ("sv", sa_t1[qt])]
                if qt >= 2:
                    w.append((f"sx2w{qt % 2}", 16 * ((qt - 2) // 2 + 1)))
                p.op(
                    "vector",
                    lambda v, _p=par, _ps=pst: v.tensor_tensor(
                        x2_sb[:, _p * D:(_p + 1) * D], _ps, bout_bc, ALU.add),
                    waits=w + [("sv", sv_bo)])
                sv_x2a[qt] = p.op(
                    "vector",
                    lambda v, _p=par: v.tensor_tensor(
                        x2_sb[:, _p * D:(_p + 1) * D], x2_sb[:, _p * D:(_p + 1) * D],
                        t1_s, ALU.add),
                    inc="sv")
                free_banks(list(range(4)) if j == 0 else [4, 5, 6, 7], "sv", sv_x2a[qt])
                p.op("vector", lambda v, _p=par, _qt=qt: v.reduce_sum(
                    stats[:, 64 + _qt:65 + _qt], x2_sb[:, _p * D:(_p + 1) * D], AX.X))
                s1 = p.op("vector", lambda v, _p=par, _qt=qt: v.scalar_tensor_tensor(
                    sqscr, x2_sb[:, _p * D:(_p + 1) * D], 1.0,
                    x2_sb[:, _p * D:(_p + 1) * D], ALU.mult, ALU.mult,
                    accum_out=stats[:, 72 + _qt:73 + _qt]), inc="sv")
                p.op("vector", lambda v, _qt=qt: v.tensor_scalar(
                    stats[:, 88:89], stats[:, 64 + _qt:65 + _qt], 1.0 / D, None, ALU.mult),
                    waits=[("sv", s1)])
                s2 = p.op("vector", lambda v, _qt=qt: v.scalar_tensor_tensor(
                    stats[:, 89:90], stats[:, 64 + _qt:65 + _qt], 1.0 / (D * D),
                    stats[:, 64 + _qt:65 + _qt], ALU.mult, ALU.mult), inc="sv")
                s3 = p.op("vector", lambda v, _qt=qt: v.scalar_tensor_tensor(
                    stats[:, 90:91], stats[:, 72 + _qt:73 + _qt], 1.0 / D,
                    stats[:, 89:90], ALU.mult, ALU.subtract),
                    waits=[("sv", s2)], inc="sv")
                a3 = p.op("scalar", lambda s: s.activation(
                    stats[:, 91:92], stats[:, 90:91], AF.Ln, bias=acst[:, 0:1]),
                    waits=[("sv", s3)], inc="sa")
                a4 = p.op("scalar", lambda s, _qt=qt: s.activation(
                    stats[:, 80 + _qt:81 + _qt], stats[:, 91:92], AF.Exp,
                    scale=acst[:, 1:2]), waits=[("sa", a3)], inc="sa")
                s4 = p.op("vector", lambda v, _qt=qt: v.scalar_tensor_tensor(
                    stats[:, 92:93], stats[:, 88:89], -1.0,
                    stats[:, 80 + _qt:81 + _qt], ALU.mult, ALU.mult),
                    waits=[("sa", a4)], inc="sv")
                w = [("sv", s4)]
                if qt >= 2:
                    w.append((f"sl2w{qt % 2}", 16 * ((qt - 2) // 2 + 1)))
                sv_l2n[qt] = p.op(
                    "vector",
                    lambda v, _p=par, _qt=qt: v.tensor_scalar(
                        ln2_tm[:, _p * D:(_p + 1) * D], x2_sb[:, _p * D:(_p + 1) * D],
                        stats[:, 80 + _qt:81 + _qt], stats[:, 92:93],
                        ALU.mult, ALU.add),
                    waits=w, inc="sv")
                sx2w_cnt = p.dma("scalar", x2_dram[qt * 128:(qt + 1) * 128, :],
                                 x2_sb[:, par * D:(par + 1) * D], f"sx2w{qt % 2}",
                                 waits=[("sv", sv_x2a[qt])])
                sl2w_cnt = p.dma("scalar", ln2_dram[qt * 128:(qt + 1) * 128, :],
                                 ln2_tm[:, par * D:(par + 1) * D], f"sl2w{qt % 2}",
                                 waits=[("sv", sv_l2n[qt])])

    if stage == 2:
        _emit_debug_stage2(p, nc, dbg, locals())
        return _finalize(nc, p, stack)

    w = [("sl2w0", 64), ("sl2w1", 64), ("st", st_gmm[(1, 7, 3)])]
    for dc in range(DC):
        p.dma("sync", ln2_fm[:, dc * 1024:(dc + 1) * 1024],
              ln2_dram[:, dc * 128:(dc + 1) * 128], f"sl2t{dc}",
              waits=w if dc == 0 else (), transpose=True)

    gw3 = gw_sb.rearrange("q (b dc i) -> q b dc i", b=2, dc=DC)
    uw3 = uw_sb.rearrange("q (b dc i) -> q b dc i", b=2, dc=DC)
    sfg = {0: 0, 1: 0}
    sfu = {0: 0, 1: 0}
    st_gumm = [None] * ICN
    sa_gs2 = [None] * ICN
    sv_hdn = [None] * ICN
    npairs = ICN // 2
    for pr in range(npairs):
        par = pr % 2
        w = []
        if pr >= 2:
            w.append(("st", st_gumm[(pr - 2) * 2 + 1]))
        if pr < 2:
            w.append(("st", st_simm_last))
            w.append(("st", st_womm[7]))
        sfg[par] = p.dma(
            "gpsimd", gw3[:, par, :, :],
            gwt_in[:, pr * 256:(pr + 1) * 256].rearrange("(dc q) i -> q dc i", q=128),
            f"sfg{par}", waits=w)
        sfu[par] = p.dma(
            "gpsimd", uw3[:, par, :, :],
            uwt_in[:, pr * 256:(pr + 1) * 256].rearrange("(dc q) i -> q dc i", q=128),
            f"sfu{par}")
        for ic in (2 * pr, 2 * pr + 1):
            icp = ic % 2
            psg = (PS_A if icp == 0 else PS_B)[:, 0:1024]
            psu = (PS_A if icp == 0 else PS_B)[:, 1024:2048]
            banks = [0, 1, 2, 3] if icp == 0 else [4, 5, 6, 7]
            w = [(f"sfg{par}", sfg[par]), (f"sfu{par}", sfu[par])] + bank_waits(banks)
            first = True
            for dc in range(DC):
                for sl_ in range(2):
                    wdc = []
                    if ic == 0 and sl_ == 0:
                        wdc = [(f"sl2t{dc}", 16)]
                    p.op(
                        "tensor",
                        lambda tn, _dc=dc, _s=sl_, _ic=ic, _par=par, _ps=psg: tn.matmul(
                            _ps[:, _s * 512:(_s + 1) * 512],
                            gw_sb[:, _par * 4096 + _dc * 256 + (_ic % 2) * 128:
                                  _par * 4096 + _dc * 256 + (_ic % 2) * 128 + 128],
                            ln2_fm[:, _dc * 1024 + _s * 512:_dc * 1024 + (_s + 1) * 512],
                            start=(_dc == 0), stop=(_dc == DC - 1)),
                        waits=(w + wdc) if first else wdc)
                    first = False
                    inc = "st" if (dc == DC - 1 and sl_ == 1) else None
                    r = p.op(
                        "tensor",
                        lambda tn, _dc=dc, _s=sl_, _ic=ic, _par=par, _ps=psu: tn.matmul(
                            _ps[:, _s * 512:(_s + 1) * 512],
                            uw_sb[:, _par * 4096 + _dc * 256 + (_ic % 2) * 128:
                                  _par * 4096 + _dc * 256 + (_ic % 2) * 128 + 128],
                            ln2_fm[:, _dc * 1024 + _s * 512:_dc * 1024 + (_s + 1) * 512],
                            start=(_dc == 0), stop=(_dc == DC - 1)),
                        inc=inc)
            st_gumm[ic] = r
            w = [("st", st_gumm[ic])]
            if ic >= 2:
                w.append(("sv", sv_hdn[ic - 2]))
            if ic == 0:
                w.append(("sv", sv_l2n[7]))
            sa_gs2[ic] = p.op(
                "scalar",
                lambda sc, _ic=ic, _ps=psg: sc.activation(
                    gs_scr[:, (_ic % 2) * 1024:(_ic % 2) * 1024 + 1024], _ps,
                    AF.Silu, scale=gs_pp[:, _ic:_ic + 1]),
                waits=w, inc="sa")
            w = [("sa", sa_gs2[ic])]
            if ic == 0:
                w += [("sx2w0", 64), ("sx2w1", 64), ("sl2w0", 64), ("sl2w1", 64),
                      ("st", st_womm[7])]
            sv_hdn[ic] = p.op(
                "vector",
                lambda v, _ic=ic, _ps=psu: v.scalar_tensor_tensor(
                    hdn[:, _ic * 1024:(_ic + 1) * 1024], _ps, us_pp[:, _ic:_ic + 1],
                    gs_scr[:, (_ic % 2) * 1024:(_ic % 2) * 1024 + 1024],
                    ALU.mult, ALU.mult),
                waits=w, inc="sv")
            free_banks(banks, "sv", sv_hdn[ic])

    sdsr = p.dma("sync", ds_row, ds_in, "sdsr", waits=[("st", st_gumm[ICN - 1])])
    w = [("sdsr", sdsr)] + bank_waits([0, 1, 2, 3])
    st_dsb = None
    for sl in range(4):
        st_dsb = p.op(
            "tensor",
            lambda tn, _sl=sl: tn.matmul(
                PS_A[:, _sl * 512:(_sl + 1) * 512], ones_s,
                ds_row[:, _sl * 512:(_sl + 1) * 512], start=True, stop=True),
            waits=w if sl == 0 else (), inc="st" if sl == 3 else None)
    sv_dsb = p.op("vector", lambda v: v.tensor_copy(ds_bc, PS_A),
                  waits=[("st", st_dsb)], inc="sv")
    free_banks([0, 1, 2, 3], "sv", sv_dsb)

    dw3 = dw_sb.rearrange("q (b ic d) -> q b ic d", b=2, ic=4)
    sfd = {0: 0, 1: 0}
    sxf = {0: 0, 1: 0}
    syw_cnt = 0
    st_dn = {}
    sv_out = {}
    outp = out_sb.rearrange("q (b f) -> q b f", b=2)
    nouts = 0
    for quad in range(2):
        for dh in range(2):
            gdh = quad * 2 + dh
            for bt in range(11):
                par = (gdh * 11 + bt) % 2
                w = [("scdw", 64)]
                gi = gdh * 11 + bt
                if bt >= 2:
                    w.append(("st", st_dn[("mm", quad, dh, min(ICN - 1, (bt - 2) * 4 + 3))]))
                elif gdh > 0:
                    pq, pd = divmod(gdh - 1, 2)
                    w.append(("st", st_dn[("mm", pq, pd, min(ICN - 1, (bt + 9) * 4 + 3))]))
                if gi < 2:
                    w.append(("sv", sv_hdn[ICN - 1]))
                sfd[par] = p.dma(
                    "sync", dw3[:, par, :, :],
                    dwt_bf[bt * 512:(bt + 1) * 512,
                           dh * 1024:(dh + 1) * 1024].rearrange("(ic q) d -> q ic d", q=128),
                    f"sfd{par}", waits=w)
                for ic in range(bt * 4, min(ICN, (bt + 1) * 4)):
                    w2 = [(f"sfd{par}", sfd[par])]
                    if ic == 0:
                        w2 += bank_waits(list(range(8)))
                    first = True
                    for j in range(4):
                        qt = quad * 4 + j
                        psq = (PS_A if j < 2 else PS_B)[:, (j % 2) * 1024:(j % 2) * 1024 + 1024]
                        for s_ in range(2):
                            inc = "st" if (j == 3 and s_ == 1) else None
                            r = p.op(
                                "tensor",
                                lambda tn, _ic=ic, _j=j, _s=s_, _qt=qt, _par=par, _ps=psq: tn.matmul(
                                    _ps[:, _s * 512:(_s + 1) * 512],
                                    hdn[:, _ic * 1024 + _qt * 128:_ic * 1024 + (_qt + 1) * 128],
                                    dw_sb[:, _par * 4096 + (_ic % 4) * 1024 + _s * 512:
                                          _par * 4096 + (_ic % 4) * 1024 + (_s + 1) * 512],
                                    start=(_ic == 0), stop=(_ic == ICN - 1)),
                                waits=w2 if first else (), inc=inc)
                            first = False
                    st_dn[("mm", quad, dh, ic)] = r
            for j in range(2):
                qt = quad * 4 + j
                opar = (nouts + j) % 2
                w = []
                if nouts + j >= 2:
                    w.append(("sv", sv_out[nouts + j - 2]))
                else:
                    w.append(("sv", sv_l2n[7]))
                sxf[opar] = p.dma(
                    "sync", x2h[:, opar * 1024:(opar + 1) * 1024],
                    x2_dram[qt * 128:(qt + 1) * 128, dh * 1024:(dh + 1) * 1024],
                    f"sxf{opar}", waits=w)
            for j in range(4):
                qt = quad * 4 + j
                psq = (PS_A if j < 2 else PS_B)[:, (j % 2) * 1024:(j % 2) * 1024 + 1024]
                opar = nouts % 2
                if j >= 2:
                    sxf[opar] = p.dma(
                        "sync", x2h[:, opar * 1024:(opar + 1) * 1024],
                        x2_dram[qt * 128:(qt + 1) * 128, dh * 1024:(dh + 1) * 1024],
                        f"sxf{opar}", waits=[("sv", sv_out[nouts - 2])])
                w = [("st", st_dn[("mm", quad, dh, ICN - 1)]), ("sv", sv_dsb)]
                if nouts >= 2:
                    w.append((f"syw{nouts % 2}", 16 * ((nouts - 2) // 2 + 1)))
                p.op(
                    "vector",
                    lambda v, _o=opar, _dh=dh, _ps=psq: v.tensor_tensor(
                        outp[:, _o, :], _ps, ds_bc[:, _dh * 1024:(_dh + 1) * 1024],
                        ALU.mult),
                    waits=w, inc=None)
                sv_out[nouts] = p.op(
                    "vector",
                    lambda v, _o=opar: v.tensor_tensor(
                        outp[:, _o, :], outp[:, _o, :], x2h[:, _o * 1024:(_o + 1) * 1024],
                        ALU.add),
                    waits=[(f"sxf{opar}", sxf[opar])], inc="sv")
                free_banks([0, 1] if j == 0 else [2, 3] if j == 1 else [4, 5] if j == 2 else [6, 7],
                           "sv", sv_out[nouts])
                syw_cnt = p.dma("sync", y_out[qt * 128:(qt + 1) * 128, dh * 1024:(dh + 1) * 1024],
                                outp[:, opar, :], f"syw{nouts % 2}",
                                waits=[("sv", sv_out[nouts])])
                nouts += 1

    p.op("sync", lambda e: None, waits=[("syw0", 128), ("syw1", 128)])
    if dbg:
        d_ln2fm = ap(nc.dram_tensor("d_ln2fm", [128, DC * 1024], BF16, kind="ExternalOutput"))
        d_hdn = ap(nc.dram_tensor("d_hdn", [128, ICN * 1024], BF16, kind="ExternalOutput"))
        d_dsb = ap(nc.dram_tensor("d_dsb", [128, D], F32, kind="ExternalOutput"))
        so = p.dma("sync", d_ln2fm, ln2_fm, "sout", waits=[(f"sl2t{DC-1}", 16)])
        so = p.dma("sync", d_hdn, hdn, "sout", waits=[("sv", sv_hdn[ICN - 1])])
        so = p.dma("sync", d_dsb, ds_bc, "sout", waits=[("sv", sv_dsb)])
        p.op("sync", lambda e: None, waits=[("sout", so)])
    return _finalize(nc, p, stack)


def _emit_debug_stage1(p, nc, dbg, L):
    outs = {}
    for name, shape, dt in (("d_ln1fm", [128, DC * 1024], BF16),
                            ("d_qk", [128, T], F32),
                            ("d_q", [128, T], BF16),
                            ("d_kfull", [128, S], BF16),
                            ("d_v0", [S, 1024], BF16),
                            ("d_v1", [S, 1024], BF16),
                            ("d_v2", [S, 1024], BF16),
                            ("d_v3", [S, 1024], BF16),
                            ("d_vc3", [T, 1024], BF16),
                            ("d_stats", [128, 96], F32)):
        outs[name] = nc.dram_tensor(name, shape, dt, kind="ExternalOutput").ap()
    w = [("sv", L["sv_qk"]), ("sv", L["sv_evac"][TT - 1][3]), ("skf", L["skf"]),
         ("scv0", L["scv"][0]), ("scv1", L["scv"][1]), ("scv2", L["scv"][2]),
         ("scv3", L["scv"][3])]
    so = p.dma("sync", outs["d_ln1fm"], L["ln1_fm"], "sout", waits=w)
    so = p.dma("sync", outs["d_qk"], L["qk_f32"], "sout")
    so = p.dma("sync", outs["d_q"], L["q_bf"], "sout")
    so = p.dma("sync", outs["d_kfull"], L["k_full"], "sout")
    so = p.dma("sync", outs["d_v0"], L["v_call"][0], "sout")
    so = p.dma("sync", outs["d_v1"], L["v_call"][1], "sout")
    so = p.dma("sync", outs["d_v2"], L["v_call"][2], "sout")
    so = p.dma("sync", outs["d_v3"], L["v_call"][3], "sout")
    so = p.dma("sync", outs["d_vc3"], L["v_cin"][3], "sout")
    so = p.dma("sync", outs["d_stats"], L["stats"], "sout")
    p.op("sync", lambda e: None, waits=[("sout", so)])


def _emit_debug_stage2(p, nc, dbg, L):
    outs = {}
    for name, shape, dt in (("d_x2", [T, D], F32),
                            ("d_ln2", [T, D], BF16),
                            ("d_om", [128, 32 * 512], BF16),
                            ("d_at", [128, KT * 512], BF16)):
        outs[name] = nc.dram_tensor(name, shape, dt, kind="ExternalOutput").ap()
    w = [("sx2w0", 64), ("sx2w1", 64), ("sl2w0", 64), ("sl2w1", 64),
         ("sv", L["sv_om_end"][(1, 7)]), ("sa", L["sa_sqr"][KT - 1])]
    so = p.dma("sync", outs["d_x2"], L["x2_dram"], "sout", waits=w)
    so = p.dma("sync", outs["d_ln2"], L["ln2_dram"], "sout")
    so = p.dma("sync", outs["d_om"], L["om"], "sout")
    so = p.dma("sync", outs["d_at"], L["attnT"], "sout")
    p.op("sync", lambda e: None, waits=[("sout", so)])


def _finalize(nc, p, stack):
    with nc.Block() as block:
        block.tensor(lambda e: [f(e) for f in p.q["tensor"]])
        block.vector(lambda e: [f(e) for f in p.q["vector"]])
        block.scalar(lambda e: [f(e) for f in p.q["scalar"]])
        block.sync(lambda e: [f(e) for f in p.q["sync"]])
        block.gpsimd(lambda e: [f(e) for f in p.q["gpsimd"]])
    stack.close()
    return nc


def make_inputs(inputs):
    f32 = lambda a: np.ascontiguousarray(np.asarray(a, dtype=np.float32))
    x = f32(inputs["L0_attn_x"]).reshape(S, D)
    b_hidden = f32(inputs["L0_attn_b_hidden"]).reshape(-1)
    nrm = np.zeros((128, 8), dtype=np.float32)
    nrm[:, 0] = f32(inputs["L0_attn_b_qk"]).reshape(-1)
    nrm[:, 1] = f32(inputs["L0_attn_gamma_q"]).reshape(-1)
    nrm[:, 2] = f32(inputs["L0_attn_beta_q"]).reshape(-1)
    nrm[:, 3] = f32(inputs["L0_attn_gamma_k"]).reshape(-1)
    nrm[:, 4] = f32(inputs["L0_attn_beta_k"]).reshape(-1)
    shared = {
        "wqk": f32(inputs["L0_attn_W_qk"]),
        "nrm": np.ascontiguousarray(nrm),
        "bhg": np.ascontiguousarray(b_hidden[HV:].reshape(32, 128).T),
        "bhv": np.ascontiguousarray(b_hidden[:HV].reshape(1, HV)),
        "whid": f32(inputs["L0_attn_W_hidden"]),
        "wout": f32(inputs["L0_attn_W_out"]),
        "bout": f32(inputs["L0_attn_b_out"]).reshape(1, D),
        "gwt": np.ascontiguousarray(f32(inputs["L0_ffn_gate_w"]).T),
        "uwt": np.ascontiguousarray(f32(inputs["L0_ffn_up_w"]).T),
        "dwt": np.ascontiguousarray(f32(inputs["L0_ffn_down_w"]).T),
        "gs": np.ascontiguousarray(f32(inputs["L0_ffn_gate_s"]).reshape(ICN, 128).T),
        "us": np.ascontiguousarray(f32(inputs["L0_ffn_up_s"]).reshape(ICN, 128).T),
        "ds": f32(inputs["L0_ffn_down_s"]).reshape(1, D),
    }
    in_maps = []
    for cid in range(NCORES):
        m = dict(shared)
        m["x"] = np.ascontiguousarray(x[cid * T:(cid + 1) * T, :])
        in_maps.append(m)
    return in_maps


def kernel(**inputs):
    in_maps = make_inputs(inputs)
    nc = build_program(stage=3)
    res = run_bass_kernel_spmd(nc, in_maps, core_ids=list(range(NCORES)))
    y = np.concatenate([np.asarray(r["y"], dtype=np.float32) for r in res.results], axis=0)
    return y.reshape(1, S, D)
